# version 16
# baseline (speedup 1.0000x reference)
import numpy as np
import concourse.bass as bass
import concourse.mybir as mybir
from concourse.bass_utils import run_bass_kernel_spmd

F32 = mybir.dt.float32
BF16 = mybir.dt.bfloat16
AF = mybir.ActivationFunctionType
ALU = mybir.AluOpType
AX = mybir.AxisListType

T = 2048
D = 1024
NT = T // 512
NFC = D // 128
EPS_LN = 1e-5
DEV_A0_HEADS = 16


def _prep_shared(inp):
    f32 = np.float32
    d = {}
    latw = np.asarray(inp["L0_attn_latent_w"], f32)
    latp = np.zeros((8, 128, 384), f32)
    latp[:, :, :288] = latw.reshape(8, 128, 288)
    d["latw"] = np.ascontiguousarray(latp.transpose(1, 0, 2)).reshape(128, 8 * 384)
    latb = np.zeros(384, f32); latb[:288] = np.asarray(inp["L0_attn_latent_b"], f32)
    d["latb"] = np.ascontiguousarray(latb.reshape(3, 128).T)
    latbr = np.zeros((1, 384), f32); latbr[0, :288] = latb[:288]
    d["latbr"] = latbr
    wd = np.asarray(inp["L0_attn_Wd_w"], f32)
    wdp = np.zeros((16, 8, 128, 384), f32)
    wdp[:, :, :, :288] = wd.reshape(16, 8, 128, 288)
    d["wd"] = np.ascontiguousarray(wdp.transpose(0, 2, 1, 3)).reshape(16, 128, 8 * 384)
    wdb = np.zeros((16, 384), f32); wdb[:, :288] = np.asarray(inp["L0_attn_Wd_b"], f32)
    d["wdb"] = np.ascontiguousarray(
        wdb.reshape(16, 3, 128).transpose(2, 0, 1)).reshape(128, 48)
    ow_src = np.asarray(inp["L0_attn_out_w"], f32).reshape(16, 288, 1024)
    ow = np.zeros((16, 3, 128, 1024), f32)
    ow[:, 0] = ow_src[:, 0:128]
    ow[:, 1] = ow_src[:, 128:256]
    ow[:, 2, 0:32] = ow_src[:, 256:288]
    ow[0, 2, 32] = np.asarray(inp["L0_attn_out_b"], f32)
    d["outw"] = np.ascontiguousarray(ow.transpose(0, 2, 1, 3)).reshape(16, 128, 3 * 1024)
    w1 = np.concatenate([np.asarray(inp["L0_ffn_w1"], f32),
                         np.asarray(inp["L0_ffn_new_w1"], f32)], axis=0)
    w3 = np.concatenate([np.asarray(inp["L0_ffn_w3"], f32),
                         np.asarray(inp["L0_ffn_new_w3"], f32)], axis=0)
    w2 = np.concatenate([np.asarray(inp["L0_ffn_w2"], f32),
                         np.asarray(inp["L0_ffn_new_w2"], f32)], axis=0)
    d["w1s"] = np.ascontiguousarray(
        w1.reshape(9, 22, 128, 8, 128).transpose(0, 1, 4, 3, 2)).reshape(9, 22, 128, 1024)
    d["w3s"] = np.ascontiguousarray(
        w3.reshape(9, 22, 128, 8, 128).transpose(0, 1, 4, 3, 2)).reshape(9, 22, 128, 1024)
    d["w2s"] = np.ascontiguousarray(
        w2.reshape(9, 8, 128, 22, 128).transpose(0, 1, 4, 3, 2)).reshape(9, 8, 128, 2816)
    g0 = np.concatenate([np.asarray(inp["L0_ffn_gate_weight"], f32),
                         np.asarray(inp["L0_ffn_new_gate_weight"], f32)], axis=0)
    d["g0w"] = np.ascontiguousarray(
        g0.reshape(9, 8, 128).transpose(2, 1, 0)).reshape(128, 72)
    g0b = np.zeros((1, 9), f32); g0b[0, 8] = np.asarray(inp["L0_ffn_gate_bias"], f32)[0]
    d["g0b"] = g0b
    H1, HD1 = 16, 64
    perm = np.zeros(1024, np.int64)
    for h in range(16):
        perm[h * 64:h * 64 + 32] = h * 64 + 2 * np.arange(32)
        perm[h * 64 + 32:h * 64 + 64] = h * 64 + 2 * np.arange(32) + 1
    wq = np.asarray(inp["L1_attn_Wq"], f32)[:, perm]
    d["wq"] = np.ascontiguousarray(
        wq.reshape(8, 128, 16, 64).transpose(2, 1, 0, 3)).reshape(16, 128, 512)
    wk = np.asarray(inp["L1_attn_Wk"], f32)
    wkp = np.zeros((1024, 256), f32)
    for f in range(32):
        for r in range(3):
            wkp[:, f * 3 + r] = wk[:, f * 6 + 2 * r]
            wkp[:, 128 + f * 3 + r] = wk[:, f * 6 + 2 * r + 1]
    d["wk"] = np.ascontiguousarray(
        wkp.reshape(8, 128, 256).transpose(1, 0, 2)).reshape(128, 2048)
    wv = np.asarray(inp["L1_attn_Wv"], f32)
    d["wv"] = np.ascontiguousarray(
        wv.reshape(8, 128, 192).transpose(1, 0, 2)).reshape(128, 1536)
    kd = np.asarray(inp["L1_attn_key_decoder"], f32)
    bdr = np.zeros((128, 512), f32); bdi = np.zeros((128, 512), f32)
    for f in range(32):
        for h in range(16):
            for r in range(3):
                bdr[f * 3 + r, h * 32 + f] = kd[f, h, r, 0]
                bdi[f * 3 + r, h * 32 + f] = kd[f, h, r, 1]
    d["bdr"], d["bdi"] = bdr, bdi
    vdec = np.asarray(inp["L1_attn_value_decoder"], f32)
    vdp = np.zeros((16, 128, 128), f32)
    vdp[:, :, 0:64] = vdec[:, 0:128]
    vdp[:, 64:128, 64:128] = vdec[:, 128:192]
    d["vd"] = np.ascontiguousarray(vdp)
    d["wproj"] = np.ascontiguousarray(
        np.asarray(inp["L1_attn_Wproj"], f32).reshape(8, 128, 8, 128)
        .transpose(2, 1, 0, 3)).reshape(8, 128, 1024)
    half = HD1 // 2
    freq = (1.0 / (10000.0 ** (np.arange(half, dtype=f32) / half))).astype(f32)
    ang = np.outer(np.arange(T, dtype=f32), freq)
    cos = np.cos(ang).T.astype(f32)
    sin = np.sin(ang).T.astype(f32)
    d["tcq"] = np.ascontiguousarray(np.concatenate([cos, cos], 0) * 0.125)
    d["tsq"] = np.ascontiguousarray(np.concatenate([-sin, sin], 0) * 0.125)
    d["tck"] = np.ascontiguousarray(np.concatenate([cos, cos], 0))
    d["tsk"] = np.ascontiguousarray(np.concatenate([-sin, sin], 0))
    d["g1w"] = np.ascontiguousarray(
        np.asarray(inp["L1_ffn_gate_w"], f32).reshape(16, 8, 128).transpose(2, 1, 0)).reshape(128, 128)
    d["wgs"] = np.ascontiguousarray(
        np.asarray(inp["L1_ffn_w_gate"], f32).reshape(16, 8, 128, 4, 128)
        .transpose(0, 2, 1, 3, 4)).reshape(16, 128, 4096)
    d["wus"] = np.ascontiguousarray(
        np.asarray(inp["L1_ffn_w_up"], f32).reshape(16, 8, 128, 4, 128)
        .transpose(0, 2, 1, 3, 4)).reshape(16, 128, 4096)
    d["wds"] = np.ascontiguousarray(
        np.asarray(inp["L1_ffn_w_down"], f32).reshape(16, 4, 128, 8, 128)
        .transpose(0, 2, 1, 3, 4)).reshape(16, 128, 4096)
    d["wsgs"] = np.ascontiguousarray(
        np.asarray(inp["L1_ffn_ws_gate"], f32).reshape(8, 128, 8, 128)
        .transpose(2, 1, 0, 3)).reshape(8, 128, 1024)
    d["wsus"] = np.ascontiguousarray(
        np.asarray(inp["L1_ffn_ws_up"], f32).reshape(8, 128, 8, 128)
        .transpose(2, 1, 0, 3)).reshape(8, 128, 1024)
    d["wsds"] = np.ascontiguousarray(
        np.asarray(inp["L1_ffn_ws_down"], f32).reshape(8, 128, 8, 128)
        .transpose(2, 1, 0, 3)).reshape(8, 128, 1024)
    d["c_ident"] = np.eye(128, dtype=f32)
    tri = np.zeros((128, 128), f32)
    iu = np.triu_indices(128)
    tri[iu] = 1.0
    d["c_tri"] = tri
    pm = np.zeros((128, 128), f32)
    for i in range(64):
        pm[(i + 32) % 64, i] = 1.0
    d["c_perm"] = pm
    return d



ENGINES = ("sync", "tensor", "vector", "scalar", "gpsimd")


class Ctx:
    def __init__(self, nc):
        self.nc = nc
        self.ops = {e: [] for e in ENGINES}
        self.sems = {}

    def _sem(self, name):
        if name not in self.sems:
            self.sems[name] = [self.nc.alloc_semaphore(name), 0]
        return self.sems[name]

    def val(self, name):
        return self._sem(name)[1]

    def op(self, eng, builder, inc=None, inck=1):
        if inc is not None:
            s = self._sem(inc)
            s[1] += inck
            h = s[0]
            def f(e, b=builder, h=h, k=inck):
                b(e).then_inc(h, k)
            self.ops[eng].append(f)
            return s[1]
        else:
            def f(e, b=builder):
                b(e)
            self.ops[eng].append(f)
            return None

    def wait(self, eng, sname, val):
        if val <= 0:
            return
        h = self._sem(sname)[0]
        def f(e, h=h, v=val):
            e.wait_ge(h, v)
        self.ops[eng].append(f)

    def dma(self, eng, out_ap, in_ap, inc):
        return self.op(eng, lambda e, o=out_ap, i=in_ap: e.dma_start(o, i), inc=inc, inck=16)


def _mm(ctx, out, lhsT, rhs, start, stop, inc=None):
    return ctx.op("tensor",
                  lambda e, o=out, l=lhsT, r=rhs, st=start, sp=stop:
                  e.matmul(o, l, r, start=st, stop=sp), inc=inc)



ALL_PHASES = ("ln0", "attn0", "ffn0", "ln1", "attn1", "ffn1")


class St:
    pass


def _alloc(nc):
    st = St()
    st.x = [nc.alloc_sbuf_tensor(f"x{i}", [128, T], F32) for i in range(NFC)]
    st.xn = [nc.alloc_sbuf_tensor(f"xn{i}", [128, T], BF16) for i in range(NFC)]
    st.onesM = nc.alloc_sbuf_tensor("onesM", [128, 128], F32)
    st.ones_bf = nc.alloc_sbuf_tensor("ones_bf", [128, 128], BF16)
    st.ident = nc.alloc_sbuf_tensor("ident", [128, 128], BF16)
    st.tri = nc.alloc_sbuf_tensor("tri", [128, 128], BF16)
    st.latb = nc.alloc_sbuf_tensor("sb_latb", [128, 3], F32)
    st.wdb = nc.alloc_sbuf_tensor("sb_wdb", [128, 48], F32)
    st.AR = nc.alloc_sbuf_tensor("AR", [128, 48640], BF16)
    st.ARF = nc.alloc_sbuf_tensor("ARF", [128, 2656], F32)
    st.vtok3 = nc.alloc_sbuf_tensor("vtok3", [128, 16, 65], BF16)
    st.pb = [nc.alloc_psum_tensor(f"pb{i}", [128, 512], F32) for i in range(8)]
    st.ptb = None
    return st


def _emit_prologue(ctx, st, dram):
    for fc in range(NFC):
        ctx.dma("sync", st.x[fc][:, :], dram["xin"][fc], inc="xdma")
    ctx.dma("gpsimd", st.ident[:, :], dram["c_ident"][:, :], inc="wdma")
    ctx.dma("gpsimd", st.tri[:, :], dram["c_tri"][:, :], inc="wdma")
    ctx.dma("sync", st.latb[:, :], dram["latb"][:, :], inc="xdma")
    ctx.dma("sync", st.wdb[:, :], dram["wdb"][:, :], inc="xdma")
    ctx.op("vector", lambda e: e.memset(st.onesM[:, :], 1.0), inc="vec")
    ctx.op("vector", lambda e: e.memset(st.ones_bf[:, :], 1.0), inc="vec")
    st.after_prologue = dict(xdma=ctx.val("xdma"), wdma=ctx.val("wdma"), vec=ctx.val("vec"))


def _emit_ln(ctx, st, li):
    pb, ARF = st.pb, st.ARF
    if li == 0:
        ctx.wait("vector", "xdma", st.after_prologue["xdma"])
        ctx.wait("scalar", "xdma", st.after_prologue["xdma"])
        ctx.wait("tensor", "vec", st.after_prologue["vec"])
    else:
        ctx.wait("vector", "pe", ctx.val("pe"))
        ctx.wait("scalar", "pe", ctx.val("pe"))
        ctx.wait("scalar", "vec", ctx.val("vec"))
        ctx.wait("tensor", "act", ctx.val("act"))
    mrow = ARF[0:1, 0:512]; erow = ARF[0:1, 512:1024]; rrow = ARF[0:1, 1024:1536]
    stats_c = [0] * NT
    norm_c = [0] * NT
    sqrt_c = [0] * NT
    last_mm = [0] * NT
    bcast_pe = [0] * NT
    sq_mm = {}

    def emit_sums(nt):
        sl = slice(512 * nt, 512 * (nt + 1))
        bsum = pb[2 * (nt % 2)]
        bsq = pb[2 * (nt % 2) + 1]
        vc = None
        for fc in range(NFC):
            vc = ctx.op("vector", lambda e, fc=fc, sl=sl:
                        e.tensor_copy(st.xn[fc][:, sl], st.x[fc][:, sl]), inc="vec")
        ctx.wait("tensor", "vec", vc)
        if nt >= 2:
            ctx.wait("tensor", "vec", stats_c[nt - 2])
        for fc in range(NFC):
            _mm(ctx, bsum[0:1, :], st.ones_bf[:, 0:1], st.xn[fc][:, sl],
                start=(fc == 0), stop=(fc == NFC - 1))
        for fc in range(NFC):
            k = fc % 2
            war = sq_mm.get(fc - 2 if fc >= 2 else NFC - 2 + fc)
            if war:
                ctx.wait("scalar", "pe", war)
            sqv = ctx.op("scalar", lambda e, fc=fc, sl=sl, k=k:
                         e.activation(st.AR[:, 512 * k:512 * k + 512], st.x[fc][:, sl],
                                      AF.Square), inc="act")
            ctx.wait("tensor", "act", sqv)
            sq_mm[fc] = _mm(ctx, bsq[0:1, :], st.ones_bf[:, 0:1],
                            st.AR[:, 512 * k:512 * k + 512],
                            start=(fc == 0), stop=(fc == NFC - 1), inc="pe")
        last_mm[nt] = sq_mm[NFC - 1]

    def emit_stats_norm(nt):
        sl = slice(512 * nt, 512 * (nt + 1))
        bsum = pb[2 * (nt % 2)]
        bsq = pb[2 * (nt % 2) + 1]
        mb = pb[4 + 2 * (nt % 2)]
        rb = pb[5 + 2 * (nt % 2)]
        ctx.wait("vector", "pe", max(last_mm[nt], bcast_pe[nt - 1] if nt else 0))
        if nt:
            ctx.wait("vector", "act", sqrt_c[nt - 1])
        ctx.op("vector", lambda e: e.tensor_scalar(mrow, bsum[0:1, :], 1.0 / D, None, ALU.mult))
        ctx.op("vector", lambda e: e.tensor_scalar(erow, bsq[0:1, :], 1.0 / D, None, ALU.mult))
        ctx.op("vector", lambda e: e.tensor_tensor(rrow, mrow, mrow, ALU.mult))
        ctx.op("vector", lambda e: e.tensor_sub(erow, erow, rrow))
        ctx.op("vector", lambda e: e.tensor_scalar(erow, erow, EPS_LN, None, ALU.add))
        stats_c[nt] = ctx.op("vector", lambda e: e.reciprocal(erow, erow), inc="vec")
        ctx.wait("scalar", "vec", stats_c[nt])
        sqrt_c[nt] = ctx.op("scalar", lambda e: e.activation(rrow, erow, AF.Sqrt), inc="act")
        ctx.wait("tensor", "act", sqrt_c[nt])
        if nt >= 2:
            ctx.wait("tensor", "vec", norm_c[nt - 2])
        _mm(ctx, mb[:, :], st.onesM[0:1, 0:128], mrow, start=True, stop=True)
        bcast_pe[nt] = pe = _mm(ctx, rb[:, :], st.onesM[0:1, 0:128], rrow, start=True, stop=True,
                                inc="pe")
        ctx.wait("vector", "pe", pe)
        for fc in range(NFC):
            ctx.op("vector", lambda e, fc=fc, sl=sl:
                   e.tensor_sub(ARF[:, 1536:2048], st.x[fc][:, sl], mb[:, :]))
            norm_c[nt] = ctx.op("vector", lambda e, fc=fc, sl=sl:
                                e.tensor_mul(st.xn[fc][:, sl], ARF[:, 1536:2048], rb[:, :]),
                                inc="vec")

    emit_sums(0)
    emit_sums(1)
    emit_stats_norm(0)
    emit_sums(2)
    emit_stats_norm(1)
    emit_sums(3)
    emit_stats_norm(2)
    emit_stats_norm(3)
    ctx.wait("tensor", "vec", norm_c[NT - 1])
    st.ln_done = dict(vec=ctx.val("vec"))


def _emit_store_dbg(ctx, st, dram):
    ctx.wait("gpsimd", "vec", ctx.val("vec"))
    ctx.wait("gpsimd", "pe", ctx.val("pe"))
    ctx.wait("gpsimd", "act", ctx.val("act"))
    ctx.wait("sync", "vec", ctx.val("vec"))
    ctx.dma("gpsimd", dram["out"][0][:, 0:144], st.AR[:, 33280:33424], inc="stdma")
    ctx.dma("sync", dram["out"][1][:, 0:1024], st.ARF[:, 0:1024], inc="stdma2")
    ctx.dma("gpsimd", dram["out"][2][:, 0:1024], st.AR[:, 0:1024], inc="stdma")
    ctx.dma("gpsimd", dram["out"][3][:, 0:1024], st.AR[:, 21504:22528], inc="stdma")
    ctx.dma("gpsimd", dram["out"][4][:, 0:72], st.AR[:, 33424:33496], inc="stdma")
    ctx.dma("gpsimd", dram["out"][5][:, 0:2048], st.AR[:, 26624:28672], inc="stdma")
    ctx.dma("gpsimd", dram["out"][6][:, 0:1024], st.AR[:, 22528:23552], inc="stdma")
    ctx.wait("gpsimd", "stdma", ctx.val("stdma"))
    ctx.wait("sync", "stdma2", ctx.val("stdma2"))


def _emit_store(ctx, st, dram, src="x"):
    tiles = st.x if src == "x" else st.xn
    eng = "sync" if src == "x" else "gpsimd"
    ctx.wait(eng, "vec", ctx.val("vec"))
    ctx.wait(eng, "pe", ctx.val("pe"))
    ctx.wait(eng, "act", ctx.val("act"))
    for fc in range(NFC):
        ctx.dma(eng, dram["out"][fc], tiles[fc][:, :], inc="stdma")
    ctx.wait(eng, "stdma", ctx.val("stdma"))


def build_program(phases=ALL_PHASES, store_src="x"):
    nc = bass.Bass("TRN2", target_bir_lowering=False, detect_race_conditions=False)
    names_shapes = {
        "xin": (NFC, 128, T), "latw": (128, 3072), "latb": (128, 3), "latbr": (1, 384),
        "wd": (16, 128, 3072),
        "wdb": (128, 48), "outw": (16, 128, 3072), "w1s": (9, 22, 128, 1024),
        "w3s": (9, 22, 128, 1024), "w2s": (9, 8, 128, 2816), "g0w": (128, 72), "g0b": (1, 9),
        "wq": (16, 128, 512), "wk": (128, 2048), "wv": (128, 1536), "bdr": (128, 512),
        "bdi": (128, 512), "vd": (16, 128, 128), "wproj": (8, 128, 1024),
        "tcq": (64, T), "tsq": (64, T), "tck": (64, T), "tsk": (64, T),
        "g1w": (128, 128), "wgs": (16, 128, 4096), "wus": (16, 128, 4096),
        "wds": (16, 128, 4096), "wsgs": (8, 128, 1024), "wsus": (8, 128, 1024),
        "wsds": (8, 128, 1024), "c_ident": (128, 128), "c_tri": (128, 128),
        "c_perm": (128, 128),
    }
    dram = {k: nc.dram_tensor(k, list(v), F32, kind="ExternalInput")
            for k, v in names_shapes.items()}
    dram["out"] = nc.dram_tensor("out", [NFC, 128, T], F32, kind="ExternalOutput")
    st = _alloc(nc)
    ctx = Ctx(nc)
    _emit_prologue(ctx, st, dram)
    for ph in phases:
        if ph == "ln0":
            _emit_ln(ctx, st, 0)
        elif ph == "ln1":
            _emit_ln(ctx, st, 1)
        elif ph == "attn0":
            _emit_attn0(ctx, st, dram)
        elif ph == "ffn0":
            _emit_ffn0(ctx, st, dram)
        elif ph == "attn1":
            _emit_attn1(ctx, st, dram)
        elif ph == "ffn1":
            _emit_ffn1(ctx, st, dram)
    if store_src == "dbg":
        _emit_store_dbg(ctx, st, dram)
    else:
        _emit_store(ctx, st, dram, src=store_src)

    with nc.Block() as block:
        @block.sync
        def _(e):
            for f in ctx.ops["sync"]:
                f(e)
        @block.tensor
        def _(e):
            for f in ctx.ops["tensor"]:
                f(e)
        @block.vector
        def _(e):
            for f in ctx.ops["vector"]:
                f(e)
        @block.scalar
        def _(e):
            for f in ctx.ops["scalar"]:
                f(e)
        @block.gpsimd
        def _(e):
            for f in ctx.ops["gpsimd"]:
                f(e)
    return nc



def _emit_attn0(ctx, st, dram):
    AR, ARF, pb = st.AR, st.ARF, st.pb
    PT = [AR[:, 512 * kc:512 * (kc + 1)] for kc in range(16)]
    y_t = [AR[:, 8192:10240], AR[:, 10240:12288], AR[:, 12288:14336]]
    kv_t = [AR[:, 14336:16384], AR[:, 16384:18432], AR[:, 18432:20480]]
    KVT = [AR[:, 20480 + 384 * kc:20480 + 384 * kc + 384] for kc in range(16)]
    q_t = [AR[:, 26624:28672], AR[:, 28672:30720], AR[:, 30720:32768]]
    latw = AR[:, 32768:35840]
    wdbuf = [AR[:, 35840:38912], AR[:, 38912:41984]]
    owbuf = [AR[:, 41984:45056], AR[:, 45056:48128]]
    recB = AR[:, 48128:48640]
    latbr_bf = AR[:, 8192:8576]
    Rt = ARF[:, 0:512]
    recF = ARF[:, 512:1024]

    c_latw = ctx.dma("gpsimd", latw[:, :], dram["latw"][:, :], inc="wdma")
    ctx.wait("vector", "xdma", st.after_prologue["xdma"])
    ctx.op("vector", lambda e: e.tensor_scalar_mul(st.wdb[:, :], st.wdb[:, :], 0.125), inc="vec")
    ctx.op("vector", lambda e: e.memset(y_t[2][32:64, :], 0.0))
    ctx.op("vector", lambda e: e.memset(y_t[2][64:128, :], 0.0), inc="vec")
    wdb8_done = ctx.val("vec")
    ctx.wait("vector", "wdma", 32)
    c_wd = [0] * 16
    c_ow = [0] * 16
    for h in (0, 1):
        c_wd[h] = ctx.dma("gpsimd", wdbuf[h][:, :], dram["wd"][h], inc="wdma")
        c_ow[h] = ctx.dma("gpsimd", owbuf[h][:, :], dram["outw"][h], inc="wdma")

    ctx.wait("tensor", "wdma", c_latw)
    ctx.wait("scalar", "xdma", st.after_prologue["xdma"])
    evac_c = 0
    for lc in range(3):
        for nt in range(NT):
            sl = slice(512 * nt, 512 * (nt + 1))
            bank = pb[(lc * NT + nt) % 2]
            if lc * NT + nt >= 2:
                ctx.wait("tensor", "act", evac_c - 1)
            for fc in range(NFC):
                lh = latw[:, fc * 384 + 128 * lc: fc * 384 + 128 * lc + 128]
                pe = _mm(ctx, bank[:, :], lh, st.xn[fc][:, sl],
                         start=(fc == 0), stop=(fc == NFC - 1),
                         inc="pe" if fc == NFC - 1 else None)
            ctx.wait("scalar", "pe", pe)
            evac_c = ctx.op("scalar", lambda e, o=kv_t[lc][:, sl], i=bank[:, :],
                            b=st.latb[:, lc:lc + 1]:
                            e.activation(o, i, AF.Identity, bias=b), inc="act")
    c_lbr = ctx.dma("gpsimd", latbr_bf[0:1, 0:384], dram["latbr"][:, :], inc="wdma")
    ctx.wait("tensor", "wdma", c_lbr)
    kve = 0
    for kc in range(16):
        ksl = slice(128 * kc, 128 * (kc + 1))
        bank = pb[2 + kc % 2]
        if kc >= 2:
            ctx.wait("tensor", "act", kve - 1)
        for fc in range(NFC):
            _mm(ctx, bank[:, 0:384], st.xn[fc][:, ksl], latw[:, fc * 384:fc * 384 + 384],
                start=(fc == 0), stop=False)
        pe = _mm(ctx, bank[:, 0:384], st.ones_bf[0:1, 0:128], latbr_bf[0:1, 0:384],
                 start=False, stop=True, inc="pe")
        ctx.wait("scalar", "pe", pe)
        kve = ctx.op("scalar", lambda e, kc=kc, bank=bank:
                     e.copy(KVT[kc][:, 0:384], bank[:, 0:384]), inc="act")
    ctx.wait("vector", "act", kve)
    for kc in range(16):
        ctx.op("vector", lambda e, kc=kc: e.memset(KVT[kc][:, 288:289], 1.0),
               inc="vec" if kc == 15 else None)

    hold = {'yev': ctx.val("vec"), 'tail': None, 'pe_last': 0, 'ytile': [0] * NT}
    for h in range(DEV_A0_HEADS):
        hb = h % 2
        ctx.wait("tensor", "wdma", c_wd[h])
        if h == 0:
            ctx.wait("scalar", "vec", wdb8_done)
        qe_c = 0
        for lc in range(3):
            for nt in range(NT):
                sl = slice(512 * nt, 512 * (nt + 1))
                bank = pb[(lc * NT + nt) % 2]
                ctx.wait("tensor", "vec", ctx.val("vec"))
                if lc * NT + nt >= 2:
                    ctx.wait("tensor", "act", qe_c - 1)
                for fc in range(NFC):
                    lh = wdbuf[hb][:, fc * 384 + 128 * lc: fc * 384 + 128 * lc + 128]
                    pe = _mm(ctx, bank[:, :], lh, st.xn[fc][:, sl],
                             start=(fc == 0), stop=(fc == NFC - 1),
                             inc="pe" if fc == NFC - 1 else None)
                ctx.wait("scalar", "pe", pe)
                qe_c = ctx.op("scalar", lambda e, o=q_t[lc][:, sl], i=bank[:, :],
                              b=st.wdb[:, 3 * h + lc:3 * h + lc + 1]:
                              e.activation(o, i, AF.Identity, bias=b, scale=0.125), inc="act")
        if h + 2 < 16:
            ctx.wait("gpsimd", "pe", ctx.val("pe"))
            c_wd[h + 2] = ctx.dma("gpsimd", wdbuf[hb][:, :], dram["wd"][h + 2], inc="wdma")
        ctx.wait("tensor", "act", qe_c)

        def gen_sty(qt, exp_prev):
            Q0 = 512 * qt
            Kc = Q0 // 128 + 4
            qsl = slice(Q0, Q0 + 512)
            c_exp = [0] * Kc
            c_fix = [0] * Kc

            def emit_st(kc):
                pbS = pb[3 + kc % 2]
                ctx.wait("tensor", "act", c_exp[kc - 2] if kc >= 2 else exp_prev)
                _mm(ctx, pbS[:, :], kv_t[0][:, 128 * kc:128 * kc + 128], q_t[0][:, qsl], True, False)
                _mm(ctx, pbS[:, :], kv_t[1][:, 128 * kc:128 * kc + 128], q_t[1][:, qsl], False, False)
                pe = _mm(ctx, pbS[:, :], kv_t[2][:, 128 * kc:128 * kc + 128], q_t[2][:, qsl],
                         False, True, inc="pe")
                ctx.wait("scalar", "pe", pe)
                c_exp[kc] = ctx.op("scalar", lambda e, kc=kc, pbS=pbS:
                                   e.activation(PT[kc][:, 0:512], pbS[:, :], AF.Exp), inc="act")
                kr = kc - Q0 // 128
                if kr >= 0:
                    ctx.wait("vector", "act", c_exp[kc])
                    for jj in range(kr):
                        ctx.op("vector", lambda e, kc=kc, jj=jj:
                               e.memset(PT[kc][:, 128 * jj:128 * jj + 128], 0.0))
                    c_fix[kc] = ctx.op("vector", lambda e, kc=kc, kr=kr:
                                       e.tensor_mul(PT[kc][:, 128 * kr:128 * kr + 128],
                                                    PT[kc][:, 128 * kr:128 * kr + 128],
                                                    st.tri[:, :]), inc="vec")

            def emit_y(kc):
                ctx.wait("tensor", "vec" if c_fix[kc] else "act",
                         c_fix[kc] if c_fix[kc] else c_exp[kc])
                if kc == 0:
                    ctx.wait("tensor", "vec", hold['yev'])
                st_ = (kc == 0)
                sp_ = (kc == Kc - 1)
                _mm(ctx, pb[5][:, :], KVT[kc][:, 0:128], PT[kc][:, 0:512], st_, sp_)
                _mm(ctx, pb[6][:, :], KVT[kc][:, 128:256], PT[kc][:, 0:512], st_, sp_)
                return _mm(ctx, pb[7][:, :], KVT[kc][:, 256:384], PT[kc][:, 0:512], st_, sp_,
                           inc="pe" if sp_ else None)

            for kc in range(Kc):
                emit_st(kc)
                yield
            for kc in range(Kc - 1):
                emit_y(kc)
                yield
            hold['pe_last'] = emit_y(Kc - 1)

        def gen_tail(qt):
            qsl = slice(512 * qt, 512 * qt + 512)
            ctx.wait("vector", "pe", hold['pe_last'])
            vc = ctx.op("vector", lambda e: e.tensor_copy(recB[32:33, :], pb[7][32:33, :]), inc="vec")
            yield
            ctx.wait("tensor", "vec", vc)
            pe = _mm(ctx, pb[2][:, :], st.ones_bf[32:33, 0:128], recB[32:33, :], True, True, inc="pe")
            ctx.wait("vector", "pe", pe)
            ctx.op("vector", lambda e: e.reciprocal(Rt[:, :], pb[2][:, :]))
            ctx.op("vector", lambda e, qsl=qsl: e.tensor_mul(y_t[0][:, qsl], pb[5][:, :], Rt[:, :]))
            ctx.op("vector", lambda e, qsl=qsl: e.tensor_mul(y_t[1][:, qsl], pb[6][:, :], Rt[:, :]))
            hold['yev'] = ctx.op("vector", lambda e, qsl=qsl:
                                 e.tensor_mul(y_t[2][0:32, qsl], pb[7][0:32, :], Rt[0:32, :]),
                                 inc="vec")
            hold['ytile'][qt] = hold['yev']
            yield

        for qt in range(NT):
            exp_prev = ctx.val("act")
            g_sty = gen_sty(qt, exp_prev)
            steps = 0
            for _ in g_sty:
                steps += 1
                if steps == 1 and hold['tail'] is not None:
                    next(hold['tail'])
                elif steps == 3 and hold['tail'] is not None:
                    for _ in hold['tail']:
                        pass
                    hold['tail'] = None
            hold['tail'] = gen_tail(qt)
        memset_c = 0
        if h == 0:
            memset_c = ctx.op("vector", lambda e: e.memset(y_t[2][32:33, :], 1.0), inc="vec")
        ctx.wait("tensor", "wdma", c_ow[h])
        if memset_c:
            ctx.wait("tensor", "vec", memset_c)
        g_tail = hold['tail']
        hold['tail'] = None
        ev = {}
        OBANKS = (0, 1, 3, 4)
        for idx, (nt, cc) in enumerate([(nt_, cc_) for nt_ in range(NT) for cc_ in range(NFC)]):
            if idx == 4:
                for _ in g_tail:
                    pass
            sl = slice(512 * nt, 512 * (nt + 1))
            bank = pb[OBANKS[idx % 4]]
            if cc == 0:
                ctx.wait("tensor", "vec", hold['ytile'][nt])
            if idx >= 4:
                ctx.wait("tensor", "vec", ev[idx - 4])
            _mm(ctx, bank[:, :], owbuf[hb][:, cc * 128:cc * 128 + 128], y_t[0][:, sl], True, False)
            _mm(ctx, bank[:, :], owbuf[hb][:, 1024 + cc * 128:1024 + cc * 128 + 128],
                y_t[1][:, sl], False, False)
            pe = _mm(ctx, bank[:, :], owbuf[hb][:, 2048 + cc * 128:2048 + cc * 128 + 128],
                     y_t[2][:, sl], False, True, inc="pe")
            ctx.wait("vector", "pe", pe)
            ev[idx] = ctx.op("vector", lambda e, cc=cc, sl=sl, bank=bank:
                             e.tensor_add(st.x[cc][:, sl], bank[:, :], st.x[cc][:, sl]), inc="vec")
        if h + 2 < 16:
            ctx.wait("gpsimd", "pe", ctx.val("pe"))
            c_ow[h + 2] = ctx.dma("gpsimd", owbuf[hb][:, :], dram["outw"][h + 2], inc="wdma")
    ctx.wait("tensor", "vec", ctx.val("vec"))


def _emit_ffn0(ctx, st, dram):
    AR, ARF, pb = st.AR, st.ARF, st.pb
    ACT = [AR[:, 1024 * hc:1024 * (hc + 1)] for hc in range(22)]
    w1buf = [AR[:, 22528:23552], AR[:, 23552:24576]]
    w3buf = [AR[:, 24576:25600], AR[:, 25600:26624]]
    w2buf = [AR[:, 26624:29440], AR[:, 29440:32256]]
    sg = [AR[:, 32256:32768], AR[:, 32768:33280]]
    cwbf = AR[:, 33280:33424]
    g0bf = AR[:, 33424:33496]
    g0b_bf = AR[:, 33496:33512]
    cwrowB = AR[:, 33512:34024]
    CWBs = ARF[:, 0:1024]
    cwrowF = ARF[:, 1024:1536]
    dtmp = ARF[:, 1536:2048]
    sgF = ARF[:, 2048:2064]
    gm1 = ARF[:, 2064:2065]; gm2 = ARF[:, 2065:2066]
    gden = ARF[:, 2066:2067]
    glt = ARF[:, 2067:2076]; gt2 = ARF[:, 2076:2085]; gge = ARF[:, 2085:2094]
    gcw = ARF[:, 2094:2103]

    ctx.wait("gpsimd", "pe", ctx.val("pe"))
    ctx.wait("gpsimd", "vec", ctx.val("vec"))
    c_g0 = ctx.dma("gpsimd", g0bf[:, :], dram["g0w"][:, :], inc="wdma")
    c_g0b = ctx.dma("gpsimd", g0b_bf[0:1, 0:9], dram["g0b"][:, :], inc="wdma")

    ctx.wait("tensor", "wdma", c_g0b)
    ctx.wait("tensor", "vec", ctx.val("vec"))
    sig_c = 0
    cw_c = 0
    for j in range(16):
        tsl = slice(128 * j, 128 * (j + 1))
        if j > 0:
            ctx.wait("tensor", "act", sig_c)
        for fc in range(NFC):
            _mm(ctx, pb[6][:, 0:9], st.xn[fc][:, tsl], g0bf[:, 9 * fc:9 * fc + 9],
                start=(fc == 0), stop=False)
        pe = _mm(ctx, pb[6][:, 0:9], st.ones_bf[0:1, 0:128], g0b_bf[0:1, 0:9],
                 start=False, stop=True, inc="pe")
        ctx.wait("scalar", "pe", pe)
        if j > 0:
            ctx.wait("scalar", "vec", cw_c)
        sig_c = ctx.op("scalar", lambda e: e.activation(sgF[:, 0:9], pb[6][:, 0:9], AF.Sigmoid),
                       inc="act")
        ctx.wait("vector", "act", sig_c)

        def vf(builder):
            c = ctx.op("vector", builder, inc="vec")
            ctx.wait("vector", "vec", c)
            return c
        vf(lambda e: e.reduce_max(gm1[:, :], sgF[:, 0:9], axis=AX.X))
        vf(lambda e: e.tensor_scalar(glt[:, :], sgF[:, 0:9], gm1[:, 0:1], None, ALU.is_lt))
        vf(lambda e: e.tensor_mul(gt2[:, :], sgF[:, 0:9], glt[:, :]))
        vf(lambda e: e.reduce_max(gm2[:, :], gt2[:, :], axis=AX.X))
        vf(lambda e: e.tensor_add(gden[:, :], gm1[:, :], gm2[:, :]))
        vf(lambda e: e.tensor_scalar(gden[:, :], gden[:, :], 1e-20, None, ALU.add))
        vf(lambda e: e.reciprocal(gden[:, :], gden[:, :]))
        vf(lambda e: e.tensor_scalar(gge[:, :], sgF[:, 0:9], gm2[:, 0:1], None, ALU.is_ge))
        vf(lambda e: e.scalar_tensor_tensor(gcw[:, :], sgF[:, 0:9], gden[:, 0:1],
                                            gge[:, :], ALU.mult, ALU.mult))
        cw_c = ctx.op("vector", lambda e, j=j: e.tensor_copy(cwbf[:, 9 * j:9 * j + 9], gcw[:, :]),
                      inc="vec")

    c_w1 = {}; c_w3 = {}; c_w2 = {}
    w1_war = [0, 0]; w3_war = [0, 0]; w2_war = [0, 0]
    cwb_war = 0
    for ttp in range(2):
        for e in range(9):
            for hc0 in (0, 1):
                ctx.wait("gpsimd", "pe", w1_war[hc0])
                c_w1[hc0] = ctx.dma("gpsimd", w1buf[hc0][:, :], dram["w1s"][e, hc0], inc="wdma")
                ctx.wait("gpsimd", "pe", w3_war[hc0])
                c_w3[hc0] = ctx.dma("gpsimd", w3buf[hc0][:, :], dram["w3s"][e, hc0], inc="wdma")
            ctx.wait("tensor", "vec", cw_c if (ttp == 0 and e == 0) else ctx.val("vec"))
            pe = 0
            for half in range(2):
                for jj in range(4):
                    j = ttp * 8 + half * 4 + jj
                    pe = _mm(ctx, pb[6][32 * half:32 * half + 1, 128 * jj:128 * jj + 128],
                             cwbf[:, 9 * j + e:9 * j + e + 1], st.ident[:, 0:128],
                             start=True, stop=True,
                             inc="pe" if (half == 1 and jj == 3) else None)
            ctx.wait("vector", "pe", pe)
            rows_c = [0, 0]
            for half in range(2):
                rows_c[half] = ctx.op("vector", lambda e_, half=half:
                                      e_.tensor_copy(cwrowB[32 * half:32 * half + 1, 0:512],
                                                     pb[6][32 * half:32 * half + 1, 0:512]),
                                      inc="vec")
            cpy = [0, 0]
            for half in range(2):
                bank = pb[6 + half]
                ctx.wait("tensor", "vec", rows_c[1] if half == 0 else rows_c[half])
                pe = _mm(ctx, bank[:, :], st.ones_bf[32 * half:32 * half + 1, 0:128],
                         cwrowB[32 * half:32 * half + 1, 0:512], True, True, inc="pe")
                ctx.wait("vector", "pe", pe)
                cpy[half] = ctx.op("vector", lambda e_, half=half, bank=bank:
                                   e_.tensor_copy(CWBs[:, 512 * half:512 * half + 512],
                                                  bank[:, :]), inc="vec")
            silu_c = {}
            act_c = {}
            for hc in range(22):
                par = hc % 2
                ctx.wait("tensor", "wdma", c_w3[hc])
                for half in range(2):
                    i2 = hc * 2 + half
                    spar = i2 % 2
                    tsl = slice(1024 * ttp + 512 * half, 1024 * ttp + 512 * half + 512)
                    bg = pb[i2 % 2]
                    bu = pb[2 + i2 % 2]
                    if i2 >= 2:
                        ctx.wait("tensor", "act", silu_c[i2 - 2])
                        ctx.wait("tensor", "vec", act_c[i2 - 2])
                    peg = peu = None
                    for fc in range(NFC):
                        peg = _mm(ctx, bg[:, :], w1buf[par][:, 128 * fc:128 * fc + 128],
                                  st.xn[fc][:, tsl], start=(fc == 0), stop=(fc == NFC - 1),
                                  inc="pe" if fc == NFC - 1 else None)
                    for fc in range(NFC):
                        peu = _mm(ctx, bu[:, :], w3buf[par][:, 128 * fc:128 * fc + 128],
                                  st.xn[fc][:, tsl], start=(fc == 0), stop=(fc == NFC - 1),
                                  inc="pe" if fc == NFC - 1 else None)
                    ctx.wait("scalar", "pe", peg)
                    if i2 >= 2:
                        ctx.wait("scalar", "vec", act_c[i2 - 2])
                    silu_c[i2] = ctx.op(
                        "scalar", lambda e_, spar=spar, bg=bg:
                        e_.activation(sg[spar][:, :], bg[:, :], AF.Silu), inc="act")
                    ctx.wait("vector", "pe", peu)
                    ctx.wait("vector", "act", silu_c[i2])
                    act_c[i2] = ctx.op(
                        "vector", lambda e_, hc=hc, half=half, spar=spar, bu=bu:
                        e_.tensor_mul(ACT[hc][:, 512 * half:512 * half + 512], bu[:, :],
                                      sg[spar][:, :]), inc="vec")
                w1_war[par] = ctx.val("pe")
                w3_war[par] = ctx.val("pe")
                if hc + 2 < 22:
                    ctx.wait("gpsimd", "pe", ctx.val("pe"))
                    c_w1[hc + 2] = ctx.dma("gpsimd", w1buf[par][:, :], dram["w1s"][e, hc + 2],
                                           inc="wdma")
                    c_w3[hc + 2] = ctx.dma("gpsimd", w3buf[par][:, :], dram["w3s"][e, hc + 2],
                                           inc="wdma")
            for dc0 in (0, 1):
                ctx.wait("gpsimd", "pe", w2_war[dc0])
                c_w2[dc0] = ctx.dma("gpsimd", w2buf[dc0][:, :], dram["w2s"][e, dc0], inc="wdma")
            ev = {}
            ctx.wait("tensor", "vec", act_c[43])
            for dc in range(8):
                par = dc % 2
                ctx.wait("tensor", "wdma", c_w2[dc])
                for half in range(2):
                    tsl = slice(1024 * ttp + 512 * half, 1024 * ttp + 512 * half + 512)
                    bank = pb[4 + (dc * 2 + half) % 2]
                    if dc * 2 + half >= 2:
                        ctx.wait("tensor", "vec", ev[dc * 2 + half - 2])
                    for hc in range(22):
                        pe = _mm(ctx, bank[:, :], w2buf[par][:, 128 * hc:128 * hc + 128],
                                 ACT[hc][:, 512 * half:512 * half + 512],
                                 start=(hc == 0), stop=(hc == 21),
                                 inc="pe" if hc == 21 else None)
                    ctx.wait("vector", "pe", pe)
                    ctx.op("vector", lambda e_, bank=bank, half=half:
                           e_.tensor_mul(dtmp[:, :], bank[:, :], CWBs[:, 512 * half:512 * half + 512]))
                    ev[dc * 2 + half] = ctx.op(
                        "vector", lambda e_, dc=dc, tsl=tsl:
                        e_.tensor_add(st.x[dc][:, tsl], dtmp[:, :], st.x[dc][:, tsl]), inc="vec")
                w2_war[par] = ctx.val("pe")
                if dc + 2 < 8:
                    ctx.wait("gpsimd", "pe", ctx.val("pe"))
                    c_w2[dc + 2] = ctx.dma("gpsimd", w2buf[par][:, :], dram["w2s"][e, dc + 2],
                                           inc="wdma")
            ctx.wait("tensor", "vec", ctx.val("vec"))
    ctx.wait("tensor", "vec", ctx.val("vec"))


def _emit_attn1(ctx, st, dram):
    AR, ARF, pb = st.AR, st.ARF, st.pb
    PT = [AR[:, 512 * kc:512 * (kc + 1)] for kc in range(16)]
    Z1 = [AR[:, 8192 + 2048 * c:8192 + 2048 * (c + 1)] for c in range(8)]
    zk = [AR[:, 24576:26624], AR[:, 26624:28672]]
    zv1 = AR[:, 28672:30720]
    qraw = AR[0:64, 30720:32768]
    qfull = AR[:, 30720:32768]
    zv2 = AR[64:128, 30720:32768]
    kraw = AR[0:64, 32768:34816]
    kfull = AR[:, 32768:34816]
    qp = AR[:, 34816:36864]
    kp = AR[:, 36864:38912]
    Tc = AR[:, 38912:40960]
    Ts = AR[:, 40960:43008]
    wqb = [AR[:, 44032:44544], AR[:, 44544:45056]]
    vdb = [AR[:, 45056:45184], AR[:, 45184:45312]]
    bdr = AR[:, 45312:45824]; bdi = AR[:, 45824:46336]; nbdi = AR[:, 46336:46848]
    wpb = AR[:, 46848:47872]
    permb = AR[:, 47872:48000]
    recB1 = AR[:, 48000:48512]
    Rt = ARF[:, 0:512]
    recF = ARF[:, 512:1024]
    tmpBf = ARF[:, 1536:2048]

    ctx.wait("gpsimd", "pe", ctx.val("pe"))
    ctx.wait("gpsimd", "vec", ctx.val("vec"))
    c_tc0 = ctx.dma("gpsimd", Tc[0:64, :], dram["tcq"][:, :], inc="wdma")
    c_ts0 = ctx.dma("gpsimd", Ts[0:64, :], dram["tsq"][:, :], inc="wdma")
    c_bdr = ctx.dma("gpsimd", bdr[:, :], dram["bdr"][:, :], inc="wdma")
    c_bdi = ctx.dma("gpsimd", bdi[:, :], dram["bdi"][:, :], inc="wdma")
    wkb = AR[:, 0:2048]
    wvb = AR[:, 2048:3584]
    c_wk = ctx.dma("gpsimd", wkb[:, :], dram["wk"][:, :], inc="wdma")
    c_wv = ctx.dma("gpsimd", wvb[:, :], dram["wv"][:, :], inc="wdma")
    c_pm = ctx.dma("gpsimd", permb[:, :], dram["c_perm"][:, :], inc="wdma")
    ctx.wait("vector", "wdma", c_bdi)
    nb_c = ctx.op("vector", lambda e: e.tensor_scalar_mul(nbdi[:, :], bdi[:, :], -1.0), inc="vec")
    ctx.op("vector", lambda e: e.memset(qp[64:128, :], 0.0))
    ctx.op("vector", lambda e: e.memset(kp[64:128, :], 0.0))
    ctx.op("vector", lambda e: e.memset(kfull[64:128, :], 0.0))
    ctx.op("vector", lambda e: e.memset(st.vtok3[:, :, 64:65], 1.0), inc="vec")

    ctx.wait("tensor", "wdma", c_wv)
    ctx.wait("tensor", "vec", ctx.val("vec"))
    ev = 0
    outspec = [(zk[0], wkb, 0, 0, 128), (zk[1], wkb, 128, 0, 128),
               (zv1, wvb, 0, 0, 128), (zv2, wvb, 128, 64, 64)]
    for ci, (dst, wsrc, coff, roff, rows) in enumerate(outspec):
        for nt in range(NT):
            sl = slice(512 * nt, 512 * (nt + 1))
            bank = pb[(ci * NT + nt) % 2]
            if ci * NT + nt >= 2:
                ctx.wait("tensor", "act", ev - 1)
            for fc in range(NFC):
                stride = 256 if wsrc is wkb else 192
                lh = wsrc[:, stride * fc + coff:stride * fc + coff + rows]
                pe = _mm(ctx, bank[roff:roff + rows, :], lh, st.xn[fc][:, sl],
                         start=(fc == 0), stop=(fc == NFC - 1),
                         inc="pe" if fc == NFC - 1 else None)
            ctx.wait("scalar", "pe", pe)
            ev = ctx.op("scalar", lambda e, dst=dst, sl=sl, bank=bank, roff=roff, rows=rows:
                        e.copy(dst[roff:roff + rows, sl] if roff == 0 else dst[:, sl],
                               bank[roff:roff + rows, :]), inc="act")
    ctx.wait("tensor", "act", ev)

    c_wq = {}
    c_vd = {}
    ctx.wait("gpsimd", "pe", ctx.val("pe"))
    c_wq[0] = ctx.dma("gpsimd", wqb[0][:, :], dram["wq"][0], inc="wdma")
    c_vd[0] = ctx.dma("gpsimd", vdb[0][:, :], dram["vd"][0], inc="wdma")
    hold = {'yevb': [ctx.val("vec")] * 2, 'tail': None, 'pe_last': 0, 'vev': 0}
    for h in range(16):
        hpar = h % 2
        zc = h // 2
        par = h % 2
        ctx.wait("tensor", "wdma", c_wq[h])
        if hold['vev']:
            ctx.wait("tensor", "act", hold['vev'])
        qel = [0] * NT
        for nt in range(NT):
            sl = slice(512 * nt, 512 * (nt + 1))
            bank = pb[nt % 2]
            if nt >= 2:
                ctx.wait("tensor", "act", qel[nt - 2])
            for fc in range(NFC):
                pe = _mm(ctx, bank[0:64, :], wqb[par][:, 64 * fc:64 * fc + 64], st.xn[fc][:, sl],
                         start=(fc == 0), stop=(fc == NFC - 1),
                         inc="pe" if fc == NFC - 1 else None)
            ctx.wait("scalar", "pe", pe)
            qel[nt] = ctx.op("scalar", lambda e, sl=sl, bank=bank:
                             e.copy(qraw[:, sl], bank[0:64, :]), inc="act")
        ctx.wait("tensor", "act", qel[NT - 1])
        ctx.wait("tensor", "vec", ctx.val("vec"))
        if h == 0:
            ctx.wait("tensor", "wdma", c_pm)
        qperm = [0] * NT
        for nt in range(NT):
            sl = slice(512 * nt, 512 * (nt + 1))
            qperm[nt] = _mm(ctx, pb[2 + nt][:, :], permb[:, :], qfull[:, sl],
                            True, True, inc="pe")
        rd = [0] * NT
        for nt in range(NT):
            sl = slice(512 * nt, 512 * (nt + 1))
            ctx.wait("vector", "pe", qperm[nt])
            ctx.op("vector", lambda e, sl=sl: e.tensor_mul(qp[0:64, sl], qraw[:, sl], Tc[0:64, sl]))
            ctx.op("vector", lambda e, sl=sl, bank=pb[2 + nt]:
                   e.tensor_mul(tmpBf[0:64, :], bank[0:64, :], Ts[0:64, sl]))
            rd[nt] = ctx.op("vector", lambda e, sl=sl:
                            e.tensor_add(qp[0:64, sl], qp[0:64, sl], tmpBf[0:64, :]), inc="vec")
        if h == 0:
            ctx.wait("tensor", "wdma", c_bdi)
            ctx.wait("tensor", "vec", nb_c)
        kel = [0] * NT
        for nt in range(NT):
            sl = slice(512 * nt, 512 * (nt + 1))
            bank = pb[nt % 2]
            if nt >= 2:
                ctx.wait("tensor", "act", kel[nt - 2])
            _mm(ctx, bank[0:32, :], bdr[:, 32 * h:32 * h + 32], zk[0][:, sl], True, False)
            _mm(ctx, bank[0:32, :], nbdi[:, 32 * h:32 * h + 32], zk[1][:, sl], False, True)
            _mm(ctx, bank[32:64, :], bdi[:, 32 * h:32 * h + 32], zk[0][:, sl], True, False)
            pe = _mm(ctx, bank[32:64, :], bdr[:, 32 * h:32 * h + 32], zk[1][:, sl], False, True,
                     inc="pe")
            ctx.wait("scalar", "pe", pe)
            kel[nt] = ctx.op("scalar", lambda e, sl=sl, bank=bank:
                             e.copy(kraw[:, sl], bank[0:64, :]), inc="act")
        ctx.wait("tensor", "wdma", c_vd[h])
        ctx.wait("tensor", "act", kel[NT - 1])
        v_ev = [0, 0]
        for halfv in range(2):
            bank = pb[halfv]
            for tci in range(8):
                tc = halfv * 8 + tci
                _mm(ctx, bank[:, 64 * tci:64 * tci + 64], zv1[:, 128 * tc:128 * tc + 128],
                    vdb[par][:, 0:64], start=True, stop=False)
                pe = _mm(ctx, bank[:, 64 * tci:64 * tci + 64], zv2[:, 128 * tc:128 * tc + 128],
                         vdb[par][64:128, 64:128], start=False, stop=True,
                         inc="pe" if tci == 7 else None)
            ctx.wait("scalar", "pe", pe)
            v_ev[halfv] = ctx.op("scalar", lambda e, halfv=halfv, bank=bank:
                                 e.copy(st.vtok3[:, 8 * halfv:8 * halfv + 8, 0:64], bank[:, :]),
                                 inc="act")
        hold['vev'] = v_ev[1]
        if h + 1 < 16:
            ctx.wait("gpsimd", "pe", ctx.val("pe"))
            c_wq[h + 1] = ctx.dma("gpsimd", wqb[1 - par][:, :], dram["wq"][h + 1], inc="wdma")
            c_vd[h + 1] = ctx.dma("gpsimd", vdb[1 - par][:, :], dram["vd"][h + 1], inc="wdma")
        kperm = [0] * NT
        for nt in range(NT):
            sl = slice(512 * nt, 512 * (nt + 1))
            ctx.wait("tensor", "vec", rd[nt])
            kperm[nt] = _mm(ctx, pb[2 + nt][:, :], permb[:, :], kfull[:, sl],
                            True, True, inc="pe")
        rdk = [0] * NT
        for nt in range(NT):
            sl = slice(512 * nt, 512 * (nt + 1))
            ctx.wait("vector", "pe", kperm[nt])
            ctx.op("vector", lambda e, sl=sl: e.scalar_tensor_tensor(
                kp[0:64, sl], kraw[:, sl], 8.0, Tc[0:64, sl], ALU.mult, ALU.mult))
            ctx.op("vector", lambda e, sl=sl, bank=pb[2 + nt]: e.scalar_tensor_tensor(
                tmpBf[0:64, :], bank[0:64, :], 8.0, Ts[0:64, sl], ALU.mult, ALU.mult))
            rdk[nt] = ctx.op("vector", lambda e, sl=sl:
                             e.tensor_add(kp[0:64, sl], kp[0:64, sl], tmpBf[0:64, :]), inc="vec")

        def gen_sty1(qt, exp_prev):
            Q0 = 512 * qt
            Kc = Q0 // 128 + 4
            qsl = slice(Q0, Q0 + 512)
            c_exp = [0] * Kc
            c_fix = [0] * Kc
            yb = pb[5 + qt % 2]

            def emit_st1(kc):
                pbS = (pb[3], pb[4], pb[7])[kc % 3]
                ctx.wait("tensor", "act", c_exp[kc - 3] if kc >= 3 else exp_prev)
                if kc == 0:
                    ctx.wait("tensor", "vec", rdk[qt])
                if qt == 0 and kc < 2:
                    ctx.wait("tensor", "vec", rdk[kc + 1])
                pe = _mm(ctx, pbS[:, :], kp[:, 128 * kc:128 * kc + 128], qp[:, qsl],
                         True, True, inc="pe")
                ctx.wait("scalar", "pe", pe)
                c_exp[kc] = ctx.op("scalar", lambda e, kc=kc, pbS=pbS:
                                   e.activation(PT[kc][:, 0:512], pbS[:, :], AF.Exp), inc="act")
                kr = kc - Q0 // 128
                if kr >= 0:
                    ctx.wait("vector", "act", c_exp[kc])
                    for jj in range(kr):
                        ctx.op("vector", lambda e, kc=kc, jj=jj:
                               e.memset(PT[kc][:, 128 * jj:128 * jj + 128], 0.0))
                    c_fix[kc] = ctx.op("vector", lambda e, kc=kc, kr=kr:
                                       e.tensor_mul(PT[kc][:, 128 * kr:128 * kr + 128],
                                                    PT[kc][:, 128 * kr:128 * kr + 128],
                                                    st.tri[:, :]), inc="vec")

            def emit_y1(kc):
                ctx.wait("tensor", "vec" if c_fix[kc] else "act",
                         c_fix[kc] if c_fix[kc] else c_exp[kc])
                if kc == 0:
                    ctx.wait("tensor", "vec", hold['yevb'][qt % 2])
                    if qt == 0:
                        ctx.wait("tensor", "vec", rdk[3])
                st_ = (kc == 0)
                sp_ = (kc == Kc - 1)
                return _mm(ctx, yb[0:65, :], st.vtok3[:, kc, :], PT[kc][:, 0:512],
                           st_, sp_, inc="pe" if sp_ else None)

            for kc in range(Kc):
                emit_st1(kc)
                yield
            for kc in range(Kc - 1):
                emit_y1(kc)
                yield
            hold['pe_last'] = emit_y1(Kc - 1)

        def gen_tail1(qt, hpar, zc, zoff):
            qsl = slice(512 * qt, 512 * qt + 512)
            yb = pb[5 + qt % 2]
            ctx.wait("vector", "pe", hold['pe_last'])
            vc = ctx.op("vector", lambda e, yb=yb:
                        e.tensor_copy(recB1[64:65, :], yb[64:65, :]), inc="vec")
            yield
            ctx.wait("tensor", "vec", vc)
            pe = _mm(ctx, pb[2][:, :], st.ones_bf[64:65, 0:128], recB1[64:65, :],
                     True, True, inc="pe")
            ctx.wait("vector", "pe", pe)
            ctx.op("vector", lambda e: e.reciprocal(Rt[0:64, :], pb[2][0:64, :]))
            hold['yevb'][qt % 2] = ctx.op(
                "vector", lambda e, yb=yb, zc=zc, zoff=zoff, qsl=qsl:
                e.tensor_mul(Z1[zc][zoff:zoff + 64, qsl], yb[0:64, :],
                             Rt[0:64, :]), inc="vec")
            yield

        for qt in range(NT):
            exp_prev = ctx.val("act")
            g_sty = gen_sty1(qt, exp_prev)
            steps = 0
            for _ in g_sty:
                steps += 1
                if steps == 1 and hold['tail'] is not None:
                    next(hold['tail'])
                elif steps == 3 and hold['tail'] is not None:
                    for _ in hold['tail']:
                        pass
                    hold['tail'] = None
            hold['tail'] = gen_tail1(qt, hpar, zc, 64 * hpar)
        for _ in hold['tail']:
            pass
        hold['tail'] = None

    ctx.wait("tensor", "vec", ctx.val("vec"))
    ctx.wait("tensor", "act", ctx.val("act"))
    wpbs = [AR[:, 0:1024], AR[:, 1024:2048]]
    c_wp = [0] * NFC
    ctx.wait("gpsimd", "pe", ctx.val("pe"))
    ctx.wait("gpsimd", "act", ctx.val("act"))
    c_wp[0] = ctx.dma("gpsimd", wpbs[0][:, :], dram["wproj"][0], inc="wdma")
    c_wp[1] = ctx.dma("gpsimd", wpbs[1][:, :], dram["wproj"][1], inc="wdma")
    wp_war = [0, 0]
    for cc in range(NFC):
        par = cc % 2
        ctx.wait("tensor", "wdma", c_wp[cc])
        ev = 0
        for nt in range(NT):
            sl = slice(512 * nt, 512 * (nt + 1))
            bank = pb[4 + nt % 2]
            if nt >= 2:
                ctx.wait("tensor", "vec", ev - 1)
            for dc in range(NFC):
                pe = _mm(ctx, bank[:, :], wpbs[par][:, 128 * dc:128 * dc + 128], Z1[dc][:, sl],
                         start=(dc == 0), stop=(dc == NFC - 1),
                         inc="pe" if dc == NFC - 1 else None)
            ctx.wait("vector", "pe", pe)
            ev = ctx.op("vector", lambda e, cc=cc, sl=sl, bank=bank:
                        e.tensor_add(st.x[cc][:, sl], bank[:, :], st.x[cc][:, sl]), inc="vec")
        wp_war[par] = ctx.val("pe")
        if cc + 2 < NFC:
            ctx.wait("gpsimd", "pe", wp_war[par])
            c_wp[cc + 2] = ctx.dma("gpsimd", wpbs[par][:, :], dram["wproj"][cc + 2], inc="wdma")
        ctx.wait("tensor", "vec", ev)
    ctx.wait("tensor", "vec", ctx.val("vec"))


def _emit_ffn1(ctx, st, dram):
    AR, ARF, pb = st.AR, st.ARF, st.pb
    ACT1 = [[AR[:, 8192 * ei + 2048 * ic:8192 * ei + 2048 * (ic + 1)] for ic in range(4)]
            for ei in range(2)]
    wgbuf = [AR[:, 16384:20480], AR[:, 20480:24576]]
    wubuf = [AR[:, 24576:28672], AR[:, 28672:32768]]
    wdbuf = [AR[:, 32768:36864], AR[:, 36864:40960]]
    uprime = AR[:, 40960:41472]
    cwbf1 = AR[:, 41472:41728]
    g1bf = AR[:, 41728:41856]
    sg = [AR[:, 41856:42368], AR[:, 42368:42880]]
    cwrowB1 = AR[:, 42880:43392]
    sact = [AR[:, 2048 * ic:2048 * (ic + 1)] for ic in range(8)]
    wsgb = [AR[:, 16384:17408], AR[:, 17408:18432]]
    wsub = [AR[:, 18432:19456], AR[:, 19456:20480]]
    wsdb = [AR[:, 20480:21504], AR[:, 21504:22528]]
    CWBe = ARF[:, 0:2048]
    cwrowF = ARF[:, 2048:2560]
    pexp = ARF[:, 2560:2576]
    nmF = ARF[:, 2576:2577]; m1F = ARF[:, 2577:2578]; m2F = ARF[:, 2578:2579]
    Ssum = ARF[:, 2579:2580]; rec1 = ARF[:, 2580:2581]
    ilt = ARF[:, 2581:2597]; it2 = ARF[:, 2597:2613]; ige = ARF[:, 2613:2629]
    cwF = ARF[:, 2629:2645]

    ctx.wait("gpsimd", "pe", ctx.val("pe"))
    ctx.wait("gpsimd", "vec", ctx.val("vec"))
    c_g1 = ctx.dma("gpsimd", g1bf[:, :], dram["g1w"][:, :], inc="wdma")

    ctx.wait("tensor", "wdma", c_g1)
    ctx.wait("tensor", "vec", ctx.val("vec"))
    sig_c = 0
    vread_c = 0
    cw_c = 0
    for j in range(16):
        tsl = slice(128 * j, 128 * (j + 1))
        if j > 0:
            ctx.wait("tensor", "act", sig_c)
            ctx.wait("tensor", "vec", vread_c)
        for fc in range(NFC):
            pe = _mm(ctx, pb[6][:, 0:16], st.xn[fc][:, tsl], g1bf[:, 16 * fc:16 * fc + 16],
                     start=(fc == 0), stop=(fc == NFC - 1),
                     inc="pe" if fc == NFC - 1 else None)
        ctx.wait("vector", "pe", pe)
        if j > 0:
            ctx.wait("vector", "act", sig_c)

        def vf(builder):
            c = ctx.op("vector", builder, inc="vec")
            ctx.wait("vector", "vec", c)
            return c
        vf(lambda e: e.reduce_max(nmF[:, :], pb[6][:, 0:16], axis=AX.X, negate=True))
        vf(lambda e: e.reduce_max(m1F[:, :], pb[6][:, 0:16], axis=AX.X))
        vc = ctx.val("vec")
        ctx.wait("scalar", "vec", vc)
        sig_c = ctx.op("scalar", lambda e:
                       e.activation(pexp[:, :], pb[6][:, 0:16], AF.Exp, bias=nmF[:, 0:1],
                                    accum_out=Ssum[:, 0:1]), inc="act")
        ctx.wait("vector", "act", sig_c)
        vf(lambda e: e.reciprocal(rec1[:, :], Ssum[:, :]))
        vf(lambda e: e.tensor_scalar(ilt[:, :], pb[6][:, 0:16], m1F[:, 0:1], None, ALU.is_lt))
        vf(lambda e: e.tensor_mul(it2[:, :], pb[6][:, 0:16], ilt[:, :]))
        vf(lambda e: e.reduce_max(m2F[:, :], it2[:, :], axis=AX.X))
        vf(lambda e: e.tensor_scalar(ige[:, :], pb[6][:, 0:16], m2F[:, 0:1], None, ALU.is_ge))
        vf(lambda e: e.scalar_tensor_tensor(cwF[:, :], pexp[:, :], rec1[:, 0:1], ige[:, :],
                                            ALU.mult, ALU.mult))
        vread_c = cw_c = ctx.op("vector", lambda e, j=j:
                                e.tensor_copy(cwbf1[:, 16 * j:16 * j + 16], cwF[:, :]), inc="vec")

    wg_war = [0, 0]; wu_war = [0, 0]; wd_war = [0, 0]
    c_wg = [0, 0]; c_wu = [0, 0]; c_wd = [0, 0]
    for ep in range(8):
        acts_done = 0
        for ei in range(2):
            e_ = 2 * ep + ei
            par = ei
            ctx.wait("gpsimd", "pe", wg_war[par])
            c_wg[par] = ctx.dma("gpsimd", wgbuf[par][:, :], dram["wgs"][e_], inc="wdma")
            ctx.wait("gpsimd", "pe", wu_war[par])
            c_wu[par] = ctx.dma("gpsimd", wubuf[par][:, :], dram["wus"][e_], inc="wdma")
            ctx.wait("gpsimd", "pe", wd_war[par])
            c_wd[par] = ctx.dma("gpsimd", wdbuf[par][:, :], dram["wds"][e_], inc="wdma")
            ctx.wait("tensor", "vec", cw_c if (ep == 0 and ei == 0) else ctx.val("vec"))
            mb = [(6, 0), (6, 32), (7, 0), (7, 32)]
            cr = [0, 32, 64, 0]
            pe = 0
            for nt in range(NT):
                bk, rw = mb[nt]
                for jj in range(4):
                    j = nt * 4 + jj
                    pe = _mm(ctx, pb[bk][rw:rw + 1, 128 * jj:128 * jj + 128],
                             cwbf1[:, 16 * j + e_:16 * j + e_ + 1], st.ident[:, 0:128],
                             start=True, stop=True,
                             inc="pe" if (nt == NT - 1 and jj == 3) else None)
            ctx.wait("vector", "pe", pe)
            rows_c = [0] * NT
            for nt in range(3):
                bk, rw = mb[nt]
                rows_c[nt] = ctx.op("vector", lambda e, nt=nt, bk=bk, rw=rw:
                                    e.tensor_copy(cwrowB1[cr[nt]:cr[nt] + 1, 0:512],
                                                  pb[bk][rw:rw + 1, 0:512]), inc="vec")
            cpy = [0] * NT
            for nt in range(NT):
                bank = pb[6 + nt % 2]
                if nt == 0:
                    ctx.wait("tensor", "vec", rows_c[1])
                elif nt == 1:
                    ctx.wait("tensor", "vec", rows_c[3])
                else:
                    ctx.wait("tensor", "vec", max(rows_c[nt], cpy[nt - 2]))
                pe = _mm(ctx, bank[:, :], st.ones_bf[cr[nt]:cr[nt] + 1, 0:128],
                         cwrowB1[cr[nt]:cr[nt] + 1, 0:512], True, True, inc="pe")
                ctx.wait("vector", "pe", pe)
                if nt == 0:
                    rows_c[3] = ctx.op("vector", lambda e:
                                       e.tensor_copy(cwrowB1[0:1, 0:512], pb[7][32:33, 0:512]),
                                       inc="vec")
                cpy[nt] = ctx.op("vector", lambda e, nt=nt, bank=bank:
                                 e.tensor_copy(CWBe[:, 512 * nt:512 * nt + 512], bank[:, :]),
                                 inc="vec")
            ctx.wait("tensor", "wdma", c_wu[par])
            silu_c = {}
            act_c = {}
            for ic in range(4):
                for nt in range(NT):
                    i2 = ic * NT + nt
                    spar = i2 % 2
                    tsl = slice(512 * nt, 512 * (nt + 1))
                    bg = pb[i2 % 2]
                    bu = pb[2 + i2 % 2]
                    if i2 >= 2:
                        ctx.wait("tensor", "act", silu_c[i2 - 2])
                        ctx.wait("tensor", "vec", act_c[i2 - 2])
                    for fc in range(NFC):
                        peg = _mm(ctx, bg[:, :], wgbuf[par][:, 512 * fc + 128 * ic:512 * fc + 128 * ic + 128],
                                  st.xn[fc][:, tsl], start=(fc == 0), stop=(fc == NFC - 1),
                                  inc="pe" if fc == NFC - 1 else None)
                    for fc in range(NFC):
                        peu = _mm(ctx, bu[:, :], wubuf[par][:, 512 * fc + 128 * ic:512 * fc + 128 * ic + 128],
                                  st.xn[fc][:, tsl], start=(fc == 0), stop=(fc == NFC - 1),
                                  inc="pe" if fc == NFC - 1 else None)
                    ctx.wait("scalar", "pe", peg)
                    if i2 >= 2:
                        ctx.wait("scalar", "vec", act_c[i2 - 2])
                    silu_c[i2] = ctx.op("scalar", lambda e, spar=spar, bg=bg:
                                        e.activation(sg[spar][:, :], bg[:, :], AF.Silu), inc="act")
                    ctx.wait("vector", "pe", peu)
                    ctx.wait("vector", "act", silu_c[i2])
                    ctx.op("vector", lambda e, nt=nt, bu=bu:
                           e.tensor_mul(uprime[:, :], bu[:, :], CWBe[:, 512 * nt:512 * nt + 512]))
                    act_c[i2] = ctx.op("vector", lambda e, ei=ei, ic=ic, tsl=tsl, spar=spar:
                                       e.tensor_mul(ACT1[ei][ic][:, tsl], sg[spar][:, :],
                                                    uprime[:, :]), inc="vec")
            wg_war[par] = wu_war[par] = ctx.val("pe")
            acts_done = act_c[15]
        ctx.wait("tensor", "vec", acts_done)
        ctx.wait("tensor", "wdma", max(c_wd))
        ev = {}
        for dc in range(NFC):
            for nt in range(NT):
                i2 = dc * NT + nt
                tsl = slice(512 * nt, 512 * (nt + 1))
                bank = pb[4 + i2 % 2]
                if i2 >= 2:
                    ctx.wait("tensor", "vec", ev[i2 - 2])
                for ei in range(2):
                    for ic in range(4):
                        pe = _mm(ctx, bank[:, :],
                                 wdbuf[ei][:, 1024 * ic + 128 * dc:1024 * ic + 128 * dc + 128],
                                 ACT1[ei][ic][:, tsl],
                                 start=(ei == 0 and ic == 0), stop=(ei == 1 and ic == 3),
                                 inc="pe" if (ei == 1 and ic == 3) else None)
                ctx.wait("vector", "pe", pe)
                ev[i2] = ctx.op("vector", lambda e, dc=dc, tsl=tsl, bank=bank:
                                e.tensor_add(st.x[dc][:, tsl], bank[:, :], st.x[dc][:, tsl]),
                                inc="vec")
        wd_war[0] = wd_war[1] = ctx.val("pe")
        ctx.wait("tensor", "vec", ctx.val("vec"))

    ws_war = [0, 0]
    c_ws = {}
    ctx.wait("gpsimd", "pe", ctx.val("pe"))
    for ic0 in (0, 1):
        c_ws[ic0] = ctx.dma("gpsimd", wsgb[ic0][:, :], dram["wsgs"][ic0], inc="wdma")
        c_ws[ic0] = ctx.dma("gpsimd", wsub[ic0][:, :], dram["wsus"][ic0], inc="wdma")
    silu_c = {}
    act_c = {}
    ctx.wait("tensor", "vec", ctx.val("vec"))
    for ic in range(8):
        par = ic % 2
        ctx.wait("tensor", "wdma", c_ws[ic])
        for nt in range(NT):
            i2 = ic * NT + nt
            spar = i2 % 2
            tsl = slice(512 * nt, 512 * (nt + 1))
            bg = pb[i2 % 2]
            bu = pb[2 + i2 % 2]
            if i2 >= 2:
                ctx.wait("tensor", "act", silu_c[i2 - 2])
                ctx.wait("tensor", "vec", act_c[i2 - 2])
            for fc in range(NFC):
                peg = _mm(ctx, bg[:, :], wsgb[par][:, 128 * fc:128 * fc + 128],
                          st.xn[fc][:, tsl], start=(fc == 0), stop=(fc == NFC - 1),
                          inc="pe" if fc == NFC - 1 else None)
            for fc in range(NFC):
                peu = _mm(ctx, bu[:, :], wsub[par][:, 128 * fc:128 * fc + 128],
                          st.xn[fc][:, tsl], start=(fc == 0), stop=(fc == NFC - 1),
                          inc="pe" if fc == NFC - 1 else None)
            ctx.wait("scalar", "pe", peg)
            if i2 >= 2:
                ctx.wait("scalar", "vec", act_c[i2 - 2])
            silu_c[i2] = ctx.op("scalar", lambda e, spar=spar, bg=bg:
                                e.activation(sg[spar][:, :], bg[:, :], AF.Silu), inc="act")
            ctx.wait("vector", "pe", peu)
            ctx.wait("vector", "act", silu_c[i2])
            act_c[i2] = ctx.op("vector", lambda e, ic=ic, tsl=tsl, spar=spar, bu=bu:
                               e.tensor_mul(sact[ic][:, tsl], bu[:, :], sg[spar][:, :]), inc="vec")
        if ic + 2 < 8:
            ctx.wait("gpsimd", "pe", ctx.val("pe"))
            c_ws[ic + 2] = ctx.dma("gpsimd", wsgb[par][:, :], dram["wsgs"][ic + 2], inc="wdma")
            c_ws[ic + 2] = ctx.dma("gpsimd", wsub[par][:, :], dram["wsus"][ic + 2], inc="wdma")
    ctx.wait("tensor", "vec", act_c[31])
    c_wsd = {}
    ctx.wait("gpsimd", "pe", ctx.val("pe"))
    for dc0 in (0, 1):
        c_wsd[dc0] = ctx.dma("gpsimd", wsdb[dc0][:, :], dram["wsds"][dc0], inc="wdma")
    ev = {}
    for dc in range(NFC):
        par = dc % 2
        ctx.wait("tensor", "wdma", c_wsd[dc])
        for nt in range(NT):
            i2 = dc * NT + nt
            tsl = slice(512 * nt, 512 * (nt + 1))
            bank = pb[4 + i2 % 2]
            if i2 >= 2:
                ctx.wait("tensor", "vec", ev[i2 - 2])
            for ic in range(8):
                pe = _mm(ctx, bank[:, :], wsdb[par][:, 128 * ic:128 * ic + 128],
                         sact[ic][:, tsl], start=(ic == 0), stop=(ic == 7),
                         inc="pe" if ic == 7 else None)
            ctx.wait("vector", "pe", pe)
            ev[i2] = ctx.op("vector", lambda e, dc=dc, tsl=tsl, bank=bank:
                            e.tensor_add(st.x[dc][:, tsl], bank[:, :], st.x[dc][:, tsl]),
                            inc="vec")
        if dc + 2 < 8:
            ctx.wait("gpsimd", "pe", ctx.val("pe"))
            c_wsd[dc + 2] = ctx.dma("gpsimd", wsdb[par][:, :], dram["wsds"][dc + 2], inc="wdma")
    ctx.wait("tensor", "vec", ctx.val("vec"))



def kernel(**inputs):
    shared = _prep_shared(inputs)
    x = np.asarray(inputs["L0_attn_x"], np.float32)
    nc = build_program()
    in_maps = []
    for c in range(8):
        m = dict(shared)
        m["xin"] = np.ascontiguousarray(x[c].T).reshape(NFC, 128, T)
        in_maps.append(m)
    res = run_bass_kernel_spmd(nc, in_maps, core_ids=list(range(8)))
    out = np.empty((8, T, D), np.float32)
    for c in range(8):
        out[c] = res.results[c]["out"].reshape(D, T).T
    return out


# version 17
# speedup vs baseline: 1.0286x; 1.0286x over previous
import numpy as np
import concourse.bass as bass
import concourse.mybir as mybir
from concourse.bass_utils import run_bass_kernel_spmd

F32 = mybir.dt.float32
BF16 = mybir.dt.bfloat16
AF = mybir.ActivationFunctionType
ALU = mybir.AluOpType
AX = mybir.AxisListType

T = 2048
D = 1024
NT = T // 512
NFC = D // 128
EPS_LN = 1e-5
DEV_A0_HEADS = 16


def _prep_shared(inp):
    f32 = np.float32
    d = {}
    latw = np.asarray(inp["L0_attn_latent_w"], f32)
    latp = np.zeros((8, 128, 384), f32)
    latp[:, :, :288] = latw.reshape(8, 128, 288)
    d["latw"] = np.ascontiguousarray(latp.transpose(1, 0, 2)).reshape(128, 8 * 384)
    latb = np.zeros(384, f32); latb[:288] = np.asarray(inp["L0_attn_latent_b"], f32)
    d["latb"] = np.ascontiguousarray(latb.reshape(3, 128).T)
    latbr = np.zeros((1, 384), f32); latbr[0, :288] = latb[:288]
    d["latbr"] = latbr
    wd = np.asarray(inp["L0_attn_Wd_w"], f32)
    wdp = np.zeros((16, 8, 128, 384), f32)
    wdp[:, :, :, :288] = wd.reshape(16, 8, 128, 288)
    d["wd"] = np.ascontiguousarray(wdp.transpose(0, 2, 1, 3)).reshape(16, 128, 8 * 384)
    wdb = np.zeros((16, 384), f32); wdb[:, :288] = np.asarray(inp["L0_attn_Wd_b"], f32)
    d["wdb"] = np.ascontiguousarray(
        wdb.reshape(16, 3, 128).transpose(2, 0, 1)).reshape(128, 48)
    ow_src = np.asarray(inp["L0_attn_out_w"], f32).reshape(16, 288, 1024)
    ow = np.zeros((16, 3, 128, 1024), f32)
    ow[:, 0] = ow_src[:, 0:128]
    ow[:, 1] = ow_src[:, 128:256]
    ow[:, 2, 0:32] = ow_src[:, 256:288]
    ow[0, 2, 32] = np.asarray(inp["L0_attn_out_b"], f32)
    d["outw"] = np.ascontiguousarray(ow.transpose(0, 2, 1, 3)).reshape(16, 128, 3 * 1024)
    w1 = np.concatenate([np.asarray(inp["L0_ffn_w1"], f32),
                         np.asarray(inp["L0_ffn_new_w1"], f32)], axis=0)
    w3 = np.concatenate([np.asarray(inp["L0_ffn_w3"], f32),
                         np.asarray(inp["L0_ffn_new_w3"], f32)], axis=0)
    w2 = np.concatenate([np.asarray(inp["L0_ffn_w2"], f32),
                         np.asarray(inp["L0_ffn_new_w2"], f32)], axis=0)
    d["w1s"] = np.ascontiguousarray(
        w1.reshape(9, 22, 128, 8, 128).transpose(0, 1, 4, 3, 2)).reshape(9, 22, 128, 1024)
    d["w3s"] = np.ascontiguousarray(
        w3.reshape(9, 22, 128, 8, 128).transpose(0, 1, 4, 3, 2)).reshape(9, 22, 128, 1024)
    d["w2s"] = np.ascontiguousarray(
        w2.reshape(9, 8, 128, 22, 128).transpose(0, 1, 4, 3, 2)).reshape(9, 8, 128, 2816)
    g0 = np.concatenate([np.asarray(inp["L0_ffn_gate_weight"], f32),
                         np.asarray(inp["L0_ffn_new_gate_weight"], f32)], axis=0)
    d["g0w"] = np.ascontiguousarray(
        g0.reshape(9, 8, 128).transpose(2, 1, 0)).reshape(128, 72)
    g0b = np.zeros((1, 9), f32); g0b[0, 8] = np.asarray(inp["L0_ffn_gate_bias"], f32)[0]
    d["g0b"] = g0b
    H1, HD1 = 16, 64
    perm = np.zeros(1024, np.int64)
    for h in range(16):
        perm[h * 64:h * 64 + 32] = h * 64 + 2 * np.arange(32)
        perm[h * 64 + 32:h * 64 + 64] = h * 64 + 2 * np.arange(32) + 1
    wq = np.asarray(inp["L1_attn_Wq"], f32)[:, perm]
    d["wq"] = np.ascontiguousarray(
        wq.reshape(8, 128, 16, 64).transpose(2, 1, 0, 3)).reshape(16, 128, 512)
    wk = np.asarray(inp["L1_attn_Wk"], f32)
    wkp = np.zeros((1024, 256), f32)
    for f in range(32):
        for r in range(3):
            wkp[:, f * 3 + r] = wk[:, f * 6 + 2 * r]
            wkp[:, 128 + f * 3 + r] = wk[:, f * 6 + 2 * r + 1]
    d["wk"] = np.ascontiguousarray(
        wkp.reshape(8, 128, 256).transpose(1, 0, 2)).reshape(128, 2048)
    wv = np.asarray(inp["L1_attn_Wv"], f32)
    d["wv"] = np.ascontiguousarray(
        wv.reshape(8, 128, 192).transpose(1, 0, 2)).reshape(128, 1536)
    kd = np.asarray(inp["L1_attn_key_decoder"], f32)
    bdr = np.zeros((128, 512), f32); bdi = np.zeros((128, 512), f32)
    for f in range(32):
        for h in range(16):
            for r in range(3):
                bdr[f * 3 + r, h * 32 + f] = kd[f, h, r, 0]
                bdi[f * 3 + r, h * 32 + f] = kd[f, h, r, 1]
    d["bdr"], d["bdi"] = bdr, bdi
    vdec = np.asarray(inp["L1_attn_value_decoder"], f32)
    vdp = np.zeros((16, 128, 128), f32)
    vdp[:, :, 0:64] = vdec[:, 0:128]
    vdp[:, 64:128, 64:128] = vdec[:, 128:192]
    d["vd"] = np.ascontiguousarray(vdp)
    d["wproj"] = np.ascontiguousarray(
        np.asarray(inp["L1_attn_Wproj"], f32).reshape(8, 128, 8, 128)
        .transpose(2, 1, 0, 3)).reshape(8, 128, 1024)
    half = HD1 // 2
    freq = (1.0 / (10000.0 ** (np.arange(half, dtype=f32) / half))).astype(f32)
    ang = np.outer(np.arange(T, dtype=f32), freq)
    cos = np.cos(ang).T.astype(f32)
    sin = np.sin(ang).T.astype(f32)
    d["tcq"] = np.ascontiguousarray(np.concatenate([cos, cos], 0) * 0.125)
    d["tsq"] = np.ascontiguousarray(np.concatenate([-sin, sin], 0) * 0.125)
    d["tck"] = np.ascontiguousarray(np.concatenate([cos, cos], 0))
    d["tsk"] = np.ascontiguousarray(np.concatenate([-sin, sin], 0))
    d["g1w"] = np.ascontiguousarray(
        np.asarray(inp["L1_ffn_gate_w"], f32).reshape(16, 8, 128).transpose(2, 1, 0)).reshape(128, 128)
    d["wgs"] = np.ascontiguousarray(
        np.asarray(inp["L1_ffn_w_gate"], f32).reshape(16, 8, 128, 4, 128)
        .transpose(0, 2, 1, 3, 4)).reshape(16, 128, 4096)
    d["wus"] = np.ascontiguousarray(
        np.asarray(inp["L1_ffn_w_up"], f32).reshape(16, 8, 128, 4, 128)
        .transpose(0, 2, 1, 3, 4)).reshape(16, 128, 4096)
    d["wds"] = np.ascontiguousarray(
        np.asarray(inp["L1_ffn_w_down"], f32).reshape(16, 4, 128, 8, 128)
        .transpose(0, 2, 1, 3, 4)).reshape(16, 128, 4096)
    d["wsgs"] = np.ascontiguousarray(
        np.asarray(inp["L1_ffn_ws_gate"], f32).reshape(8, 128, 8, 128)
        .transpose(2, 1, 0, 3)).reshape(8, 128, 1024)
    d["wsus"] = np.ascontiguousarray(
        np.asarray(inp["L1_ffn_ws_up"], f32).reshape(8, 128, 8, 128)
        .transpose(2, 1, 0, 3)).reshape(8, 128, 1024)
    d["wsds"] = np.ascontiguousarray(
        np.asarray(inp["L1_ffn_ws_down"], f32).reshape(8, 128, 8, 128)
        .transpose(2, 1, 0, 3)).reshape(8, 128, 1024)
    d["c_ident"] = np.eye(128, dtype=f32)
    tri = np.zeros((128, 128), f32)
    iu = np.triu_indices(128)
    tri[iu] = 1.0
    d["c_tri"] = tri
    pm = np.zeros((128, 128), f32)
    for i in range(64):
        pm[(i + 32) % 64, i] = 1.0
    d["c_perm"] = pm
    return d



ENGINES = ("sync", "tensor", "vector", "scalar", "gpsimd")


class Ctx:
    def __init__(self, nc):
        self.nc = nc
        self.ops = {e: [] for e in ENGINES}
        self.sems = {}

    def _sem(self, name):
        if name not in self.sems:
            self.sems[name] = [self.nc.alloc_semaphore(name), 0]
        return self.sems[name]

    def val(self, name):
        return self._sem(name)[1]

    def op(self, eng, builder, inc=None, inck=1):
        if inc is not None:
            s = self._sem(inc)
            s[1] += inck
            h = s[0]
            def f(e, b=builder, h=h, k=inck):
                b(e).then_inc(h, k)
            self.ops[eng].append(f)
            return s[1]
        else:
            def f(e, b=builder):
                b(e)
            self.ops[eng].append(f)
            return None

    def wait(self, eng, sname, val):
        if val <= 0:
            return
        h = self._sem(sname)[0]
        def f(e, h=h, v=val):
            e.wait_ge(h, v)
        self.ops[eng].append(f)

    def dma(self, eng, out_ap, in_ap, inc):
        return self.op(eng, lambda e, o=out_ap, i=in_ap: e.dma_start(o, i), inc=inc, inck=16)


def _mm(ctx, out, lhsT, rhs, start, stop, inc=None):
    return ctx.op("tensor",
                  lambda e, o=out, l=lhsT, r=rhs, st=start, sp=stop:
                  e.matmul(o, l, r, start=st, stop=sp), inc=inc)



ALL_PHASES = ("ln0", "attn0", "ffn0", "ln1", "attn1", "ffn1")


class St:
    pass


def _alloc(nc):
    st = St()
    st.x = [nc.alloc_sbuf_tensor(f"x{i}", [128, T], F32) for i in range(NFC)]
    st.xn = [nc.alloc_sbuf_tensor(f"xn{i}", [128, T], BF16) for i in range(NFC)]
    st.onesM = nc.alloc_sbuf_tensor("onesM", [128, 128], F32)
    st.ones_bf = nc.alloc_sbuf_tensor("ones_bf", [128, 128], BF16)
    st.ident = nc.alloc_sbuf_tensor("ident", [128, 128], BF16)
    st.tri = nc.alloc_sbuf_tensor("tri", [128, 128], BF16)
    st.latb = nc.alloc_sbuf_tensor("sb_latb", [128, 3], F32)
    st.wdb = nc.alloc_sbuf_tensor("sb_wdb", [128, 48], F32)
    st.AR = nc.alloc_sbuf_tensor("AR", [128, 48640], BF16)
    st.ARF = nc.alloc_sbuf_tensor("ARF", [128, 2656], F32)
    st.vtok3 = nc.alloc_sbuf_tensor("vtok3", [128, 16, 65], BF16)
    st.pb = [nc.alloc_psum_tensor(f"pb{i}", [128, 512], F32) for i in range(8)]
    st.ptb = None
    return st


def _emit_prologue(ctx, st, dram):
    for fc in range(NFC):
        ctx.dma("sync", st.x[fc][:, :], dram["xin"][fc], inc="xdma")
    ctx.dma("gpsimd", st.ident[:, :], dram["c_ident"][:, :], inc="wdma")
    ctx.dma("gpsimd", st.tri[:, :], dram["c_tri"][:, :], inc="wdma")
    ctx.dma("sync", st.latb[:, :], dram["latb"][:, :], inc="xdma")
    ctx.dma("sync", st.wdb[:, :], dram["wdb"][:, :], inc="xdma")
    ctx.op("vector", lambda e: e.memset(st.onesM[:, :], 1.0), inc="vec")
    ctx.op("vector", lambda e: e.memset(st.ones_bf[:, :], 1.0), inc="vec")
    st.after_prologue = dict(xdma=ctx.val("xdma"), wdma=ctx.val("wdma"), vec=ctx.val("vec"))


def _emit_ln(ctx, st, li):
    pb, ARF = st.pb, st.ARF
    if li == 0:
        ctx.wait("vector", "xdma", st.after_prologue["xdma"])
        ctx.wait("scalar", "xdma", st.after_prologue["xdma"])
        ctx.wait("tensor", "vec", st.after_prologue["vec"])
    else:
        ctx.wait("vector", "pe", ctx.val("pe"))
        ctx.wait("scalar", "pe", ctx.val("pe"))
        ctx.wait("scalar", "vec", ctx.val("vec"))
        ctx.wait("tensor", "act", ctx.val("act"))
    mrow = ARF[0:1, 0:512]; erow = ARF[0:1, 512:1024]; rrow = ARF[0:1, 1024:1536]
    stats_c = [0] * NT
    norm_c = [0] * NT
    sqrt_c = [0] * NT
    last_mm = [0] * NT
    bcast_pe = [0] * NT
    sq_mm = {}

    def emit_sums(nt):
        sl = slice(512 * nt, 512 * (nt + 1))
        bsum = pb[2 * (nt % 2)]
        bsq = pb[2 * (nt % 2) + 1]
        vc = None
        for fc in range(NFC):
            vc = ctx.op("vector", lambda e, fc=fc, sl=sl:
                        e.tensor_copy(st.xn[fc][:, sl], st.x[fc][:, sl]), inc="vec")
        ctx.wait("tensor", "vec", vc)
        if nt >= 2:
            ctx.wait("tensor", "vec", stats_c[nt - 2])
        for fc in range(NFC):
            _mm(ctx, bsum[0:1, :], st.ones_bf[:, 0:1], st.xn[fc][:, sl],
                start=(fc == 0), stop=(fc == NFC - 1))
        for fc in range(NFC):
            k = fc % 2
            war = sq_mm.get(fc - 2 if fc >= 2 else NFC - 2 + fc)
            if war:
                ctx.wait("scalar", "pe", war)
            sqv = ctx.op("scalar", lambda e, fc=fc, sl=sl, k=k:
                         e.activation(st.AR[:, 512 * k:512 * k + 512], st.x[fc][:, sl],
                                      AF.Square), inc="act")
            ctx.wait("tensor", "act", sqv)
            sq_mm[fc] = _mm(ctx, bsq[0:1, :], st.ones_bf[:, 0:1],
                            st.AR[:, 512 * k:512 * k + 512],
                            start=(fc == 0), stop=(fc == NFC - 1), inc="pe")
        last_mm[nt] = sq_mm[NFC - 1]

    def emit_stats_norm(nt):
        sl = slice(512 * nt, 512 * (nt + 1))
        bsum = pb[2 * (nt % 2)]
        bsq = pb[2 * (nt % 2) + 1]
        mb = pb[4 + 2 * (nt % 2)]
        rb = pb[5 + 2 * (nt % 2)]
        ctx.wait("vector", "pe", max(last_mm[nt], bcast_pe[nt - 1] if nt else 0))
        if nt:
            ctx.wait("vector", "act", sqrt_c[nt - 1])
        ctx.op("vector", lambda e: e.tensor_scalar(mrow, bsum[0:1, :], 1.0 / D, None, ALU.mult))
        ctx.op("vector", lambda e: e.tensor_scalar(erow, bsq[0:1, :], 1.0 / D, None, ALU.mult))
        ctx.op("vector", lambda e: e.tensor_tensor(rrow, mrow, mrow, ALU.mult))
        ctx.op("vector", lambda e: e.tensor_sub(erow, erow, rrow))
        ctx.op("vector", lambda e: e.tensor_scalar(erow, erow, EPS_LN, None, ALU.add))
        stats_c[nt] = ctx.op("vector", lambda e: e.reciprocal(erow, erow), inc="vec")
        ctx.wait("scalar", "vec", stats_c[nt])
        sqrt_c[nt] = ctx.op("scalar", lambda e: e.activation(rrow, erow, AF.Sqrt), inc="act")
        ctx.wait("tensor", "act", sqrt_c[nt])
        if nt >= 2:
            ctx.wait("tensor", "vec", norm_c[nt - 2])
        _mm(ctx, mb[:, :], st.onesM[0:1, 0:128], mrow, start=True, stop=True)
        bcast_pe[nt] = pe = _mm(ctx, rb[:, :], st.onesM[0:1, 0:128], rrow, start=True, stop=True,
                                inc="pe")
        ctx.wait("vector", "pe", pe)
        for fc in range(NFC):
            ctx.op("vector", lambda e, fc=fc, sl=sl:
                   e.tensor_sub(ARF[:, 1536:2048], st.x[fc][:, sl], mb[:, :]))
            norm_c[nt] = ctx.op("vector", lambda e, fc=fc, sl=sl:
                                e.tensor_mul(st.xn[fc][:, sl], ARF[:, 1536:2048], rb[:, :]),
                                inc="vec")

    emit_sums(0)
    emit_sums(1)
    emit_stats_norm(0)
    emit_sums(2)
    emit_stats_norm(1)
    emit_sums(3)
    emit_stats_norm(2)
    emit_stats_norm(3)
    ctx.wait("tensor", "vec", norm_c[NT - 1])
    st.ln_done = dict(vec=ctx.val("vec"))


def _emit_store_dbg(ctx, st, dram):
    ctx.wait("gpsimd", "vec", ctx.val("vec"))
    ctx.wait("gpsimd", "pe", ctx.val("pe"))
    ctx.wait("gpsimd", "act", ctx.val("act"))
    ctx.wait("sync", "vec", ctx.val("vec"))
    ctx.dma("gpsimd", dram["out"][0][:, 0:144], st.AR[:, 33280:33424], inc="stdma")
    ctx.dma("sync", dram["out"][1][:, 0:1024], st.ARF[:, 0:1024], inc="stdma2")
    ctx.dma("gpsimd", dram["out"][2][:, 0:1024], st.AR[:, 0:1024], inc="stdma")
    ctx.dma("gpsimd", dram["out"][3][:, 0:1024], st.AR[:, 21504:22528], inc="stdma")
    ctx.dma("gpsimd", dram["out"][4][:, 0:72], st.AR[:, 33424:33496], inc="stdma")
    ctx.dma("gpsimd", dram["out"][5][:, 0:2048], st.AR[:, 26624:28672], inc="stdma")
    ctx.dma("gpsimd", dram["out"][6][:, 0:1024], st.AR[:, 22528:23552], inc="stdma")
    ctx.wait("gpsimd", "stdma", ctx.val("stdma"))
    ctx.wait("sync", "stdma2", ctx.val("stdma2"))


def _emit_store(ctx, st, dram, src="x"):
    tiles = st.x if src == "x" else st.xn
    eng = "sync" if src == "x" else "gpsimd"
    ctx.wait(eng, "vec", ctx.val("vec"))
    ctx.wait(eng, "pe", ctx.val("pe"))
    ctx.wait(eng, "act", ctx.val("act"))
    for fc in range(NFC):
        ctx.dma(eng, dram["out"][fc], tiles[fc][:, :], inc="stdma")
    ctx.wait(eng, "stdma", ctx.val("stdma"))


def build_program(phases=ALL_PHASES, store_src="x"):
    nc = bass.Bass("TRN2", target_bir_lowering=False, detect_race_conditions=False)
    names_shapes = {
        "xin": (NFC, 128, T), "latw": (128, 3072), "latb": (128, 3), "latbr": (1, 384),
        "wd": (16, 128, 3072),
        "wdb": (128, 48), "outw": (16, 128, 3072), "w1s": (9, 22, 128, 1024),
        "w3s": (9, 22, 128, 1024), "w2s": (9, 8, 128, 2816), "g0w": (128, 72), "g0b": (1, 9),
        "wq": (16, 128, 512), "wk": (128, 2048), "wv": (128, 1536), "bdr": (128, 512),
        "bdi": (128, 512), "vd": (16, 128, 128), "wproj": (8, 128, 1024),
        "tcq": (64, T), "tsq": (64, T), "tck": (64, T), "tsk": (64, T),
        "g1w": (128, 128), "wgs": (16, 128, 4096), "wus": (16, 128, 4096),
        "wds": (16, 128, 4096), "wsgs": (8, 128, 1024), "wsus": (8, 128, 1024),
        "wsds": (8, 128, 1024), "c_ident": (128, 128), "c_tri": (128, 128),
        "c_perm": (128, 128),
    }
    dram = {k: nc.dram_tensor(k, list(v), F32, kind="ExternalInput")
            for k, v in names_shapes.items()}
    dram["out"] = nc.dram_tensor("out", [NFC, 128, T], F32, kind="ExternalOutput")
    st = _alloc(nc)
    ctx = Ctx(nc)
    _emit_prologue(ctx, st, dram)
    for ph in phases:
        if ph == "ln0":
            _emit_ln(ctx, st, 0)
        elif ph == "ln1":
            _emit_ln(ctx, st, 1)
        elif ph == "attn0":
            _emit_attn0(ctx, st, dram)
        elif ph == "ffn0":
            _emit_ffn0(ctx, st, dram)
        elif ph == "attn1":
            _emit_attn1(ctx, st, dram)
        elif ph == "ffn1":
            _emit_ffn1(ctx, st, dram)
    if store_src == "dbg":
        _emit_store_dbg(ctx, st, dram)
    else:
        _emit_store(ctx, st, dram, src=store_src)

    with nc.Block() as block:
        @block.sync
        def _(e):
            for f in ctx.ops["sync"]:
                f(e)
        @block.tensor
        def _(e):
            for f in ctx.ops["tensor"]:
                f(e)
        @block.vector
        def _(e):
            for f in ctx.ops["vector"]:
                f(e)
        @block.scalar
        def _(e):
            for f in ctx.ops["scalar"]:
                f(e)
        @block.gpsimd
        def _(e):
            for f in ctx.ops["gpsimd"]:
                f(e)
    return nc



def _emit_attn0(ctx, st, dram):
    AR, ARF, pb = st.AR, st.ARF, st.pb
    PT = [AR[:, 512 * kc:512 * (kc + 1)] for kc in range(16)]
    y_t = [AR[:, 8192:10240], AR[:, 10240:12288], AR[:, 12288:14336]]
    kv_t = [AR[:, 14336:16384], AR[:, 16384:18432], AR[:, 18432:20480]]
    KVT = [AR[:, 20480 + 384 * kc:20480 + 384 * kc + 384] for kc in range(16)]
    q_t = [AR[:, 26624:28672], AR[:, 28672:30720], AR[:, 30720:32768]]
    latw = AR[:, 32768:35840]
    wdbuf = [AR[:, 35840:38912], AR[:, 38912:41984]]
    owbuf = [AR[:, 41984:45056], AR[:, 45056:48128]]
    recB = AR[:, 48128:48640]
    latbr_bf = AR[:, 8192:8576]
    Rt = ARF[:, 0:512]
    recF = ARF[:, 512:1024]

    c_latw = ctx.dma("gpsimd", latw[:, :], dram["latw"][:, :], inc="wdma")
    ctx.wait("vector", "xdma", st.after_prologue["xdma"])
    ctx.op("vector", lambda e: e.tensor_scalar_mul(st.wdb[:, :], st.wdb[:, :], 0.125), inc="vec")
    ctx.op("vector", lambda e: e.memset(y_t[2][32:64, :], 0.0))
    ctx.op("vector", lambda e: e.memset(y_t[2][64:128, :], 0.0), inc="vec")
    wdb8_done = ctx.val("vec")
    ctx.wait("vector", "wdma", 32)
    c_wd = [0] * 16
    c_ow = [0] * 16
    for h in (0, 1):
        c_wd[h] = ctx.dma("gpsimd", wdbuf[h][:, :], dram["wd"][h], inc="wdma")
        c_ow[h] = ctx.dma("gpsimd", owbuf[h][:, :], dram["outw"][h], inc="wdma")

    ctx.wait("tensor", "wdma", c_latw)
    ctx.wait("scalar", "xdma", st.after_prologue["xdma"])
    evac_c = 0
    for lc in range(3):
        for nt in range(NT):
            sl = slice(512 * nt, 512 * (nt + 1))
            bank = pb[(lc * NT + nt) % 2]
            if lc * NT + nt >= 2:
                ctx.wait("tensor", "act", evac_c - 1)
            for fc in range(NFC):
                lh = latw[:, fc * 384 + 128 * lc: fc * 384 + 128 * lc + 128]
                pe = _mm(ctx, bank[:, :], lh, st.xn[fc][:, sl],
                         start=(fc == 0), stop=(fc == NFC - 1),
                         inc="pe" if fc == NFC - 1 else None)
            ctx.wait("scalar", "pe", pe)
            evac_c = ctx.op("scalar", lambda e, o=kv_t[lc][:, sl], i=bank[:, :],
                            b=st.latb[:, lc:lc + 1]:
                            e.activation(o, i, AF.Identity, bias=b), inc="act")
    c_lbr = ctx.dma("gpsimd", latbr_bf[0:1, 0:384], dram["latbr"][:, :], inc="wdma")
    ctx.wait("tensor", "wdma", c_lbr)
    kve = 0
    for kc in range(16):
        ksl = slice(128 * kc, 128 * (kc + 1))
        bank = pb[2 + kc % 2]
        if kc >= 2:
            ctx.wait("tensor", "act", kve - 1)
        for fc in range(NFC):
            _mm(ctx, bank[:, 0:384], st.xn[fc][:, ksl], latw[:, fc * 384:fc * 384 + 384],
                start=(fc == 0), stop=False)
        pe = _mm(ctx, bank[:, 0:384], st.ones_bf[0:1, 0:128], latbr_bf[0:1, 0:384],
                 start=False, stop=True, inc="pe")
        ctx.wait("scalar", "pe", pe)
        kve = ctx.op("scalar", lambda e, kc=kc, bank=bank:
                     e.copy(KVT[kc][:, 0:384], bank[:, 0:384]), inc="act")
    ctx.wait("vector", "act", kve)
    for kc in range(16):
        ctx.op("vector", lambda e, kc=kc: e.memset(KVT[kc][:, 288:289], 1.0),
               inc="vec" if kc == 15 else None)

    hold = {'yev': ctx.val("vec"), 'tail': None, 'pe_last': 0, 'ytile': [0] * NT}
    for h in range(DEV_A0_HEADS):
        hb = h % 2
        ctx.wait("tensor", "wdma", c_wd[h])
        if h == 0:
            ctx.wait("scalar", "vec", wdb8_done)
        qe_c = 0
        for lc in range(3):
            for nt in range(NT):
                sl = slice(512 * nt, 512 * (nt + 1))
                bank = pb[(lc * NT + nt) % 2]
                ctx.wait("tensor", "vec", ctx.val("vec"))
                if lc * NT + nt >= 2:
                    ctx.wait("tensor", "act", qe_c - 1)
                for fc in range(NFC):
                    lh = wdbuf[hb][:, fc * 384 + 128 * lc: fc * 384 + 128 * lc + 128]
                    pe = _mm(ctx, bank[:, :], lh, st.xn[fc][:, sl],
                             start=(fc == 0), stop=(fc == NFC - 1),
                             inc="pe" if fc == NFC - 1 else None)
                ctx.wait("scalar", "pe", pe)
                qe_c = ctx.op("scalar", lambda e, o=q_t[lc][:, sl], i=bank[:, :],
                              b=st.wdb[:, 3 * h + lc:3 * h + lc + 1]:
                              e.activation(o, i, AF.Identity, bias=b, scale=0.125), inc="act")
        if h + 2 < 16:
            ctx.wait("gpsimd", "pe", ctx.val("pe"))
            c_wd[h + 2] = ctx.dma("gpsimd", wdbuf[hb][:, :], dram["wd"][h + 2], inc="wdma")
        ctx.wait("tensor", "act", qe_c)

        def gen_sty(qt, exp_prev):
            Q0 = 512 * qt
            Kc = Q0 // 128 + 4
            qsl = slice(Q0, Q0 + 512)
            c_exp = [0] * Kc
            c_fix = [0] * Kc

            def emit_st(kc):
                pbS = pb[3 + kc % 2]
                ctx.wait("tensor", "act", c_exp[kc - 2] if kc >= 2 else exp_prev)
                _mm(ctx, pbS[:, :], kv_t[0][:, 128 * kc:128 * kc + 128], q_t[0][:, qsl], True, False)
                _mm(ctx, pbS[:, :], kv_t[1][:, 128 * kc:128 * kc + 128], q_t[1][:, qsl], False, False)
                pe = _mm(ctx, pbS[:, :], kv_t[2][:, 128 * kc:128 * kc + 128], q_t[2][:, qsl],
                         False, True, inc="pe")
                ctx.wait("scalar", "pe", pe)
                c_exp[kc] = ctx.op("scalar", lambda e, kc=kc, pbS=pbS:
                                   e.activation(PT[kc][:, 0:512], pbS[:, :], AF.Exp), inc="act")
                kr = kc - Q0 // 128
                if kr >= 0:
                    ctx.wait("vector", "act", c_exp[kc])
                    for jj in range(kr):
                        ctx.op("vector", lambda e, kc=kc, jj=jj:
                               e.memset(PT[kc][:, 128 * jj:128 * jj + 128], 0.0))
                    c_fix[kc] = ctx.op("vector", lambda e, kc=kc, kr=kr:
                                       e.tensor_mul(PT[kc][:, 128 * kr:128 * kr + 128],
                                                    PT[kc][:, 128 * kr:128 * kr + 128],
                                                    st.tri[:, :]), inc="vec")

            def emit_y(kc):
                ctx.wait("tensor", "vec" if c_fix[kc] else "act",
                         c_fix[kc] if c_fix[kc] else c_exp[kc])
                if kc == 0:
                    ctx.wait("tensor", "vec", hold['yev'])
                st_ = (kc == 0)
                sp_ = (kc == Kc - 1)
                _mm(ctx, pb[5][:, :], KVT[kc][:, 0:128], PT[kc][:, 0:512], st_, sp_)
                _mm(ctx, pb[6][:, :], KVT[kc][:, 128:256], PT[kc][:, 0:512], st_, sp_)
                return _mm(ctx, pb[7][:, :], KVT[kc][:, 256:384], PT[kc][:, 0:512], st_, sp_,
                           inc="pe" if sp_ else None)

            for kc in range(Kc):
                emit_st(kc)
                yield
            for kc in range(Kc - 1):
                emit_y(kc)
                yield
            hold['pe_last'] = emit_y(Kc - 1)

        def gen_tail(qt):
            qsl = slice(512 * qt, 512 * qt + 512)
            ctx.wait("vector", "pe", hold['pe_last'])
            vc = ctx.op("vector", lambda e: e.tensor_copy(recB[32:33, :], pb[7][32:33, :]), inc="vec")
            yield
            ctx.wait("tensor", "vec", vc)
            pe = _mm(ctx, pb[2][:, :], st.ones_bf[32:33, 0:128], recB[32:33, :], True, True, inc="pe")
            ctx.wait("vector", "pe", pe)
            ctx.op("vector", lambda e: e.reciprocal(Rt[:, :], pb[2][:, :]))
            ctx.op("vector", lambda e, qsl=qsl: e.tensor_mul(y_t[0][:, qsl], pb[5][:, :], Rt[:, :]))
            ctx.op("vector", lambda e, qsl=qsl: e.tensor_mul(y_t[1][:, qsl], pb[6][:, :], Rt[:, :]))
            hold['yev'] = ctx.op("vector", lambda e, qsl=qsl:
                                 e.tensor_mul(y_t[2][0:32, qsl], pb[7][0:32, :], Rt[0:32, :]),
                                 inc="vec")
            hold['ytile'][qt] = hold['yev']
            yield

        for qt in range(NT):
            exp_prev = ctx.val("act")
            g_sty = gen_sty(qt, exp_prev)
            steps = 0
            for _ in g_sty:
                steps += 1
                if steps == 1 and hold['tail'] is not None:
                    next(hold['tail'])
                elif steps == 3 and hold['tail'] is not None:
                    for _ in hold['tail']:
                        pass
                    hold['tail'] = None
            hold['tail'] = gen_tail(qt)
        memset_c = 0
        if h == 0:
            memset_c = ctx.op("vector", lambda e: e.memset(y_t[2][32:33, :], 1.0), inc="vec")
        ctx.wait("tensor", "wdma", c_ow[h])
        if memset_c:
            ctx.wait("tensor", "vec", memset_c)
        for _ in hold['tail']:
            pass
        hold['tail'] = None
        ev = {}
        for idx, (nt, cc) in enumerate([(nt_, cc_) for nt_ in range(NT) for cc_ in range(NFC)]):
            sl = slice(512 * nt, 512 * (nt + 1))
            bank = pb[idx % 2]
            if cc == 0:
                ctx.wait("tensor", "vec", hold['ytile'][nt])
            if idx >= 2:
                ctx.wait("tensor", "vec", ev[idx - 2])
            _mm(ctx, bank[:, :], owbuf[hb][:, cc * 128:cc * 128 + 128], y_t[0][:, sl], True, False)
            _mm(ctx, bank[:, :], owbuf[hb][:, 1024 + cc * 128:1024 + cc * 128 + 128],
                y_t[1][:, sl], False, False)
            pe = _mm(ctx, bank[:, :], owbuf[hb][:, 2048 + cc * 128:2048 + cc * 128 + 128],
                     y_t[2][:, sl], False, True, inc="pe")
            ctx.wait("vector", "pe", pe)
            ev[idx] = ctx.op("vector", lambda e, cc=cc, sl=sl, bank=bank:
                             e.tensor_add(st.x[cc][:, sl], bank[:, :], st.x[cc][:, sl]), inc="vec")
        if h + 2 < 16:
            ctx.wait("gpsimd", "pe", ctx.val("pe"))
            c_ow[h + 2] = ctx.dma("gpsimd", owbuf[hb][:, :], dram["outw"][h + 2], inc="wdma")
    ctx.wait("tensor", "vec", ctx.val("vec"))


def _emit_ffn0(ctx, st, dram):
    AR, ARF, pb = st.AR, st.ARF, st.pb
    ACT = [AR[:, 1024 * hc:1024 * (hc + 1)] for hc in range(22)]
    w1buf = [AR[:, 22528:23552], AR[:, 23552:24576]]
    w3buf = [AR[:, 24576:25600], AR[:, 25600:26624]]
    w2buf = [AR[:, 26624:29440], AR[:, 29440:32256]]
    sg = [AR[:, 32256:32768], AR[:, 32768:33280]]
    cwbf = AR[:, 33280:33424]
    g0bf = AR[:, 33424:33496]
    g0b_bf = AR[:, 33496:33512]
    cwrowB = AR[:, 33512:34024]
    CWBs = ARF[:, 0:1024]
    cwrowF = ARF[:, 1024:1536]
    dtmp = ARF[:, 1536:2048]
    sgF = ARF[:, 2048:2064]
    gm1 = ARF[:, 2064:2065]; gm2 = ARF[:, 2065:2066]
    gden = ARF[:, 2066:2067]
    glt = ARF[:, 2067:2076]; gt2 = ARF[:, 2076:2085]; gge = ARF[:, 2085:2094]
    gcw = ARF[:, 2094:2103]

    ctx.wait("gpsimd", "pe", ctx.val("pe"))
    ctx.wait("gpsimd", "vec", ctx.val("vec"))
    c_g0 = ctx.dma("gpsimd", g0bf[:, :], dram["g0w"][:, :], inc="wdma")
    c_g0b = ctx.dma("gpsimd", g0b_bf[0:1, 0:9], dram["g0b"][:, :], inc="wdma")

    ctx.wait("tensor", "wdma", c_g0b)
    ctx.wait("tensor", "vec", ctx.val("vec"))
    sig_c = 0
    cw_c = 0
    for j in range(16):
        tsl = slice(128 * j, 128 * (j + 1))
        if j > 0:
            ctx.wait("tensor", "act", sig_c)
        for fc in range(NFC):
            _mm(ctx, pb[6][:, 0:9], st.xn[fc][:, tsl], g0bf[:, 9 * fc:9 * fc + 9],
                start=(fc == 0), stop=False)
        pe = _mm(ctx, pb[6][:, 0:9], st.ones_bf[0:1, 0:128], g0b_bf[0:1, 0:9],
                 start=False, stop=True, inc="pe")
        ctx.wait("scalar", "pe", pe)
        if j > 0:
            ctx.wait("scalar", "vec", cw_c)
        sig_c = ctx.op("scalar", lambda e: e.activation(sgF[:, 0:9], pb[6][:, 0:9], AF.Sigmoid),
                       inc="act")
        ctx.wait("vector", "act", sig_c)

        def vf(builder):
            c = ctx.op("vector", builder, inc="vec")
            ctx.wait("vector", "vec", c)
            return c
        vf(lambda e: e.reduce_max(gm1[:, :], sgF[:, 0:9], axis=AX.X))
        vf(lambda e: e.tensor_scalar(glt[:, :], sgF[:, 0:9], gm1[:, 0:1], None, ALU.is_lt))
        vf(lambda e: e.tensor_mul(gt2[:, :], sgF[:, 0:9], glt[:, :]))
        vf(lambda e: e.reduce_max(gm2[:, :], gt2[:, :], axis=AX.X))
        vf(lambda e: e.tensor_add(gden[:, :], gm1[:, :], gm2[:, :]))
        vf(lambda e: e.tensor_scalar(gden[:, :], gden[:, :], 1e-20, None, ALU.add))
        vf(lambda e: e.reciprocal(gden[:, :], gden[:, :]))
        vf(lambda e: e.tensor_scalar(gge[:, :], sgF[:, 0:9], gm2[:, 0:1], None, ALU.is_ge))
        vf(lambda e: e.scalar_tensor_tensor(gcw[:, :], sgF[:, 0:9], gden[:, 0:1],
                                            gge[:, :], ALU.mult, ALU.mult))
        cw_c = ctx.op("vector", lambda e, j=j: e.tensor_copy(cwbf[:, 9 * j:9 * j + 9], gcw[:, :]),
                      inc="vec")

    c_w1 = {}; c_w3 = {}; c_w2 = {}
    w1_war = [0, 0]; w3_war = [0, 0]; w2_war = [0, 0]
    cwb_war = 0
    for ttp in range(2):
        for e in range(9):
            for hc0 in (0, 1):
                ctx.wait("gpsimd", "pe", w1_war[hc0])
                c_w1[hc0] = ctx.dma("gpsimd", w1buf[hc0][:, :], dram["w1s"][e, hc0], inc="wdma")
                ctx.wait("gpsimd", "pe", w3_war[hc0])
                c_w3[hc0] = ctx.dma("gpsimd", w3buf[hc0][:, :], dram["w3s"][e, hc0], inc="wdma")
            ctx.wait("tensor", "vec", cw_c if (ttp == 0 and e == 0) else ctx.val("vec"))
            pe = 0
            for half in range(2):
                for jj in range(4):
                    j = ttp * 8 + half * 4 + jj
                    pe = _mm(ctx, pb[6][32 * half:32 * half + 1, 128 * jj:128 * jj + 128],
                             cwbf[:, 9 * j + e:9 * j + e + 1], st.ident[:, 0:128],
                             start=True, stop=True,
                             inc="pe" if (half == 1 and jj == 3) else None)
            ctx.wait("vector", "pe", pe)
            rows_c = [0, 0]
            for half in range(2):
                rows_c[half] = ctx.op("vector", lambda e_, half=half:
                                      e_.tensor_copy(cwrowB[32 * half:32 * half + 1, 0:512],
                                                     pb[6][32 * half:32 * half + 1, 0:512]),
                                      inc="vec")
            cpy = [0, 0]
            for half in range(2):
                bank = pb[6 + half]
                ctx.wait("tensor", "vec", rows_c[1] if half == 0 else rows_c[half])
                pe = _mm(ctx, bank[:, :], st.ones_bf[32 * half:32 * half + 1, 0:128],
                         cwrowB[32 * half:32 * half + 1, 0:512], True, True, inc="pe")
                ctx.wait("vector", "pe", pe)
                cpy[half] = ctx.op("vector", lambda e_, half=half, bank=bank:
                                   e_.tensor_copy(CWBs[:, 512 * half:512 * half + 512],
                                                  bank[:, :]), inc="vec")
            silu_c = {}
            act_c = {}
            for hc in range(22):
                par = hc % 2
                ctx.wait("tensor", "wdma", c_w3[hc])
                for half in range(2):
                    i2 = hc * 2 + half
                    spar = i2 % 2
                    tsl = slice(1024 * ttp + 512 * half, 1024 * ttp + 512 * half + 512)
                    bg = pb[i2 % 2]
                    bu = pb[2 + i2 % 2]
                    if i2 >= 2:
                        ctx.wait("tensor", "act", silu_c[i2 - 2])
                        ctx.wait("tensor", "vec", act_c[i2 - 2])
                    peg = peu = None
                    for fc in range(NFC):
                        peg = _mm(ctx, bg[:, :], w1buf[par][:, 128 * fc:128 * fc + 128],
                                  st.xn[fc][:, tsl], start=(fc == 0), stop=(fc == NFC - 1),
                                  inc="pe" if fc == NFC - 1 else None)
                    for fc in range(NFC):
                        peu = _mm(ctx, bu[:, :], w3buf[par][:, 128 * fc:128 * fc + 128],
                                  st.xn[fc][:, tsl], start=(fc == 0), stop=(fc == NFC - 1),
                                  inc="pe" if fc == NFC - 1 else None)
                    ctx.wait("scalar", "pe", peg)
                    if i2 >= 2:
                        ctx.wait("scalar", "vec", act_c[i2 - 2])
                    silu_c[i2] = ctx.op(
                        "scalar", lambda e_, spar=spar, bg=bg:
                        e_.activation(sg[spar][:, :], bg[:, :], AF.Silu), inc="act")
                    ctx.wait("vector", "pe", peu)
                    ctx.wait("vector", "act", silu_c[i2])
                    act_c[i2] = ctx.op(
                        "vector", lambda e_, hc=hc, half=half, spar=spar, bu=bu:
                        e_.tensor_mul(ACT[hc][:, 512 * half:512 * half + 512], bu[:, :],
                                      sg[spar][:, :]), inc="vec")
                w1_war[par] = ctx.val("pe")
                w3_war[par] = ctx.val("pe")
                if hc + 2 < 22:
                    ctx.wait("gpsimd", "pe", ctx.val("pe"))
                    c_w1[hc + 2] = ctx.dma("gpsimd", w1buf[par][:, :], dram["w1s"][e, hc + 2],
                                           inc="wdma")
                    c_w3[hc + 2] = ctx.dma("gpsimd", w3buf[par][:, :], dram["w3s"][e, hc + 2],
                                           inc="wdma")
            for dc0 in (0, 1):
                ctx.wait("gpsimd", "pe", w2_war[dc0])
                c_w2[dc0] = ctx.dma("gpsimd", w2buf[dc0][:, :], dram["w2s"][e, dc0], inc="wdma")
            ev = {}
            ctx.wait("tensor", "vec", act_c[43])
            for dc in range(8):
                par = dc % 2
                ctx.wait("tensor", "wdma", c_w2[dc])
                for half in range(2):
                    tsl = slice(1024 * ttp + 512 * half, 1024 * ttp + 512 * half + 512)
                    bank = pb[4 + (dc * 2 + half) % 2]
                    if dc * 2 + half >= 2:
                        ctx.wait("tensor", "vec", ev[dc * 2 + half - 2])
                    for hc in range(22):
                        pe = _mm(ctx, bank[:, :], w2buf[par][:, 128 * hc:128 * hc + 128],
                                 ACT[hc][:, 512 * half:512 * half + 512],
                                 start=(hc == 0), stop=(hc == 21),
                                 inc="pe" if hc == 21 else None)
                    ctx.wait("vector", "pe", pe)
                    ctx.op("vector", lambda e_, bank=bank, half=half:
                           e_.tensor_mul(dtmp[:, :], bank[:, :], CWBs[:, 512 * half:512 * half + 512]))
                    ev[dc * 2 + half] = ctx.op(
                        "vector", lambda e_, dc=dc, tsl=tsl:
                        e_.tensor_add(st.x[dc][:, tsl], dtmp[:, :], st.x[dc][:, tsl]), inc="vec")
                w2_war[par] = ctx.val("pe")
                if dc + 2 < 8:
                    ctx.wait("gpsimd", "pe", ctx.val("pe"))
                    c_w2[dc + 2] = ctx.dma("gpsimd", w2buf[par][:, :], dram["w2s"][e, dc + 2],
                                           inc="wdma")
            ctx.wait("tensor", "vec", ctx.val("vec"))
    ctx.wait("tensor", "vec", ctx.val("vec"))


def _emit_attn1(ctx, st, dram):
    AR, ARF, pb = st.AR, st.ARF, st.pb
    PT = [AR[:, 512 * kc:512 * (kc + 1)] for kc in range(16)]
    Z1 = [AR[:, 8192 + 2048 * c:8192 + 2048 * (c + 1)] for c in range(8)]
    zk = [AR[:, 24576:26624], AR[:, 26624:28672]]
    zv1 = AR[:, 28672:30720]
    qraw = AR[0:64, 30720:32768]
    qfull = AR[:, 30720:32768]
    zv2 = AR[64:128, 30720:32768]
    kraw = AR[0:64, 32768:34816]
    kfull = AR[:, 32768:34816]
    qp = AR[:, 34816:36864]
    kp = AR[:, 36864:38912]
    Tc = AR[:, 38912:40960]
    Ts = AR[:, 40960:43008]
    wqb = [AR[:, 44032:44544], AR[:, 44544:45056]]
    vdb = [AR[:, 45056:45184], AR[:, 45184:45312]]
    bdr = AR[:, 45312:45824]; bdi = AR[:, 45824:46336]; nbdi = AR[:, 46336:46848]
    wpb = AR[:, 46848:47872]
    permb = AR[:, 47872:48000]
    recB1 = AR[:, 48000:48512]
    Rt = ARF[:, 0:512]
    recF = ARF[:, 512:1024]
    tmpBf = ARF[:, 1536:2048]

    ctx.wait("gpsimd", "pe", ctx.val("pe"))
    ctx.wait("gpsimd", "vec", ctx.val("vec"))
    c_tc0 = ctx.dma("gpsimd", Tc[0:64, :], dram["tcq"][:, :], inc="wdma")
    c_ts0 = ctx.dma("gpsimd", Ts[0:64, :], dram["tsq"][:, :], inc="wdma")
    c_bdr = ctx.dma("gpsimd", bdr[:, :], dram["bdr"][:, :], inc="wdma")
    c_bdi = ctx.dma("gpsimd", bdi[:, :], dram["bdi"][:, :], inc="wdma")
    wkb = AR[:, 0:2048]
    wvb = AR[:, 2048:3584]
    c_wk = ctx.dma("gpsimd", wkb[:, :], dram["wk"][:, :], inc="wdma")
    c_wv = ctx.dma("gpsimd", wvb[:, :], dram["wv"][:, :], inc="wdma")
    c_pm = ctx.dma("gpsimd", permb[:, :], dram["c_perm"][:, :], inc="wdma")
    ctx.wait("vector", "wdma", c_bdi)
    nb_c = ctx.op("vector", lambda e: e.tensor_scalar_mul(nbdi[:, :], bdi[:, :], -1.0), inc="vec")
    ctx.op("vector", lambda e: e.memset(qp[64:128, :], 0.0))
    ctx.op("vector", lambda e: e.memset(kp[64:128, :], 0.0))
    ctx.op("vector", lambda e: e.memset(kfull[64:128, :], 0.0))
    ctx.op("vector", lambda e: e.memset(st.vtok3[:, :, 64:65], 1.0), inc="vec")

    ctx.wait("tensor", "wdma", c_wv)
    ctx.wait("tensor", "vec", ctx.val("vec"))
    ev = 0
    outspec = [(zk[0], wkb, 0, 0, 128), (zk[1], wkb, 128, 0, 128),
               (zv1, wvb, 0, 0, 128), (zv2, wvb, 128, 64, 64)]
    for ci, (dst, wsrc, coff, roff, rows) in enumerate(outspec):
        for nt in range(NT):
            sl = slice(512 * nt, 512 * (nt + 1))
            bank = pb[(ci * NT + nt) % 2]
            if ci * NT + nt >= 2:
                ctx.wait("tensor", "act", ev - 1)
            for fc in range(NFC):
                stride = 256 if wsrc is wkb else 192
                lh = wsrc[:, stride * fc + coff:stride * fc + coff + rows]
                pe = _mm(ctx, bank[roff:roff + rows, :], lh, st.xn[fc][:, sl],
                         start=(fc == 0), stop=(fc == NFC - 1),
                         inc="pe" if fc == NFC - 1 else None)
            ctx.wait("scalar", "pe", pe)
            ev = ctx.op("scalar", lambda e, dst=dst, sl=sl, bank=bank, roff=roff, rows=rows:
                        e.copy(dst[roff:roff + rows, sl] if roff == 0 else dst[:, sl],
                               bank[roff:roff + rows, :]), inc="act")
    ctx.wait("tensor", "act", ev)

    c_wq = {}
    c_vd = {}
    ctx.wait("gpsimd", "pe", ctx.val("pe"))
    c_wq[0] = ctx.dma("gpsimd", wqb[0][:, :], dram["wq"][0], inc="wdma")
    c_vd[0] = ctx.dma("gpsimd", vdb[0][:, :], dram["vd"][0], inc="wdma")
    hold = {'yevb': [ctx.val("vec")] * 2, 'tail': None, 'pe_last': 0, 'vev': 0}
    for h in range(16):
        hpar = h % 2
        zc = h // 2
        par = h % 2
        ctx.wait("tensor", "wdma", c_wq[h])
        if hold['vev']:
            ctx.wait("tensor", "act", hold['vev'])
        qel = [0] * NT
        for nt in range(NT):
            sl = slice(512 * nt, 512 * (nt + 1))
            bank = pb[nt % 2]
            if nt >= 2:
                ctx.wait("tensor", "act", qel[nt - 2])
            for fc in range(NFC):
                pe = _mm(ctx, bank[0:64, :], wqb[par][:, 64 * fc:64 * fc + 64], st.xn[fc][:, sl],
                         start=(fc == 0), stop=(fc == NFC - 1),
                         inc="pe" if fc == NFC - 1 else None)
            ctx.wait("scalar", "pe", pe)
            qel[nt] = ctx.op("scalar", lambda e, sl=sl, bank=bank:
                             e.copy(qraw[:, sl], bank[0:64, :]), inc="act")
        ctx.wait("tensor", "act", qel[NT - 1])
        ctx.wait("tensor", "vec", ctx.val("vec"))
        if h == 0:
            ctx.wait("tensor", "wdma", c_pm)
        qperm = [0] * NT
        for nt in range(NT):
            sl = slice(512 * nt, 512 * (nt + 1))
            qperm[nt] = _mm(ctx, pb[2 + nt][:, :], permb[:, :], qfull[:, sl],
                            True, True, inc="pe")
        rd = [0] * NT
        for nt in range(NT):
            sl = slice(512 * nt, 512 * (nt + 1))
            ctx.wait("vector", "pe", qperm[nt])
            ctx.op("vector", lambda e, sl=sl: e.tensor_mul(qp[0:64, sl], qraw[:, sl], Tc[0:64, sl]))
            ctx.op("vector", lambda e, sl=sl, bank=pb[2 + nt]:
                   e.tensor_mul(tmpBf[0:64, :], bank[0:64, :], Ts[0:64, sl]))
            rd[nt] = ctx.op("vector", lambda e, sl=sl:
                            e.tensor_add(qp[0:64, sl], qp[0:64, sl], tmpBf[0:64, :]), inc="vec")
        if h == 0:
            ctx.wait("tensor", "wdma", c_bdi)
            ctx.wait("tensor", "vec", nb_c)
        kel = [0] * NT
        for nt in range(NT):
            sl = slice(512 * nt, 512 * (nt + 1))
            bank = pb[nt % 2]
            if nt >= 2:
                ctx.wait("tensor", "act", kel[nt - 2])
            _mm(ctx, bank[0:32, :], bdr[:, 32 * h:32 * h + 32], zk[0][:, sl], True, False)
            _mm(ctx, bank[0:32, :], nbdi[:, 32 * h:32 * h + 32], zk[1][:, sl], False, True)
            _mm(ctx, bank[32:64, :], bdi[:, 32 * h:32 * h + 32], zk[0][:, sl], True, False)
            pe = _mm(ctx, bank[32:64, :], bdr[:, 32 * h:32 * h + 32], zk[1][:, sl], False, True,
                     inc="pe")
            ctx.wait("scalar", "pe", pe)
            kel[nt] = ctx.op("scalar", lambda e, sl=sl, bank=bank:
                             e.copy(kraw[:, sl], bank[0:64, :]), inc="act")
        ctx.wait("tensor", "wdma", c_vd[h])
        ctx.wait("tensor", "act", kel[NT - 1])
        v_ev = [0, 0]
        for halfv in range(2):
            bank = pb[halfv]
            for tci in range(8):
                tc = halfv * 8 + tci
                _mm(ctx, bank[:, 64 * tci:64 * tci + 64], zv1[:, 128 * tc:128 * tc + 128],
                    vdb[par][:, 0:64], start=True, stop=False)
                pe = _mm(ctx, bank[:, 64 * tci:64 * tci + 64], zv2[:, 128 * tc:128 * tc + 128],
                         vdb[par][64:128, 64:128], start=False, stop=True,
                         inc="pe" if tci == 7 else None)
            ctx.wait("scalar", "pe", pe)
            v_ev[halfv] = ctx.op("scalar", lambda e, halfv=halfv, bank=bank:
                                 e.copy(st.vtok3[:, 8 * halfv:8 * halfv + 8, 0:64], bank[:, :]),
                                 inc="act")
        hold['vev'] = v_ev[1]
        if h + 1 < 16:
            ctx.wait("gpsimd", "pe", ctx.val("pe"))
            c_wq[h + 1] = ctx.dma("gpsimd", wqb[1 - par][:, :], dram["wq"][h + 1], inc="wdma")
            c_vd[h + 1] = ctx.dma("gpsimd", vdb[1 - par][:, :], dram["vd"][h + 1], inc="wdma")
        kperm = [0] * NT
        for nt in range(NT):
            sl = slice(512 * nt, 512 * (nt + 1))
            ctx.wait("tensor", "vec", rd[nt])
            kperm[nt] = _mm(ctx, pb[2 + nt][:, :], permb[:, :], kfull[:, sl],
                            True, True, inc="pe")
        rdk = [0] * NT
        for nt in range(NT):
            sl = slice(512 * nt, 512 * (nt + 1))
            ctx.wait("vector", "pe", kperm[nt])
            ctx.op("vector", lambda e, sl=sl: e.scalar_tensor_tensor(
                kp[0:64, sl], kraw[:, sl], 8.0, Tc[0:64, sl], ALU.mult, ALU.mult))
            ctx.op("vector", lambda e, sl=sl, bank=pb[2 + nt]: e.scalar_tensor_tensor(
                tmpBf[0:64, :], bank[0:64, :], 8.0, Ts[0:64, sl], ALU.mult, ALU.mult))
            rdk[nt] = ctx.op("vector", lambda e, sl=sl:
                             e.tensor_add(kp[0:64, sl], kp[0:64, sl], tmpBf[0:64, :]), inc="vec")

        def gen_sty1(qt, exp_prev):
            Q0 = 512 * qt
            Kc = Q0 // 128 + 4
            qsl = slice(Q0, Q0 + 512)
            c_exp = [0] * Kc
            c_fix = [0] * Kc
            yb = pb[5 + qt % 2]

            def emit_st1(kc):
                pbS = (pb[3], pb[4], pb[7])[kc % 3]
                ctx.wait("tensor", "act", c_exp[kc - 3] if kc >= 3 else exp_prev)
                if kc == 0:
                    ctx.wait("tensor", "vec", rdk[qt])
                if qt == 0 and kc < 2:
                    ctx.wait("tensor", "vec", rdk[kc + 1])
                pe = _mm(ctx, pbS[:, :], kp[:, 128 * kc:128 * kc + 128], qp[:, qsl],
                         True, True, inc="pe")
                ctx.wait("scalar", "pe", pe)
                c_exp[kc] = ctx.op("scalar", lambda e, kc=kc, pbS=pbS:
                                   e.activation(PT[kc][:, 0:512], pbS[:, :], AF.Exp), inc="act")
                kr = kc - Q0 // 128
                if kr >= 0:
                    ctx.wait("vector", "act", c_exp[kc])
                    for jj in range(kr):
                        ctx.op("vector", lambda e, kc=kc, jj=jj:
                               e.memset(PT[kc][:, 128 * jj:128 * jj + 128], 0.0))
                    c_fix[kc] = ctx.op("vector", lambda e, kc=kc, kr=kr:
                                       e.tensor_mul(PT[kc][:, 128 * kr:128 * kr + 128],
                                                    PT[kc][:, 128 * kr:128 * kr + 128],
                                                    st.tri[:, :]), inc="vec")

            def emit_y1(kc):
                ctx.wait("tensor", "vec" if c_fix[kc] else "act",
                         c_fix[kc] if c_fix[kc] else c_exp[kc])
                if kc == 0:
                    ctx.wait("tensor", "vec", hold['yevb'][qt % 2])
                    if qt == 0:
                        ctx.wait("tensor", "vec", rdk[3])
                st_ = (kc == 0)
                sp_ = (kc == Kc - 1)
                return _mm(ctx, yb[0:65, :], st.vtok3[:, kc, :], PT[kc][:, 0:512],
                           st_, sp_, inc="pe" if sp_ else None)

            for kc in range(Kc):
                emit_st1(kc)
                yield
            for kc in range(Kc - 1):
                emit_y1(kc)
                yield
            hold['pe_last'] = emit_y1(Kc - 1)

        def gen_tail1(qt, hpar, zc, zoff):
            qsl = slice(512 * qt, 512 * qt + 512)
            yb = pb[5 + qt % 2]
            ctx.wait("vector", "pe", hold['pe_last'])
            vc = ctx.op("vector", lambda e, yb=yb:
                        e.tensor_copy(recB1[64:65, :], yb[64:65, :]), inc="vec")
            yield
            ctx.wait("tensor", "vec", vc)
            pe = _mm(ctx, pb[2][:, :], st.ones_bf[64:65, 0:128], recB1[64:65, :],
                     True, True, inc="pe")
            ctx.wait("vector", "pe", pe)
            ctx.op("vector", lambda e: e.reciprocal(Rt[0:64, :], pb[2][0:64, :]))
            hold['yevb'][qt % 2] = ctx.op(
                "vector", lambda e, yb=yb, zc=zc, zoff=zoff, qsl=qsl:
                e.tensor_mul(Z1[zc][zoff:zoff + 64, qsl], yb[0:64, :],
                             Rt[0:64, :]), inc="vec")
            yield

        for qt in range(NT):
            exp_prev = ctx.val("act")
            g_sty = gen_sty1(qt, exp_prev)
            steps = 0
            for _ in g_sty:
                steps += 1
                if steps == 1 and hold['tail'] is not None:
                    next(hold['tail'])
                elif steps == 3 and hold['tail'] is not None:
                    for _ in hold['tail']:
                        pass
                    hold['tail'] = None
            hold['tail'] = gen_tail1(qt, hpar, zc, 64 * hpar)
        for _ in hold['tail']:
            pass
        hold['tail'] = None

    ctx.wait("tensor", "vec", ctx.val("vec"))
    ctx.wait("tensor", "act", ctx.val("act"))
    wpbs = [AR[:, 0:1024], AR[:, 1024:2048]]
    c_wp = [0] * NFC
    ctx.wait("gpsimd", "pe", ctx.val("pe"))
    ctx.wait("gpsimd", "act", ctx.val("act"))
    c_wp[0] = ctx.dma("gpsimd", wpbs[0][:, :], dram["wproj"][0], inc="wdma")
    c_wp[1] = ctx.dma("gpsimd", wpbs[1][:, :], dram["wproj"][1], inc="wdma")
    wp_war = [0, 0]
    for cc in range(NFC):
        par = cc % 2
        ctx.wait("tensor", "wdma", c_wp[cc])
        ev = 0
        for nt in range(NT):
            sl = slice(512 * nt, 512 * (nt + 1))
            bank = pb[4 + nt % 2]
            if nt >= 2:
                ctx.wait("tensor", "vec", ev - 1)
            for dc in range(NFC):
                pe = _mm(ctx, bank[:, :], wpbs[par][:, 128 * dc:128 * dc + 128], Z1[dc][:, sl],
                         start=(dc == 0), stop=(dc == NFC - 1),
                         inc="pe" if dc == NFC - 1 else None)
            ctx.wait("vector", "pe", pe)
            ev = ctx.op("vector", lambda e, cc=cc, sl=sl, bank=bank:
                        e.tensor_add(st.x[cc][:, sl], bank[:, :], st.x[cc][:, sl]), inc="vec")
        wp_war[par] = ctx.val("pe")
        if cc + 2 < NFC:
            ctx.wait("gpsimd", "pe", wp_war[par])
            c_wp[cc + 2] = ctx.dma("gpsimd", wpbs[par][:, :], dram["wproj"][cc + 2], inc="wdma")
        ctx.wait("tensor", "vec", ev)
    ctx.wait("tensor", "vec", ctx.val("vec"))


def _emit_ffn1(ctx, st, dram):
    AR, ARF, pb = st.AR, st.ARF, st.pb
    ACT1 = [[AR[:, 8192 * ei + 2048 * ic:8192 * ei + 2048 * (ic + 1)] for ic in range(4)]
            for ei in range(2)]
    wgbuf = [AR[:, 16384:20480], AR[:, 20480:24576]]
    wubuf = [AR[:, 24576:28672], AR[:, 28672:32768]]
    wdbuf = [AR[:, 32768:36864], AR[:, 36864:40960]]
    uprime = AR[:, 40960:41472]
    cwbf1 = AR[:, 41472:41728]
    g1bf = AR[:, 41728:41856]
    sg = [AR[:, 41856:42368], AR[:, 42368:42880]]
    cwrowB1 = AR[:, 42880:43392]
    sact = [AR[:, 2048 * ic:2048 * (ic + 1)] for ic in range(8)]
    wsgb = [AR[:, 16384:17408], AR[:, 17408:18432]]
    wsub = [AR[:, 18432:19456], AR[:, 19456:20480]]
    wsdb = [AR[:, 20480:21504], AR[:, 21504:22528]]
    CWBe = ARF[:, 0:2048]
    cwrowF = ARF[:, 2048:2560]
    pexp = ARF[:, 2560:2576]
    nmF = ARF[:, 2576:2577]; m1F = ARF[:, 2577:2578]; m2F = ARF[:, 2578:2579]
    Ssum = ARF[:, 2579:2580]; rec1 = ARF[:, 2580:2581]
    ilt = ARF[:, 2581:2597]; it2 = ARF[:, 2597:2613]; ige = ARF[:, 2613:2629]
    cwF = ARF[:, 2629:2645]

    ctx.wait("gpsimd", "pe", ctx.val("pe"))
    ctx.wait("gpsimd", "vec", ctx.val("vec"))
    c_g1 = ctx.dma("gpsimd", g1bf[:, :], dram["g1w"][:, :], inc="wdma")

    ctx.wait("tensor", "wdma", c_g1)
    ctx.wait("tensor", "vec", ctx.val("vec"))
    sig_c = 0
    vread_c = 0
    cw_c = 0
    for j in range(16):
        tsl = slice(128 * j, 128 * (j + 1))
        if j > 0:
            ctx.wait("tensor", "act", sig_c)
            ctx.wait("tensor", "vec", vread_c)
        for fc in range(NFC):
            pe = _mm(ctx, pb[6][:, 0:16], st.xn[fc][:, tsl], g1bf[:, 16 * fc:16 * fc + 16],
                     start=(fc == 0), stop=(fc == NFC - 1),
                     inc="pe" if fc == NFC - 1 else None)
        ctx.wait("vector", "pe", pe)
        if j > 0:
            ctx.wait("vector", "act", sig_c)

        def vf(builder):
            c = ctx.op("vector", builder, inc="vec")
            ctx.wait("vector", "vec", c)
            return c
        vf(lambda e: e.reduce_max(nmF[:, :], pb[6][:, 0:16], axis=AX.X, negate=True))
        vf(lambda e: e.reduce_max(m1F[:, :], pb[6][:, 0:16], axis=AX.X))
        vc = ctx.val("vec")
        ctx.wait("scalar", "vec", vc)
        sig_c = ctx.op("scalar", lambda e:
                       e.activation(pexp[:, :], pb[6][:, 0:16], AF.Exp, bias=nmF[:, 0:1],
                                    accum_out=Ssum[:, 0:1]), inc="act")
        ctx.wait("vector", "act", sig_c)
        vf(lambda e: e.reciprocal(rec1[:, :], Ssum[:, :]))
        vf(lambda e: e.tensor_scalar(ilt[:, :], pb[6][:, 0:16], m1F[:, 0:1], None, ALU.is_lt))
        vf(lambda e: e.tensor_mul(it2[:, :], pb[6][:, 0:16], ilt[:, :]))
        vf(lambda e: e.reduce_max(m2F[:, :], it2[:, :], axis=AX.X))
        vf(lambda e: e.tensor_scalar(ige[:, :], pb[6][:, 0:16], m2F[:, 0:1], None, ALU.is_ge))
        vf(lambda e: e.scalar_tensor_tensor(cwF[:, :], pexp[:, :], rec1[:, 0:1], ige[:, :],
                                            ALU.mult, ALU.mult))
        vread_c = cw_c = ctx.op("vector", lambda e, j=j:
                                e.tensor_copy(cwbf1[:, 16 * j:16 * j + 16], cwF[:, :]), inc="vec")

    wg_war = [0, 0]; wu_war = [0, 0]; wd_war = [0, 0]
    c_wg = [0, 0]; c_wu = [0, 0]; c_wd = [0, 0]
    for ep in range(8):
        acts_done = 0
        for ei in range(2):
            e_ = 2 * ep + ei
            par = ei
            ctx.wait("gpsimd", "pe", wg_war[par])
            c_wg[par] = ctx.dma("gpsimd", wgbuf[par][:, :], dram["wgs"][e_], inc="wdma")
            ctx.wait("gpsimd", "pe", wu_war[par])
            c_wu[par] = ctx.dma("gpsimd", wubuf[par][:, :], dram["wus"][e_], inc="wdma")
            ctx.wait("gpsimd", "pe", wd_war[par])
            c_wd[par] = ctx.dma("gpsimd", wdbuf[par][:, :], dram["wds"][e_], inc="wdma")
            ctx.wait("tensor", "vec", cw_c if (ep == 0 and ei == 0) else ctx.val("vec"))
            mb = [(6, 0), (6, 32), (7, 0), (7, 32)]
            cr = [0, 32, 64, 0]
            pe = 0
            for nt in range(NT):
                bk, rw = mb[nt]
                for jj in range(4):
                    j = nt * 4 + jj
                    pe = _mm(ctx, pb[bk][rw:rw + 1, 128 * jj:128 * jj + 128],
                             cwbf1[:, 16 * j + e_:16 * j + e_ + 1], st.ident[:, 0:128],
                             start=True, stop=True,
                             inc="pe" if (nt == NT - 1 and jj == 3) else None)
            ctx.wait("vector", "pe", pe)
            rows_c = [0] * NT
            for nt in range(3):
                bk, rw = mb[nt]
                rows_c[nt] = ctx.op("vector", lambda e, nt=nt, bk=bk, rw=rw:
                                    e.tensor_copy(cwrowB1[cr[nt]:cr[nt] + 1, 0:512],
                                                  pb[bk][rw:rw + 1, 0:512]), inc="vec")
            cpy = [0] * NT
            for nt in range(NT):
                bank = pb[6 + nt % 2]
                if nt == 0:
                    ctx.wait("tensor", "vec", rows_c[1])
                elif nt == 1:
                    ctx.wait("tensor", "vec", rows_c[3])
                else:
                    ctx.wait("tensor", "vec", max(rows_c[nt], cpy[nt - 2]))
                pe = _mm(ctx, bank[:, :], st.ones_bf[cr[nt]:cr[nt] + 1, 0:128],
                         cwrowB1[cr[nt]:cr[nt] + 1, 0:512], True, True, inc="pe")
                ctx.wait("vector", "pe", pe)
                if nt == 0:
                    rows_c[3] = ctx.op("vector", lambda e:
                                       e.tensor_copy(cwrowB1[0:1, 0:512], pb[7][32:33, 0:512]),
                                       inc="vec")
                cpy[nt] = ctx.op("vector", lambda e, nt=nt, bank=bank:
                                 e.tensor_copy(CWBe[:, 512 * nt:512 * nt + 512], bank[:, :]),
                                 inc="vec")
            ctx.wait("tensor", "wdma", c_wu[par])
            silu_c = {}
            act_c = {}
            for ic in range(4):
                for nt in range(NT):
                    i2 = ic * NT + nt
                    spar = i2 % 2
                    tsl = slice(512 * nt, 512 * (nt + 1))
                    bg = pb[i2 % 2]
                    bu = pb[2 + i2 % 2]
                    if i2 >= 2:
                        ctx.wait("tensor", "act", silu_c[i2 - 2])
                        ctx.wait("tensor", "vec", act_c[i2 - 2])
                    for fc in range(NFC):
                        peg = _mm(ctx, bg[:, :], wgbuf[par][:, 512 * fc + 128 * ic:512 * fc + 128 * ic + 128],
                                  st.xn[fc][:, tsl], start=(fc == 0), stop=(fc == NFC - 1),
                                  inc="pe" if fc == NFC - 1 else None)
                    for fc in range(NFC):
                        peu = _mm(ctx, bu[:, :], wubuf[par][:, 512 * fc + 128 * ic:512 * fc + 128 * ic + 128],
                                  st.xn[fc][:, tsl], start=(fc == 0), stop=(fc == NFC - 1),
                                  inc="pe" if fc == NFC - 1 else None)
                    ctx.wait("scalar", "pe", peg)
                    if i2 >= 2:
                        ctx.wait("scalar", "vec", act_c[i2 - 2])
                    silu_c[i2] = ctx.op("scalar", lambda e, spar=spar, bg=bg:
                                        e.activation(sg[spar][:, :], bg[:, :], AF.Silu), inc="act")
                    ctx.wait("vector", "pe", peu)
                    ctx.wait("vector", "act", silu_c[i2])
                    ctx.op("vector", lambda e, nt=nt, bu=bu:
                           e.tensor_mul(uprime[:, :], bu[:, :], CWBe[:, 512 * nt:512 * nt + 512]))
                    act_c[i2] = ctx.op("vector", lambda e, ei=ei, ic=ic, tsl=tsl, spar=spar:
                                       e.tensor_mul(ACT1[ei][ic][:, tsl], sg[spar][:, :],
                                                    uprime[:, :]), inc="vec")
            wg_war[par] = wu_war[par] = ctx.val("pe")
            acts_done = act_c[15]
        ctx.wait("tensor", "vec", acts_done)
        ctx.wait("tensor", "wdma", max(c_wd))
        ev = {}
        for dc in range(NFC):
            for nt in range(NT):
                i2 = dc * NT + nt
                tsl = slice(512 * nt, 512 * (nt + 1))
                bank = pb[4 + i2 % 2]
                if i2 >= 2:
                    ctx.wait("tensor", "vec", ev[i2 - 2])
                for ei in range(2):
                    for ic in range(4):
                        pe = _mm(ctx, bank[:, :],
                                 wdbuf[ei][:, 1024 * ic + 128 * dc:1024 * ic + 128 * dc + 128],
                                 ACT1[ei][ic][:, tsl],
                                 start=(ei == 0 and ic == 0), stop=(ei == 1 and ic == 3),
                                 inc="pe" if (ei == 1 and ic == 3) else None)
                ctx.wait("vector", "pe", pe)
                ev[i2] = ctx.op("vector", lambda e, dc=dc, tsl=tsl, bank=bank:
                                e.tensor_add(st.x[dc][:, tsl], bank[:, :], st.x[dc][:, tsl]),
                                inc="vec")
        wd_war[0] = wd_war[1] = ctx.val("pe")
        ctx.wait("tensor", "vec", ctx.val("vec"))

    ws_war = [0, 0]
    c_ws = {}
    ctx.wait("gpsimd", "pe", ctx.val("pe"))
    for ic0 in (0, 1):
        c_ws[ic0] = ctx.dma("gpsimd", wsgb[ic0][:, :], dram["wsgs"][ic0], inc="wdma")
        c_ws[ic0] = ctx.dma("gpsimd", wsub[ic0][:, :], dram["wsus"][ic0], inc="wdma")
    silu_c = {}
    act_c = {}
    ctx.wait("tensor", "vec", ctx.val("vec"))
    for ic in range(8):
        par = ic % 2
        ctx.wait("tensor", "wdma", c_ws[ic])
        for nt in range(NT):
            i2 = ic * NT + nt
            spar = i2 % 2
            tsl = slice(512 * nt, 512 * (nt + 1))
            bg = pb[i2 % 2]
            bu = pb[2 + i2 % 2]
            if i2 >= 2:
                ctx.wait("tensor", "act", silu_c[i2 - 2])
                ctx.wait("tensor", "vec", act_c[i2 - 2])
            for fc in range(NFC):
                peg = _mm(ctx, bg[:, :], wsgb[par][:, 128 * fc:128 * fc + 128],
                          st.xn[fc][:, tsl], start=(fc == 0), stop=(fc == NFC - 1),
                          inc="pe" if fc == NFC - 1 else None)
            for fc in range(NFC):
                peu = _mm(ctx, bu[:, :], wsub[par][:, 128 * fc:128 * fc + 128],
                          st.xn[fc][:, tsl], start=(fc == 0), stop=(fc == NFC - 1),
                          inc="pe" if fc == NFC - 1 else None)
            ctx.wait("scalar", "pe", peg)
            if i2 >= 2:
                ctx.wait("scalar", "vec", act_c[i2 - 2])
            silu_c[i2] = ctx.op("scalar", lambda e, spar=spar, bg=bg:
                                e.activation(sg[spar][:, :], bg[:, :], AF.Silu), inc="act")
            ctx.wait("vector", "pe", peu)
            ctx.wait("vector", "act", silu_c[i2])
            act_c[i2] = ctx.op("vector", lambda e, ic=ic, tsl=tsl, spar=spar, bu=bu:
                               e.tensor_mul(sact[ic][:, tsl], bu[:, :], sg[spar][:, :]), inc="vec")
        if ic + 2 < 8:
            ctx.wait("gpsimd", "pe", ctx.val("pe"))
            c_ws[ic + 2] = ctx.dma("gpsimd", wsgb[par][:, :], dram["wsgs"][ic + 2], inc="wdma")
            c_ws[ic + 2] = ctx.dma("gpsimd", wsub[par][:, :], dram["wsus"][ic + 2], inc="wdma")
    ctx.wait("tensor", "vec", act_c[31])
    c_wsd = {}
    ctx.wait("gpsimd", "pe", ctx.val("pe"))
    for dc0 in (0, 1):
        c_wsd[dc0] = ctx.dma("gpsimd", wsdb[dc0][:, :], dram["wsds"][dc0], inc="wdma")
    ev = {}
    for dc in range(NFC):
        par = dc % 2
        ctx.wait("tensor", "wdma", c_wsd[dc])
        for nt in range(NT):
            i2 = dc * NT + nt
            tsl = slice(512 * nt, 512 * (nt + 1))
            bank = pb[4 + i2 % 2]
            if i2 >= 2:
                ctx.wait("tensor", "vec", ev[i2 - 2])
            for ic in range(8):
                pe = _mm(ctx, bank[:, :], wsdb[par][:, 128 * ic:128 * ic + 128],
                         sact[ic][:, tsl], start=(ic == 0), stop=(ic == 7),
                         inc="pe" if ic == 7 else None)
            ctx.wait("vector", "pe", pe)
            ev[i2] = ctx.op("vector", lambda e, dc=dc, tsl=tsl, bank=bank:
                            e.tensor_add(st.x[dc][:, tsl], bank[:, :], st.x[dc][:, tsl]),
                            inc="vec")
        if dc + 2 < 8:
            ctx.wait("gpsimd", "pe", ctx.val("pe"))
            c_wsd[dc + 2] = ctx.dma("gpsimd", wsdb[par][:, :], dram["wsds"][dc + 2], inc="wdma")
    ctx.wait("tensor", "vec", ctx.val("vec"))



def kernel(**inputs):
    shared = _prep_shared(inputs)
    x = np.asarray(inputs["L0_attn_x"], np.float32)
    nc = build_program()
    in_maps = []
    for c in range(8):
        m = dict(shared)
        m["xin"] = np.ascontiguousarray(x[c].T).reshape(NFC, 128, T)
        in_maps.append(m)
    res = run_bass_kernel_spmd(nc, in_maps, core_ids=list(range(8)))
    out = np.empty((8, T, D), np.float32)
    for c in range(8):
        out[c] = res.results[c]["out"].reshape(D, T).T
    return out


# version 19
# speedup vs baseline: 1.0695x; 1.0397x over previous
import numpy as np
import concourse.bass as bass
import concourse.mybir as mybir
from concourse.bass_utils import run_bass_kernel_spmd

F32 = mybir.dt.float32
BF16 = mybir.dt.bfloat16
AF = mybir.ActivationFunctionType
ALU = mybir.AluOpType
AX = mybir.AxisListType

T = 2048
D = 1024
NT = T // 512
NFC = D // 128
EPS_LN = 1e-5
DEV_A0_HEADS = 16


def _prep_shared(inp):
    f32 = np.float32
    d = {}
    latw = np.asarray(inp["L0_attn_latent_w"], f32)
    latp = np.zeros((8, 128, 384), f32)
    latp[:, :, :288] = latw.reshape(8, 128, 288)
    d["latw"] = np.ascontiguousarray(latp.transpose(1, 0, 2)).reshape(128, 8 * 384)
    latb = np.zeros(384, f32); latb[:288] = np.asarray(inp["L0_attn_latent_b"], f32)
    d["latb"] = np.ascontiguousarray(latb.reshape(3, 128).T)
    latbr = np.zeros((1, 384), f32); latbr[0, :288] = latb[:288]
    d["latbr"] = latbr
    wd = np.asarray(inp["L0_attn_Wd_w"], f32)
    wdp = np.zeros((16, 8, 128, 384), f32)
    wdp[:, :, :, :288] = wd.reshape(16, 8, 128, 288)
    d["wd"] = np.ascontiguousarray(wdp.transpose(0, 2, 1, 3)).reshape(16, 128, 8 * 384)
    wdb = np.zeros((16, 384), f32); wdb[:, :288] = np.asarray(inp["L0_attn_Wd_b"], f32)
    d["wdb"] = np.ascontiguousarray(
        wdb.reshape(16, 3, 128).transpose(2, 0, 1)).reshape(128, 48)
    ow_src = np.asarray(inp["L0_attn_out_w"], f32).reshape(16, 288, 1024)
    ow = np.zeros((16, 3, 128, 1024), f32)
    ow[:, 0] = ow_src[:, 0:128]
    ow[:, 1] = ow_src[:, 128:256]
    ow[:, 2, 0:32] = ow_src[:, 256:288]
    ow[0, 2, 32] = np.asarray(inp["L0_attn_out_b"], f32)
    d["outw"] = np.ascontiguousarray(ow.transpose(0, 2, 1, 3)).reshape(16, 128, 3 * 1024)
    w1 = np.concatenate([np.asarray(inp["L0_ffn_w1"], f32),
                         np.asarray(inp["L0_ffn_new_w1"], f32)], axis=0)
    w3 = np.concatenate([np.asarray(inp["L0_ffn_w3"], f32),
                         np.asarray(inp["L0_ffn_new_w3"], f32)], axis=0)
    w2 = np.concatenate([np.asarray(inp["L0_ffn_w2"], f32),
                         np.asarray(inp["L0_ffn_new_w2"], f32)], axis=0)
    d["w1s"] = np.ascontiguousarray(
        w1.reshape(9, 22, 128, 8, 128).transpose(0, 1, 4, 3, 2)).reshape(9, 22, 128, 1024)
    d["w3s"] = np.ascontiguousarray(
        w3.reshape(9, 22, 128, 8, 128).transpose(0, 1, 4, 3, 2)).reshape(9, 22, 128, 1024)
    d["w2s"] = np.ascontiguousarray(
        w2.reshape(9, 8, 128, 22, 128).transpose(0, 1, 4, 3, 2)).reshape(9, 8, 128, 2816)
    g0 = np.concatenate([np.asarray(inp["L0_ffn_gate_weight"], f32),
                         np.asarray(inp["L0_ffn_new_gate_weight"], f32)], axis=0)
    d["g0w"] = np.ascontiguousarray(
        g0.reshape(9, 8, 128).transpose(2, 1, 0)).reshape(128, 72)
    g0b = np.zeros((1, 9), f32); g0b[0, 8] = np.asarray(inp["L0_ffn_gate_bias"], f32)[0]
    d["g0b"] = g0b
    H1, HD1 = 16, 64
    perm = np.zeros(1024, np.int64)
    for h in range(16):
        perm[h * 64:h * 64 + 32] = h * 64 + 2 * np.arange(32)
        perm[h * 64 + 32:h * 64 + 64] = h * 64 + 2 * np.arange(32) + 1
    wq = np.asarray(inp["L1_attn_Wq"], f32)[:, perm]
    d["wq"] = np.ascontiguousarray(
        wq.reshape(8, 128, 16, 64).transpose(2, 1, 0, 3)).reshape(16, 128, 512)
    wk = np.asarray(inp["L1_attn_Wk"], f32)
    wkp = np.zeros((1024, 256), f32)
    for f in range(32):
        for r in range(3):
            wkp[:, f * 3 + r] = wk[:, f * 6 + 2 * r]
            wkp[:, 128 + f * 3 + r] = wk[:, f * 6 + 2 * r + 1]
    d["wk"] = np.ascontiguousarray(
        wkp.reshape(8, 128, 256).transpose(1, 0, 2)).reshape(128, 2048)
    wv = np.asarray(inp["L1_attn_Wv"], f32)
    d["wv"] = np.ascontiguousarray(
        wv.reshape(8, 128, 192).transpose(1, 0, 2)).reshape(128, 1536)
    kd = np.asarray(inp["L1_attn_key_decoder"], f32)
    bdr = np.zeros((128, 512), f32); bdi = np.zeros((128, 512), f32)
    for f in range(32):
        for h in range(16):
            for r in range(3):
                bdr[f * 3 + r, h * 32 + f] = kd[f, h, r, 0]
                bdi[f * 3 + r, h * 32 + f] = kd[f, h, r, 1]
    d["bdr"], d["bdi"] = bdr, bdi
    vdec = np.asarray(inp["L1_attn_value_decoder"], f32)
    vdp = np.zeros((16, 128, 128), f32)
    vdp[:, :, 0:64] = vdec[:, 0:128]
    vdp[:, 64:128, 64:128] = vdec[:, 128:192]
    d["vd"] = np.ascontiguousarray(vdp)
    d["wproj"] = np.ascontiguousarray(
        np.asarray(inp["L1_attn_Wproj"], f32).reshape(8, 128, 8, 128)
        .transpose(2, 1, 0, 3)).reshape(8, 128, 1024)
    half = HD1 // 2
    freq = (1.0 / (10000.0 ** (np.arange(half, dtype=f32) / half))).astype(f32)
    ang = np.outer(np.arange(T, dtype=f32), freq)
    cos = np.cos(ang).T.astype(f32)
    sin = np.sin(ang).T.astype(f32)
    d["tcq"] = np.ascontiguousarray(np.concatenate([cos, cos], 0) * 0.125)
    d["tsq"] = np.ascontiguousarray(np.concatenate([-sin, sin], 0) * 0.125)
    d["tck"] = np.ascontiguousarray(np.concatenate([cos, cos], 0))
    d["tsk"] = np.ascontiguousarray(np.concatenate([-sin, sin], 0))
    d["g1w"] = np.ascontiguousarray(
        np.asarray(inp["L1_ffn_gate_w"], f32).reshape(16, 8, 128).transpose(2, 1, 0)).reshape(128, 128)
    d["wgs"] = np.ascontiguousarray(
        np.asarray(inp["L1_ffn_w_gate"], f32).reshape(16, 8, 128, 4, 128)
        .transpose(0, 2, 1, 3, 4)).reshape(16, 128, 4096)
    d["wus"] = np.ascontiguousarray(
        np.asarray(inp["L1_ffn_w_up"], f32).reshape(16, 8, 128, 4, 128)
        .transpose(0, 2, 1, 3, 4)).reshape(16, 128, 4096)
    d["wds"] = np.ascontiguousarray(
        np.asarray(inp["L1_ffn_w_down"], f32).reshape(16, 4, 128, 8, 128)
        .transpose(0, 2, 1, 3, 4)).reshape(16, 128, 4096)
    d["wsgs"] = np.ascontiguousarray(
        np.asarray(inp["L1_ffn_ws_gate"], f32).reshape(8, 128, 8, 128)
        .transpose(2, 1, 0, 3)).reshape(8, 128, 1024)
    d["wsus"] = np.ascontiguousarray(
        np.asarray(inp["L1_ffn_ws_up"], f32).reshape(8, 128, 8, 128)
        .transpose(2, 1, 0, 3)).reshape(8, 128, 1024)
    d["wsds"] = np.ascontiguousarray(
        np.asarray(inp["L1_ffn_ws_down"], f32).reshape(8, 128, 8, 128)
        .transpose(2, 1, 0, 3)).reshape(8, 128, 1024)
    d["c_ident"] = np.eye(128, dtype=f32)
    tri = np.zeros((128, 128), f32)
    iu = np.triu_indices(128)
    tri[iu] = 1.0
    d["c_tri"] = tri
    pm = np.zeros((128, 128), f32)
    for i in range(64):
        pm[(i + 32) % 64, i] = 1.0
    d["c_perm"] = pm
    return d



ENGINES = ("sync", "tensor", "vector", "scalar", "gpsimd")


class Ctx:
    def __init__(self, nc):
        self.nc = nc
        self.ops = {e: [] for e in ENGINES}
        self.sems = {}

    def _sem(self, name):
        if name not in self.sems:
            self.sems[name] = [self.nc.alloc_semaphore(name), 0]
        return self.sems[name]

    def val(self, name):
        return self._sem(name)[1]

    def op(self, eng, builder, inc=None, inck=1):
        if inc is not None:
            s = self._sem(inc)
            s[1] += inck
            h = s[0]
            def f(e, b=builder, h=h, k=inck):
                b(e).then_inc(h, k)
            self.ops[eng].append(f)
            return s[1]
        else:
            def f(e, b=builder):
                b(e)
            self.ops[eng].append(f)
            return None

    def wait(self, eng, sname, val):
        if val <= 0:
            return
        h = self._sem(sname)[0]
        def f(e, h=h, v=val):
            e.wait_ge(h, v)
        self.ops[eng].append(f)

    def dma(self, eng, out_ap, in_ap, inc):
        return self.op(eng, lambda e, o=out_ap, i=in_ap: e.dma_start(o, i), inc=inc, inck=16)


def _mm(ctx, out, lhsT, rhs, start, stop, inc=None):
    return ctx.op("tensor",
                  lambda e, o=out, l=lhsT, r=rhs, st=start, sp=stop:
                  e.matmul(o, l, r, start=st, stop=sp), inc=inc)



ALL_PHASES = ("ln0", "attn0", "ffn0", "ln1", "attn1", "ffn1")


class St:
    pass


def _alloc(nc):
    st = St()
    st.x = [nc.alloc_sbuf_tensor(f"x{i}", [128, T], F32) for i in range(NFC)]
    st.xn = [nc.alloc_sbuf_tensor(f"xn{i}", [128, T], BF16) for i in range(NFC)]
    st.onesM = nc.alloc_sbuf_tensor("onesM", [128, 128], F32)
    st.ones_bf = nc.alloc_sbuf_tensor("ones_bf", [128, 128], BF16)
    st.ident = nc.alloc_sbuf_tensor("ident", [128, 128], BF16)
    st.tri = nc.alloc_sbuf_tensor("tri", [128, 128], BF16)
    st.latb = nc.alloc_sbuf_tensor("sb_latb", [128, 3], F32)
    st.wdb = nc.alloc_sbuf_tensor("sb_wdb", [128, 48], F32)
    st.AR = nc.alloc_sbuf_tensor("AR", [128, 48640], BF16)
    st.ARF = nc.alloc_sbuf_tensor("ARF", [128, 2656], F32)
    st.pb = [nc.alloc_psum_tensor(f"pb{i}", [128, 512], F32) for i in range(8)]
    st.ptb = None
    return st


def _emit_prologue(ctx, st, dram):
    for nt in range(NT):
        for fc in range(NFC):
            ctx.dma("sync", st.x[fc][:, 512 * nt:512 * (nt + 1)],
                    dram["xin"][fc][:, 512 * nt:512 * (nt + 1)], inc="xdma")
    ctx.dma("gpsimd", st.ident[:, :], dram["c_ident"][:, :], inc="wdma")
    ctx.dma("gpsimd", st.tri[:, :], dram["c_tri"][:, :], inc="wdma")
    ctx.dma("sync", st.latb[:, :], dram["latb"][:, :], inc="xdma")
    ctx.dma("sync", st.wdb[:, :], dram["wdb"][:, :], inc="xdma")
    ctx.op("vector", lambda e: e.memset(st.onesM[:, :], 1.0), inc="vec")
    ctx.op("vector", lambda e: e.memset(st.ones_bf[:, :], 1.0), inc="vec")
    st.after_prologue = dict(xdma=ctx.val("xdma"), wdma=ctx.val("wdma"), vec=ctx.val("vec"))


def _emit_ln(ctx, st, li):
    pb, ARF = st.pb, st.ARF
    if li == 0:
        ctx.wait("tensor", "vec", st.after_prologue["vec"])
    else:
        ctx.wait("vector", "pe", ctx.val("pe"))
        ctx.wait("scalar", "pe", ctx.val("pe"))
        ctx.wait("scalar", "vec", ctx.val("vec"))
        ctx.wait("tensor", "act", ctx.val("act"))
    mrow = ARF[0:1, 0:512]; erow = ARF[0:1, 512:1024]; rrow = ARF[0:1, 1024:1536]
    stats_c = [0] * NT
    norm_c = [0] * NT
    sqrt_c = [0] * NT
    last_mm = [0] * NT
    bcast_pe = [0] * NT
    sq_mm = {}

    def emit_sums(nt):
        sl = slice(512 * nt, 512 * (nt + 1))
        bsum = pb[2 * (nt % 2)]
        bsq = pb[2 * (nt % 2) + 1]
        if li == 0:
            ctx.wait("vector", "xdma", 128 * (nt + 1))
            ctx.wait("scalar", "xdma", 128 * (nt + 1))
        vc = None
        for fc in range(NFC):
            vc = ctx.op("vector", lambda e, fc=fc, sl=sl:
                        e.tensor_copy(st.xn[fc][:, sl], st.x[fc][:, sl]), inc="vec")
        ctx.wait("tensor", "vec", vc)
        if nt >= 2:
            ctx.wait("tensor", "vec", stats_c[nt - 2])
        for fc in range(NFC):
            _mm(ctx, bsum[0:1, :], st.ones_bf[:, 0:1], st.xn[fc][:, sl],
                start=(fc == 0), stop=(fc == NFC - 1))
        for fc in range(NFC):
            k = fc % 2
            war = sq_mm.get(fc - 2 if fc >= 2 else NFC - 2 + fc)
            if war:
                ctx.wait("scalar", "pe", war)
            sqv = ctx.op("scalar", lambda e, fc=fc, sl=sl, k=k:
                         e.activation(st.AR[:, 512 * k:512 * k + 512], st.x[fc][:, sl],
                                      AF.Square), inc="act")
            ctx.wait("tensor", "act", sqv)
            sq_mm[fc] = _mm(ctx, bsq[0:1, :], st.ones_bf[:, 0:1],
                            st.AR[:, 512 * k:512 * k + 512],
                            start=(fc == 0), stop=(fc == NFC - 1), inc="pe")
        last_mm[nt] = sq_mm[NFC - 1]

    def emit_stats_norm(nt):
        sl = slice(512 * nt, 512 * (nt + 1))
        bsum = pb[2 * (nt % 2)]
        bsq = pb[2 * (nt % 2) + 1]
        mb = pb[4 + 2 * (nt % 2)]
        rb = pb[5 + 2 * (nt % 2)]
        ctx.wait("vector", "pe", max(last_mm[nt], bcast_pe[nt - 1] if nt else 0))
        if nt:
            ctx.wait("vector", "act", sqrt_c[nt - 1])
        ctx.op("vector", lambda e: e.tensor_scalar(mrow, bsum[0:1, :], 1.0 / D, None, ALU.mult))
        ctx.op("vector", lambda e: e.tensor_scalar(erow, bsq[0:1, :], 1.0 / D, None, ALU.mult))
        ctx.op("vector", lambda e: e.tensor_tensor(rrow, mrow, mrow, ALU.mult))
        ctx.op("vector", lambda e: e.tensor_sub(erow, erow, rrow))
        ctx.op("vector", lambda e: e.tensor_scalar(erow, erow, EPS_LN, None, ALU.add))
        stats_c[nt] = ctx.op("vector", lambda e: e.reciprocal(erow, erow), inc="vec")
        ctx.wait("scalar", "vec", stats_c[nt])
        sqrt_c[nt] = ctx.op("scalar", lambda e: e.activation(rrow, erow, AF.Sqrt), inc="act")
        ctx.wait("tensor", "act", sqrt_c[nt])
        if nt >= 2:
            ctx.wait("tensor", "vec", norm_c[nt - 2])
        _mm(ctx, mb[:, :], st.onesM[0:1, 0:128], mrow, start=True, stop=True)
        bcast_pe[nt] = pe = _mm(ctx, rb[:, :], st.onesM[0:1, 0:128], rrow, start=True, stop=True,
                                inc="pe")
        ctx.wait("vector", "pe", pe)
        for fc in range(NFC):
            ctx.op("vector", lambda e, fc=fc, sl=sl:
                   e.tensor_sub(ARF[:, 1536:2048], st.x[fc][:, sl], mb[:, :]))
            norm_c[nt] = ctx.op("vector", lambda e, fc=fc, sl=sl:
                                e.tensor_mul(st.xn[fc][:, sl], ARF[:, 1536:2048], rb[:, :]),
                                inc="vec")

    emit_sums(0)
    emit_sums(1)
    emit_stats_norm(0)
    emit_sums(2)
    emit_stats_norm(1)
    emit_sums(3)
    emit_stats_norm(2)
    emit_stats_norm(3)
    ctx.wait("tensor", "vec", norm_c[NT - 1])
    st.ln_done = dict(vec=ctx.val("vec"))


def _emit_store_dbg(ctx, st, dram):
    ctx.wait("gpsimd", "vec", ctx.val("vec"))
    ctx.wait("gpsimd", "pe", ctx.val("pe"))
    ctx.wait("gpsimd", "act", ctx.val("act"))
    ctx.wait("sync", "vec", ctx.val("vec"))
    ctx.dma("gpsimd", dram["out"][0][:, 0:144], st.AR[:, 33280:33424], inc="stdma")
    ctx.dma("sync", dram["out"][1][:, 0:1024], st.ARF[:, 0:1024], inc="stdma2")
    ctx.dma("gpsimd", dram["out"][2][:, 0:1024], st.AR[:, 0:1024], inc="stdma")
    ctx.dma("gpsimd", dram["out"][3][:, 0:1024], st.AR[:, 21504:22528], inc="stdma")
    ctx.dma("gpsimd", dram["out"][4][:, 0:72], st.AR[:, 33424:33496], inc="stdma")
    ctx.dma("gpsimd", dram["out"][5][:, 0:2048], st.AR[:, 26624:28672], inc="stdma")
    ctx.dma("gpsimd", dram["out"][6][:, 0:1024], st.AR[:, 22528:23552], inc="stdma")
    ctx.wait("gpsimd", "stdma", ctx.val("stdma"))
    ctx.wait("sync", "stdma2", ctx.val("stdma2"))


def _emit_store(ctx, st, dram, src="x"):
    tiles = st.x if src == "x" else st.xn
    eng = "sync" if src == "x" else "gpsimd"
    ctx.wait(eng, "vec", ctx.val("vec"))
    ctx.wait(eng, "pe", ctx.val("pe"))
    ctx.wait(eng, "act", ctx.val("act"))
    for fc in range(NFC):
        ctx.dma(eng, dram["out"][fc], tiles[fc][:, :], inc="stdma")
    ctx.wait(eng, "stdma", ctx.val("stdma"))


def build_program(phases=ALL_PHASES, store_src="x"):
    nc = bass.Bass("TRN2", target_bir_lowering=False, detect_race_conditions=False)
    names_shapes = {
        "xin": (NFC, 128, T), "latw": (128, 3072), "latb": (128, 3), "latbr": (1, 384),
        "wd": (16, 128, 3072),
        "wdb": (128, 48), "outw": (16, 128, 3072), "w1s": (9, 22, 128, 1024),
        "w3s": (9, 22, 128, 1024), "w2s": (9, 8, 128, 2816), "g0w": (128, 72), "g0b": (1, 9),
        "wq": (16, 128, 512), "wk": (128, 2048), "wv": (128, 1536), "bdr": (128, 512),
        "bdi": (128, 512), "vd": (16, 128, 128), "wproj": (8, 128, 1024),
        "tcq": (64, T), "tsq": (64, T), "tck": (64, T), "tsk": (64, T),
        "g1w": (128, 128), "wgs": (16, 128, 4096), "wus": (16, 128, 4096),
        "wds": (16, 128, 4096), "wsgs": (8, 128, 1024), "wsus": (8, 128, 1024),
        "wsds": (8, 128, 1024), "c_ident": (128, 128), "c_tri": (128, 128),
        "c_perm": (128, 128),
    }
    dram = {k: nc.dram_tensor(k, list(v), F32, kind="ExternalInput")
            for k, v in names_shapes.items()}
    dram["out"] = nc.dram_tensor("out", [NFC, 128, T], F32, kind="ExternalOutput")
    st = _alloc(nc)
    ctx = Ctx(nc)
    _emit_prologue(ctx, st, dram)
    for ph in phases:
        if ph == "ln0":
            _emit_ln(ctx, st, 0)
        elif ph == "ln1":
            _emit_ln(ctx, st, 1)
        elif ph == "attn0":
            _emit_attn0(ctx, st, dram)
        elif ph == "ffn0":
            _emit_ffn0(ctx, st, dram)
        elif ph == "attn1":
            _emit_attn1(ctx, st, dram)
        elif ph == "ffn1":
            _emit_ffn1(ctx, st, dram)
    if store_src == "dbg":
        _emit_store_dbg(ctx, st, dram)
    else:
        _emit_store(ctx, st, dram, src=store_src)

    with nc.Block() as block:
        @block.sync
        def _(e):
            for f in ctx.ops["sync"]:
                f(e)
        @block.tensor
        def _(e):
            for f in ctx.ops["tensor"]:
                f(e)
        @block.vector
        def _(e):
            for f in ctx.ops["vector"]:
                f(e)
        @block.scalar
        def _(e):
            for f in ctx.ops["scalar"]:
                f(e)
        @block.gpsimd
        def _(e):
            for f in ctx.ops["gpsimd"]:
                f(e)
    return nc



def _emit_attn0(ctx, st, dram):
    AR, ARF, pb = st.AR, st.ARF, st.pb
    PT = [AR[:, 512 * kc:512 * (kc + 1)] for kc in range(16)]
    y_t = [AR[:, 8192:10240], AR[:, 10240:12288], AR[:, 12288:14336]]
    kv_t = [AR[:, 14336:16384], AR[:, 16384:18432], AR[:, 18432:20480]]
    KVT = [AR[:, 20480 + 384 * kc:20480 + 384 * kc + 384] for kc in range(16)]
    q_t = [AR[:, 26624:28672], AR[:, 28672:30720], AR[:, 30720:32768]]
    latw = AR[:, 32768:35840]
    wdbuf = [AR[:, 35840:38912], AR[:, 38912:41984]]
    owbuf = [AR[:, 41984:45056], AR[:, 45056:48128]]
    recB = AR[:, 48128:48640]
    latbr_bf = AR[:, 8192:8576]
    Rt = ARF[:, 0:512]
    recF = ARF[:, 512:1024]

    c_latw = ctx.dma("gpsimd", latw[:, :], dram["latw"][:, :], inc="wdma")
    ctx.wait("vector", "xdma", st.after_prologue["xdma"])
    ctx.op("vector", lambda e: e.tensor_scalar_mul(st.wdb[:, :], st.wdb[:, :], 0.125), inc="vec")
    ctx.op("vector", lambda e: e.memset(y_t[2][32:64, :], 0.0))
    ctx.op("vector", lambda e: e.memset(y_t[2][64:128, :], 0.0), inc="vec")
    wdb8_done = ctx.val("vec")
    ctx.wait("vector", "wdma", 32)
    c_wd = [0] * 16
    c_ow = [0] * 16
    for h in (0, 1):
        c_wd[h] = ctx.dma("gpsimd", wdbuf[h][:, :], dram["wd"][h], inc="wdma")
        c_ow[h] = ctx.dma("gpsimd", owbuf[h][:, :], dram["outw"][h], inc="wdma")

    ctx.wait("tensor", "wdma", c_latw)
    ctx.wait("scalar", "xdma", st.after_prologue["xdma"])
    evac_c = 0
    for lc in range(3):
        for nt in range(NT):
            sl = slice(512 * nt, 512 * (nt + 1))
            bank = pb[(lc * NT + nt) % 2]
            if lc * NT + nt >= 2:
                ctx.wait("tensor", "act", evac_c - 1)
            for fc in range(NFC):
                lh = latw[:, fc * 384 + 128 * lc: fc * 384 + 128 * lc + 128]
                pe = _mm(ctx, bank[:, :], lh, st.xn[fc][:, sl],
                         start=(fc == 0), stop=(fc == NFC - 1),
                         inc="pe" if fc == NFC - 1 else None)
            ctx.wait("scalar", "pe", pe)
            evac_c = ctx.op("scalar", lambda e, o=kv_t[lc][:, sl], i=bank[:, :],
                            b=st.latb[:, lc:lc + 1]:
                            e.activation(o, i, AF.Identity, bias=b), inc="act")
    c_lbr = ctx.dma("gpsimd", latbr_bf[0:1, 0:384], dram["latbr"][:, :], inc="wdma")
    ctx.wait("tensor", "wdma", c_lbr)
    kve = 0
    for kc in range(16):
        ksl = slice(128 * kc, 128 * (kc + 1))
        bank = pb[2 + kc % 2]
        if kc >= 2:
            ctx.wait("tensor", "act", kve - 1)
        for fc in range(NFC):
            _mm(ctx, bank[:, 0:384], st.xn[fc][:, ksl], latw[:, fc * 384:fc * 384 + 384],
                start=(fc == 0), stop=False)
        pe = _mm(ctx, bank[:, 0:384], st.ones_bf[0:1, 0:128], latbr_bf[0:1, 0:384],
                 start=False, stop=True, inc="pe")
        ctx.wait("scalar", "pe", pe)
        kve = ctx.op("scalar", lambda e, kc=kc, bank=bank:
                     e.copy(KVT[kc][:, 0:384], bank[:, 0:384]), inc="act")
    ctx.wait("vector", "act", kve)
    for kc in range(16):
        ctx.op("vector", lambda e, kc=kc: e.memset(KVT[kc][:, 288:289], 1.0),
               inc="vec" if kc == 15 else None)

    st.gate0_gen = _gen_ffn0_gating(ctx, st, dram)
    st.gate0_cwc = 0
    hold = {'yev': ctx.val("vec"), 'tail': None, 'pe_last': 0, 'ytile': [0] * NT}
    for h in range(DEV_A0_HEADS):
        hb = h % 2
        ctx.wait("tensor", "wdma", c_wd[h])
        if h == 0:
            ctx.wait("scalar", "vec", wdb8_done)
        qe_c = 0
        for lc in range(3):
            for nt in range(NT):
                sl = slice(512 * nt, 512 * (nt + 1))
                bank = pb[(lc * NT + nt) % 2]
                ctx.wait("tensor", "vec", ctx.val("vec"))
                if lc * NT + nt >= 2:
                    ctx.wait("tensor", "act", qe_c - 1)
                for fc in range(NFC):
                    lh = wdbuf[hb][:, fc * 384 + 128 * lc: fc * 384 + 128 * lc + 128]
                    pe = _mm(ctx, bank[:, :], lh, st.xn[fc][:, sl],
                             start=(fc == 0), stop=(fc == NFC - 1),
                             inc="pe" if fc == NFC - 1 else None)
                ctx.wait("scalar", "pe", pe)
                qe_c = ctx.op("scalar", lambda e, o=q_t[lc][:, sl], i=bank[:, :],
                              b=st.wdb[:, 3 * h + lc:3 * h + lc + 1]:
                              e.activation(o, i, AF.Identity, bias=b, scale=0.125), inc="act")
        if h + 2 < 16:
            ctx.wait("gpsimd", "pe", ctx.val("pe"))
            c_wd[h + 2] = ctx.dma("gpsimd", wdbuf[hb][:, :], dram["wd"][h + 2], inc="wdma")
        ctx.wait("tensor", "act", qe_c)

        def gen_sty(qt, exp_prev):
            Q0 = 512 * qt
            Kc = Q0 // 128 + 4
            qsl = slice(Q0, Q0 + 512)
            c_exp = [0] * Kc
            c_fix = [0] * Kc

            def emit_st(kc):
                pbS = pb[3 + kc % 2]
                ctx.wait("tensor", "act", c_exp[kc - 2] if kc >= 2 else exp_prev)
                _mm(ctx, pbS[:, :], kv_t[0][:, 128 * kc:128 * kc + 128], q_t[0][:, qsl], True, False)
                _mm(ctx, pbS[:, :], kv_t[1][:, 128 * kc:128 * kc + 128], q_t[1][:, qsl], False, False)
                pe = _mm(ctx, pbS[:, :], kv_t[2][:, 128 * kc:128 * kc + 128], q_t[2][:, qsl],
                         False, True, inc="pe")
                ctx.wait("scalar", "pe", pe)
                c_exp[kc] = ctx.op("scalar", lambda e, kc=kc, pbS=pbS:
                                   e.activation(PT[kc][:, 0:512], pbS[:, :], AF.Exp), inc="act")
                kr = kc - Q0 // 128
                if kr >= 0:
                    ctx.wait("vector", "act", c_exp[kc])
                    for jj in range(kr):
                        ctx.op("vector", lambda e, kc=kc, jj=jj:
                               e.memset(PT[kc][:, 128 * jj:128 * jj + 128], 0.0))
                    c_fix[kc] = ctx.op("vector", lambda e, kc=kc, kr=kr:
                                       e.tensor_mul(PT[kc][:, 128 * kr:128 * kr + 128],
                                                    PT[kc][:, 128 * kr:128 * kr + 128],
                                                    st.tri[:, :]), inc="vec")

            def emit_y(kc):
                ctx.wait("tensor", "vec" if c_fix[kc] else "act",
                         c_fix[kc] if c_fix[kc] else c_exp[kc])
                if kc == 0:
                    ctx.wait("tensor", "vec", hold['yev'])
                st_ = (kc == 0)
                sp_ = (kc == Kc - 1)
                _mm(ctx, pb[5][:, :], KVT[kc][:, 0:128], PT[kc][:, 0:512], st_, sp_)
                _mm(ctx, pb[6][:, :], KVT[kc][:, 128:256], PT[kc][:, 0:512], st_, sp_)
                return _mm(ctx, pb[7][:, :], KVT[kc][:, 256:384], PT[kc][:, 0:512], st_, sp_,
                           inc="pe" if sp_ else None)

            for kc in range(Kc):
                emit_st(kc)
                yield
            for kc in range(Kc - 1):
                emit_y(kc)
                yield
            hold['pe_last'] = emit_y(Kc - 1)

        def gen_tail(qt):
            qsl = slice(512 * qt, 512 * qt + 512)
            ctx.wait("vector", "pe", hold['pe_last'])
            vc = ctx.op("vector", lambda e: e.tensor_copy(recB[32:33, :], pb[7][32:33, :]), inc="vec")
            yield
            ctx.wait("tensor", "vec", vc)
            pe = _mm(ctx, pb[2][:, :], st.ones_bf[32:33, 0:128], recB[32:33, :], True, True, inc="pe")
            ctx.wait("vector", "pe", pe)
            ctx.op("vector", lambda e: e.reciprocal(Rt[:, :], pb[2][:, :]))
            ctx.op("vector", lambda e, qsl=qsl: e.tensor_mul(y_t[0][:, qsl], pb[5][:, :], Rt[:, :]))
            ctx.op("vector", lambda e, qsl=qsl: e.tensor_mul(y_t[1][:, qsl], pb[6][:, :], Rt[:, :]))
            hold['yev'] = ctx.op("vector", lambda e, qsl=qsl:
                                 e.tensor_mul(y_t[2][0:32, qsl], pb[7][0:32, :], Rt[0:32, :]),
                                 inc="vec")
            hold['ytile'][qt] = hold['yev']
            yield

        for qt in range(NT):
            exp_prev = ctx.val("act")
            g_sty = gen_sty(qt, exp_prev)
            steps = 0
            for _ in g_sty:
                steps += 1
                if steps == 1 and hold['tail'] is not None:
                    next(hold['tail'])
                elif steps == 3 and hold['tail'] is not None:
                    for _ in hold['tail']:
                        pass
                    hold['tail'] = None
            hold['tail'] = gen_tail(qt)
        memset_c = 0
        if h == 0:
            memset_c = ctx.op("vector", lambda e: e.memset(y_t[2][32:33, :], 1.0), inc="vec")
        ctx.wait("tensor", "wdma", c_ow[h])
        if memset_c:
            ctx.wait("tensor", "vec", memset_c)
        for _ in hold['tail']:
            pass
        hold['tail'] = None
        ev = {}
        for idx, (nt, cc) in enumerate([(nt_, cc_) for nt_ in range(NT) for cc_ in range(NFC)]):
            if ((h == 14 and idx == 20) or (h == 15 and idx >= 8)) and st.gate0_gen is not None:
                try:
                    next(st.gate0_gen)
                except StopIteration:
                    st.gate0_gen = None
            sl = slice(512 * nt, 512 * (nt + 1))
            bank = pb[idx % 2]
            if cc == 0:
                ctx.wait("tensor", "vec", hold['ytile'][nt])
            if idx >= 2:
                ctx.wait("tensor", "vec", ev[idx - 2])
            _mm(ctx, bank[:, :], owbuf[hb][:, cc * 128:cc * 128 + 128], y_t[0][:, sl], True, False)
            _mm(ctx, bank[:, :], owbuf[hb][:, 1024 + cc * 128:1024 + cc * 128 + 128],
                y_t[1][:, sl], False, False)
            pe = _mm(ctx, bank[:, :], owbuf[hb][:, 2048 + cc * 128:2048 + cc * 128 + 128],
                     y_t[2][:, sl], False, True, inc="pe")
            ctx.wait("vector", "pe", pe)
            ev[idx] = ctx.op("vector", lambda e, cc=cc, sl=sl, bank=bank:
                             e.tensor_add(st.x[cc][:, sl], bank[:, :], st.x[cc][:, sl]), inc="vec")
        if h + 2 < 16:
            ctx.wait("gpsimd", "pe", ctx.val("pe"))
            c_ow[h + 2] = ctx.dma("gpsimd", owbuf[hb][:, :], dram["outw"][h + 2], inc="wdma")
    ctx.wait("tensor", "vec", ctx.val("vec"))


def _gen_ffn0_gating(ctx, st, dram):
    AR, ARF, pb = st.AR, st.ARF, st.pb
    g0bf = AR[:, 33424:33496]
    g0b_bf = AR[:, 33496:33512]
    cwbf = AR[:, 33280:33424]
    sgF = ARF[:, 2048:2064]
    gm1 = ARF[:, 2064:2065]; gm2 = ARF[:, 2065:2066]
    gden = ARF[:, 2066:2067]
    glt = ARF[:, 2067:2076]; gt2 = ARF[:, 2076:2085]; gge = ARF[:, 2085:2094]
    gcw = ARF[:, 2094:2103]
    ctx.wait("gpsimd", "vec", ctx.val("vec"))
    c_g0 = ctx.dma("gpsimd", g0bf[:, :], dram["g0w"][:, :], inc="wdma")
    c_g0b = ctx.dma("gpsimd", g0b_bf[0:1, 0:9], dram["g0b"][:, :], inc="wdma")
    yield
    ctx.wait("tensor", "wdma", c_g0b)
    ctx.wait("tensor", "vec", ctx.val("vec"))
    sig_c = 0
    cw_c = 0
    for j in range(16):
        tsl = slice(128 * j, 128 * (j + 1))
        if j > 0:
            ctx.wait("tensor", "act", sig_c)
        for fc in range(NFC):
            _mm(ctx, pb[6][:, 0:9], st.xn[fc][:, tsl], g0bf[:, 9 * fc:9 * fc + 9],
                start=(fc == 0), stop=False)
        pe = _mm(ctx, pb[6][:, 0:9], st.ones_bf[0:1, 0:128], g0b_bf[0:1, 0:9],
                 start=False, stop=True, inc="pe")
        ctx.wait("scalar", "pe", pe)
        if j > 0:
            ctx.wait("scalar", "vec", cw_c)
        sig_c = ctx.op("scalar", lambda e: e.activation(sgF[:, 0:9], pb[6][:, 0:9], AF.Sigmoid),
                       inc="act")
        ctx.wait("vector", "act", sig_c)

        def vf(builder):
            c = ctx.op("vector", builder, inc="vec")
            ctx.wait("vector", "vec", c)
            return c
        vf(lambda e: e.reduce_max(gm1[:, :], sgF[:, 0:9], axis=AX.X))
        vf(lambda e: e.tensor_scalar(glt[:, :], sgF[:, 0:9], gm1[:, 0:1], None, ALU.is_lt))
        vf(lambda e: e.tensor_mul(gt2[:, :], sgF[:, 0:9], glt[:, :]))
        vf(lambda e: e.reduce_max(gm2[:, :], gt2[:, :], axis=AX.X))
        vf(lambda e: e.tensor_add(gden[:, :], gm1[:, :], gm2[:, :]))
        vf(lambda e: e.tensor_scalar(gden[:, :], gden[:, :], 1e-20, None, ALU.add))
        vf(lambda e: e.reciprocal(gden[:, :], gden[:, :]))
        vf(lambda e: e.tensor_scalar(gge[:, :], sgF[:, 0:9], gm2[:, 0:1], None, ALU.is_ge))
        vf(lambda e: e.scalar_tensor_tensor(gcw[:, :], sgF[:, 0:9], gden[:, 0:1],
                                            gge[:, :], ALU.mult, ALU.mult))
        cw_c = ctx.op("vector", lambda e, j=j: e.tensor_copy(cwbf[:, 9 * j:9 * j + 9], gcw[:, :]),
                      inc="vec")
        st.gate0_cwc = cw_c
        yield


def _emit_ffn0(ctx, st, dram):
    AR, ARF, pb = st.AR, st.ARF, st.pb
    ACT = [AR[:, 1024 * hc:1024 * (hc + 1)] for hc in range(22)]
    w1buf = [AR[:, 22528:23552], AR[:, 23552:24576]]
    w3buf = [AR[:, 24576:25600], AR[:, 25600:26624]]
    w2buf = [AR[:, 26624:29440], AR[:, 29440:32256]]
    sg = [AR[:, 32256:32768], AR[:, 32768:33280]]
    cwbf = AR[:, 33280:33424]
    g0bf = AR[:, 33424:33496]
    g0b_bf = AR[:, 33496:33512]
    cwrowB = AR[:, 33512:34024]
    CWBs = ARF[:, 0:1024]
    cwrowF = ARF[:, 1024:1536]
    dtmp = ARF[:, 1536:2048]
    sgF = ARF[:, 2048:2064]
    gm1 = ARF[:, 2064:2065]; gm2 = ARF[:, 2065:2066]
    gden = ARF[:, 2066:2067]
    glt = ARF[:, 2067:2076]; gt2 = ARF[:, 2076:2085]; gge = ARF[:, 2085:2094]
    gcw = ARF[:, 2094:2103]

    g = getattr(st, "gate0_gen", None)
    if g is not None:
        for _ in g:
            pass
        st.gate0_gen = None
    cw_c = st.gate0_cwc
    ctx.wait("gpsimd", "pe", ctx.val("pe"))
    ctx.wait("gpsimd", "vec", ctx.val("vec"))

    c_w1 = {}; c_w3 = {}; c_w2 = {}
    w1_war = [0, 0]; w3_war = [0, 0]; w2_war = [0, 0]
    cwb_war = 0
    for ttp in range(2):
        for e in range(9):
            for hc0 in (0, 1):
                ctx.wait("gpsimd", "pe", w1_war[hc0])
                c_w1[hc0] = ctx.dma("gpsimd", w1buf[hc0][:, :], dram["w1s"][e, hc0], inc="wdma")
                ctx.wait("gpsimd", "pe", w3_war[hc0])
                c_w3[hc0] = ctx.dma("gpsimd", w3buf[hc0][:, :], dram["w3s"][e, hc0], inc="wdma")
            ctx.wait("tensor", "vec", cw_c if (ttp == 0 and e == 0) else ctx.val("vec"))
            pe = 0
            for half in range(2):
                for jj in range(4):
                    j = ttp * 8 + half * 4 + jj
                    pe = _mm(ctx, pb[6][32 * half:32 * half + 1, 128 * jj:128 * jj + 128],
                             cwbf[:, 9 * j + e:9 * j + e + 1], st.ident[:, 0:128],
                             start=True, stop=True,
                             inc="pe" if (half == 1 and jj == 3) else None)
            ctx.wait("vector", "pe", pe)
            rows_c = [0, 0]
            for half in range(2):
                rows_c[half] = ctx.op("vector", lambda e_, half=half:
                                      e_.tensor_copy(cwrowB[32 * half:32 * half + 1, 0:512],
                                                     pb[6][32 * half:32 * half + 1, 0:512]),
                                      inc="vec")
            cpy = [0, 0]
            for half in range(2):
                bank = pb[6 + half]
                ctx.wait("tensor", "vec", rows_c[1] if half == 0 else rows_c[half])
                pe = _mm(ctx, bank[:, :], st.ones_bf[32 * half:32 * half + 1, 0:128],
                         cwrowB[32 * half:32 * half + 1, 0:512], True, True, inc="pe")
                ctx.wait("vector", "pe", pe)
                cpy[half] = ctx.op("vector", lambda e_, half=half, bank=bank:
                                   e_.tensor_copy(CWBs[:, 512 * half:512 * half + 512],
                                                  bank[:, :]), inc="vec")
            silu_c = {}
            act_c = {}
            for hc in range(22):
                par = hc % 2
                ctx.wait("tensor", "wdma", c_w3[hc])
                for half in range(2):
                    i2 = hc * 2 + half
                    spar = i2 % 2
                    tsl = slice(1024 * ttp + 512 * half, 1024 * ttp + 512 * half + 512)
                    bg = pb[i2 % 2]
                    bu = pb[2 + i2 % 2]
                    if i2 >= 2:
                        ctx.wait("tensor", "act", silu_c[i2 - 2])
                        ctx.wait("tensor", "vec", act_c[i2 - 2])
                    peg = peu = None
                    for fc in range(NFC):
                        peg = _mm(ctx, bg[:, :], w1buf[par][:, 128 * fc:128 * fc + 128],
                                  st.xn[fc][:, tsl], start=(fc == 0), stop=(fc == NFC - 1),
                                  inc="pe" if fc == NFC - 1 else None)
                    for fc in range(NFC):
                        peu = _mm(ctx, bu[:, :], w3buf[par][:, 128 * fc:128 * fc + 128],
                                  st.xn[fc][:, tsl], start=(fc == 0), stop=(fc == NFC - 1),
                                  inc="pe" if fc == NFC - 1 else None)
                    ctx.wait("scalar", "pe", peg)
                    if i2 >= 2:
                        ctx.wait("scalar", "vec", act_c[i2 - 2])
                    silu_c[i2] = ctx.op(
                        "scalar", lambda e_, spar=spar, bg=bg:
                        e_.activation(sg[spar][:, :], bg[:, :], AF.Silu), inc="act")
                    ctx.wait("vector", "pe", peu)
                    ctx.wait("vector", "act", silu_c[i2])
                    act_c[i2] = ctx.op(
                        "vector", lambda e_, hc=hc, half=half, spar=spar, bu=bu:
                        e_.tensor_mul(ACT[hc][:, 512 * half:512 * half + 512], bu[:, :],
                                      sg[spar][:, :]), inc="vec")
                w1_war[par] = ctx.val("pe")
                w3_war[par] = ctx.val("pe")
                if hc + 2 < 22:
                    ctx.wait("gpsimd", "pe", ctx.val("pe"))
                    c_w1[hc + 2] = ctx.dma("gpsimd", w1buf[par][:, :], dram["w1s"][e, hc + 2],
                                           inc="wdma")
                    c_w3[hc + 2] = ctx.dma("gpsimd", w3buf[par][:, :], dram["w3s"][e, hc + 2],
                                           inc="wdma")
            for dc0 in (0, 1):
                ctx.wait("gpsimd", "pe", w2_war[dc0])
                c_w2[dc0] = ctx.dma("gpsimd", w2buf[dc0][:, :], dram["w2s"][e, dc0], inc="wdma")
            ev = {}
            ctx.wait("tensor", "vec", act_c[43])
            for dc in range(8):
                par = dc % 2
                ctx.wait("tensor", "wdma", c_w2[dc])
                for half in range(2):
                    tsl = slice(1024 * ttp + 512 * half, 1024 * ttp + 512 * half + 512)
                    bank = pb[4 + (dc * 2 + half) % 2]
                    if dc * 2 + half >= 2:
                        ctx.wait("tensor", "vec", ev[dc * 2 + half - 2])
                    for hc in range(22):
                        pe = _mm(ctx, bank[:, :], w2buf[par][:, 128 * hc:128 * hc + 128],
                                 ACT[hc][:, 512 * half:512 * half + 512],
                                 start=(hc == 0), stop=(hc == 21),
                                 inc="pe" if hc == 21 else None)
                    ctx.wait("vector", "pe", pe)
                    ctx.op("vector", lambda e_, bank=bank, half=half:
                           e_.tensor_mul(dtmp[:, :], bank[:, :], CWBs[:, 512 * half:512 * half + 512]))
                    ev[dc * 2 + half] = ctx.op(
                        "vector", lambda e_, dc=dc, tsl=tsl:
                        e_.tensor_add(st.x[dc][:, tsl], dtmp[:, :], st.x[dc][:, tsl]), inc="vec")
                w2_war[par] = ctx.val("pe")
                if dc + 2 < 8:
                    ctx.wait("gpsimd", "pe", ctx.val("pe"))
                    c_w2[dc + 2] = ctx.dma("gpsimd", w2buf[par][:, :], dram["w2s"][e, dc + 2],
                                           inc="wdma")
            ctx.wait("tensor", "vec", ctx.val("vec"))
    ctx.wait("tensor", "vec", ctx.val("vec"))


def _emit_attn1(ctx, st, dram):
    AR, ARF, pb = st.AR, st.ARF, st.pb
    PT = [AR[:, 512 * kc:512 * (kc + 1)] for kc in range(16)]
    Z1 = [AR[:, 8192 + 2048 * c:8192 + 2048 * (c + 1)] for c in range(8)]
    zk = [AR[:, 24576:26624], AR[:, 26624:28672]]
    zv1 = AR[:, 28672:30720]
    qraw = AR[0:64, 30720:32768]
    qfull = AR[:, 30720:32768]
    zv2 = AR[64:128, 30720:32768]
    kraw = AR[0:64, 32768:34816]
    kfull = AR[:, 32768:34816]
    qp = AR[:, 34816:36864]
    kp = AR[:, 36864:38912]
    Tc = AR[:, 38912:40960]
    Ts = AR[:, 40960:43008]
    VT = [AR[:, 43008 + 128 * kc:43008 + 128 * (kc + 1)] for kc in range(8)] + \
         [AR[:, 46848 + 128 * kc:46848 + 128 * (kc + 1)] for kc in range(8)]
    wqb = [AR[:, 44032:44544], AR[:, 44544:45056]]
    vdb = [AR[:, 45056:45184], AR[:, 45184:45312]]
    bdr = AR[:, 45312:45824]; bdi = AR[:, 45824:46336]; nbdi = AR[:, 46336:46848]
    permb = AR[:, 47872:48000]
    recB1 = AR[:, 48000:48512]
    Rt = ARF[:, 0:512]
    recF = ARF[:, 512:1024]
    tmpBf = ARF[:, 1536:2048]

    ctx.wait("gpsimd", "pe", ctx.val("pe"))
    ctx.wait("gpsimd", "vec", ctx.val("vec"))
    c_tc0 = ctx.dma("gpsimd", Tc[0:64, :], dram["tcq"][:, :], inc="wdma")
    c_ts0 = ctx.dma("gpsimd", Ts[0:64, :], dram["tsq"][:, :], inc="wdma")
    c_bdr = ctx.dma("gpsimd", bdr[:, :], dram["bdr"][:, :], inc="wdma")
    c_bdi = ctx.dma("gpsimd", bdi[:, :], dram["bdi"][:, :], inc="wdma")
    wkb = AR[:, 0:2048]
    wvb = AR[:, 2048:3584]
    c_wk = ctx.dma("gpsimd", wkb[:, :], dram["wk"][:, :], inc="wdma")
    c_wv = ctx.dma("gpsimd", wvb[:, :], dram["wv"][:, :], inc="wdma")
    c_pm = ctx.dma("gpsimd", permb[:, :], dram["c_perm"][:, :], inc="wdma")
    ctx.wait("vector", "wdma", c_bdi)
    nb_c = ctx.op("vector", lambda e: e.tensor_scalar_mul(nbdi[:, :], bdi[:, :], -1.0), inc="vec")
    ctx.op("vector", lambda e: e.memset(qp[64:128, :], 0.0))
    ctx.op("vector", lambda e: e.memset(kp[64:128, :], 0.0))
    ctx.op("vector", lambda e: e.memset(kfull[64:128, :], 0.0))
    for kc in range(16):
        ctx.op("vector", lambda e, kc=kc: e.memset(VT[kc][:, 64:65], 1.0))
        ctx.op("vector", lambda e, kc=kc: e.memset(VT[kc][:, 65:128], 0.0),
               inc="vec" if kc == 15 else None)

    ctx.wait("tensor", "wdma", c_wv)
    ctx.wait("tensor", "vec", ctx.val("vec"))
    ev = 0
    outspec = [(zk[0], wkb, 0, 0, 128), (zk[1], wkb, 128, 0, 128),
               (zv1, wvb, 0, 0, 128), (zv2, wvb, 128, 64, 64)]
    for ci, (dst, wsrc, coff, roff, rows) in enumerate(outspec):
        for nt in range(NT):
            sl = slice(512 * nt, 512 * (nt + 1))
            bank = pb[(ci * NT + nt) % 2]
            if ci * NT + nt >= 2:
                ctx.wait("tensor", "act", ev - 1)
            for fc in range(NFC):
                stride = 256 if wsrc is wkb else 192
                lh = wsrc[:, stride * fc + coff:stride * fc + coff + rows]
                pe = _mm(ctx, bank[roff:roff + rows, :], lh, st.xn[fc][:, sl],
                         start=(fc == 0), stop=(fc == NFC - 1),
                         inc="pe" if fc == NFC - 1 else None)
            ctx.wait("scalar", "pe", pe)
            ev = ctx.op("scalar", lambda e, dst=dst, sl=sl, bank=bank, roff=roff, rows=rows:
                        e.copy(dst[roff:roff + rows, sl] if roff == 0 else dst[:, sl],
                               bank[roff:roff + rows, :]), inc="act")
    ctx.wait("tensor", "act", ev)

    c_wq = {}
    c_vd = {}
    ctx.wait("gpsimd", "pe", ctx.val("pe"))
    c_wq[0] = ctx.dma("gpsimd", wqb[0][:, :], dram["wq"][0], inc="wdma")
    c_vd[0] = ctx.dma("gpsimd", vdb[0][:, :], dram["vd"][0], inc="wdma")
    hold = {'yevb': [ctx.val("vec")] * 2, 'tail': None, 'pe_last': 0, 'vev': 0}
    for h in range(16):
        hpar = h % 2
        zc = h // 2
        par = h % 2
        ctx.wait("tensor", "wdma", c_wq[h])
        if hold['vev']:
            ctx.wait("tensor", "act", hold['vev'])
        qel = [0] * NT
        for nt in range(NT):
            sl = slice(512 * nt, 512 * (nt + 1))
            bank = pb[nt % 2]
            if nt >= 2:
                ctx.wait("tensor", "act", qel[nt - 2])
            for fc in range(NFC):
                pe = _mm(ctx, bank[0:64, :], wqb[par][:, 64 * fc:64 * fc + 64], st.xn[fc][:, sl],
                         start=(fc == 0), stop=(fc == NFC - 1),
                         inc="pe" if fc == NFC - 1 else None)
            ctx.wait("scalar", "pe", pe)
            qel[nt] = ctx.op("scalar", lambda e, sl=sl, bank=bank:
                             e.copy(qraw[:, sl], bank[0:64, :]), inc="act")
        ctx.wait("tensor", "act", qel[NT - 1])
        ctx.wait("tensor", "vec", ctx.val("vec"))
        if h == 0:
            ctx.wait("tensor", "wdma", c_pm)
        qperm = [0] * NT
        for nt in range(NT):
            sl = slice(512 * nt, 512 * (nt + 1))
            qperm[nt] = _mm(ctx, pb[2 + nt][:, :], permb[:, :], qfull[:, sl],
                            True, True, inc="pe")
        rd = [0] * NT
        for nt in range(NT):
            sl = slice(512 * nt, 512 * (nt + 1))
            ctx.wait("vector", "pe", qperm[nt])
            ctx.op("vector", lambda e, sl=sl: e.tensor_mul(qp[0:64, sl], qraw[:, sl], Tc[0:64, sl]))
            ctx.op("vector", lambda e, sl=sl, bank=pb[2 + nt]:
                   e.tensor_mul(tmpBf[0:64, :], bank[0:64, :], Ts[0:64, sl]))
            rd[nt] = ctx.op("vector", lambda e, sl=sl:
                            e.tensor_add(qp[0:64, sl], qp[0:64, sl], tmpBf[0:64, :]), inc="vec")
        if h == 0:
            ctx.wait("tensor", "wdma", c_bdi)
            ctx.wait("tensor", "vec", nb_c)
        kel = [0] * NT
        for nt in range(NT):
            sl = slice(512 * nt, 512 * (nt + 1))
            bank = pb[nt % 2]
            if nt >= 2:
                ctx.wait("tensor", "act", kel[nt - 2])
            _mm(ctx, bank[0:32, :], bdr[:, 32 * h:32 * h + 32], zk[0][:, sl], True, False)
            _mm(ctx, bank[0:32, :], nbdi[:, 32 * h:32 * h + 32], zk[1][:, sl], False, True)
            _mm(ctx, bank[32:64, :], bdi[:, 32 * h:32 * h + 32], zk[0][:, sl], True, False)
            pe = _mm(ctx, bank[32:64, :], bdr[:, 32 * h:32 * h + 32], zk[1][:, sl], False, True,
                     inc="pe")
            ctx.wait("scalar", "pe", pe)
            kel[nt] = ctx.op("scalar", lambda e, sl=sl, bank=bank:
                             e.copy(kraw[:, sl], bank[0:64, :]), inc="act")
        ctx.wait("tensor", "wdma", c_vd[h])
        ctx.wait("tensor", "act", kel[NT - 1])
        v_ev = [0, 0]
        for halfv in range(2):
            bank = pb[halfv]
            for tci in range(8):
                tc = halfv * 8 + tci
                _mm(ctx, bank[:, 64 * tci:64 * tci + 64], zv1[:, 128 * tc:128 * tc + 128],
                    vdb[par][:, 0:64], start=True, stop=False)
                pe = _mm(ctx, bank[:, 64 * tci:64 * tci + 64], zv2[:, 128 * tc:128 * tc + 128],
                         vdb[par][64:128, 64:128], start=False, stop=True,
                         inc="pe" if tci == 7 else None)
            ctx.wait("scalar", "pe", pe)
            for tci in range(8):
                v_ev[halfv] = ctx.op(
                    "scalar", lambda e, halfv=halfv, bank=bank, tci=tci:
                    e.copy(VT[8 * halfv + tci][:, 0:64], bank[:, 64 * tci:64 * tci + 64]),
                    inc="act")
        hold['vev'] = v_ev[1]
        if h + 1 < 16:
            ctx.wait("gpsimd", "pe", ctx.val("pe"))
            c_wq[h + 1] = ctx.dma("gpsimd", wqb[1 - par][:, :], dram["wq"][h + 1], inc="wdma")
            c_vd[h + 1] = ctx.dma("gpsimd", vdb[1 - par][:, :], dram["vd"][h + 1], inc="wdma")
        kperm = [0] * NT
        for nt in range(NT):
            sl = slice(512 * nt, 512 * (nt + 1))
            ctx.wait("tensor", "vec", rd[nt])
            kperm[nt] = _mm(ctx, pb[2 + nt][:, :], permb[:, :], kfull[:, sl],
                            True, True, inc="pe")
        rdk = [0] * NT
        for nt in range(NT):
            sl = slice(512 * nt, 512 * (nt + 1))
            ctx.wait("vector", "pe", kperm[nt])
            ctx.op("vector", lambda e, sl=sl: e.scalar_tensor_tensor(
                kp[0:64, sl], kraw[:, sl], 8.0, Tc[0:64, sl], ALU.mult, ALU.mult))
            ctx.op("vector", lambda e, sl=sl, bank=pb[2 + nt]: e.scalar_tensor_tensor(
                tmpBf[0:64, :], bank[0:64, :], 8.0, Ts[0:64, sl], ALU.mult, ALU.mult))
            rdk[nt] = ctx.op("vector", lambda e, sl=sl:
                             e.tensor_add(kp[0:64, sl], kp[0:64, sl], tmpBf[0:64, :]), inc="vec")

        def gen_sty1(qt, exp_prev):
            Q0 = 512 * qt
            Kc = Q0 // 128 + 4
            qsl = slice(Q0, Q0 + 512)
            c_exp = [0] * Kc
            c_fix = [0] * Kc
            yb = pb[5 + qt % 2]

            def emit_st1(kc):
                pbS = (pb[3], pb[4], pb[7])[kc % 3]
                ctx.wait("tensor", "act", c_exp[kc - 3] if kc >= 3 else exp_prev)
                if kc == 0:
                    ctx.wait("tensor", "vec", rdk[qt])
                if qt == 0 and kc < 2:
                    ctx.wait("tensor", "vec", rdk[kc + 1])
                pe = _mm(ctx, pbS[:, :], kp[:, 128 * kc:128 * kc + 128], qp[:, qsl],
                         True, True, inc="pe")
                ctx.wait("scalar", "pe", pe)
                c_exp[kc] = ctx.op("scalar", lambda e, kc=kc, pbS=pbS:
                                   e.activation(PT[kc][:, 0:512], pbS[:, :], AF.Exp), inc="act")
                kr = kc - Q0 // 128
                if kr >= 0:
                    ctx.wait("vector", "act", c_exp[kc])
                    for jj in range(kr):
                        ctx.op("vector", lambda e, kc=kc, jj=jj:
                               e.memset(PT[kc][:, 128 * jj:128 * jj + 128], 0.0))
                    c_fix[kc] = ctx.op("vector", lambda e, kc=kc, kr=kr:
                                       e.tensor_mul(PT[kc][:, 128 * kr:128 * kr + 128],
                                                    PT[kc][:, 128 * kr:128 * kr + 128],
                                                    st.tri[:, :]), inc="vec")

            def emit_y1(kc):
                ctx.wait("tensor", "vec" if c_fix[kc] else "act",
                         c_fix[kc] if c_fix[kc] else c_exp[kc])
                if kc == 0:
                    ctx.wait("tensor", "vec", hold['yevb'][qt % 2])
                    if qt == 0:
                        ctx.wait("tensor", "vec", rdk[3])
                st_ = (kc == 0)
                sp_ = (kc == Kc - 1)
                return _mm(ctx, yb[:, :], VT[kc][:, :], PT[kc][:, 0:512],
                           st_, sp_, inc="pe" if sp_ else None)

            for kc in range(Kc):
                emit_st1(kc)
                yield
            for kc in range(Kc - 1):
                emit_y1(kc)
                yield
            hold['pe_last'] = emit_y1(Kc - 1)

        def gen_tail1(qt, hpar, zc, zoff):
            qsl = slice(512 * qt, 512 * qt + 512)
            yb = pb[5 + qt % 2]
            ctx.wait("vector", "pe", hold['pe_last'])
            vc = ctx.op("vector", lambda e, yb=yb:
                        e.tensor_copy(recB1[64:65, :], yb[64:65, :]), inc="vec")
            yield
            ctx.wait("tensor", "vec", vc)
            pe = _mm(ctx, pb[2][:, :], st.ones_bf[64:65, 0:128], recB1[64:65, :],
                     True, True, inc="pe")
            ctx.wait("vector", "pe", pe)
            ctx.op("vector", lambda e: e.reciprocal(Rt[0:64, :], pb[2][0:64, :]))
            hold['yevb'][qt % 2] = ctx.op(
                "vector", lambda e, yb=yb, zc=zc, zoff=zoff, qsl=qsl:
                e.tensor_mul(Z1[zc][zoff:zoff + 64, qsl], yb[0:64, :],
                             Rt[0:64, :]), inc="vec")
            yield

        for qt in range(NT):
            exp_prev = ctx.val("act")
            g_sty = gen_sty1(qt, exp_prev)
            steps = 0
            for _ in g_sty:
                steps += 1
                if steps == 1 and hold['tail'] is not None:
                    next(hold['tail'])
                elif steps == 3 and hold['tail'] is not None:
                    for _ in hold['tail']:
                        pass
                    hold['tail'] = None
            hold['tail'] = gen_tail1(qt, hpar, zc, 64 * hpar)
        for _ in hold['tail']:
            pass
        hold['tail'] = None

    st.gate1_gen = _gen_ffn1_gating(ctx, st, dram)
    st.gate1_cwc = 0
    next(st.gate1_gen)
    ctx.wait("tensor", "vec", ctx.val("vec"))
    ctx.wait("tensor", "act", ctx.val("act"))
    wpbs = [AR[:, 0:1024], AR[:, 1024:2048]]
    c_wp = [0] * NFC
    ctx.wait("gpsimd", "pe", ctx.val("pe"))
    ctx.wait("gpsimd", "act", ctx.val("act"))
    c_wp[0] = ctx.dma("gpsimd", wpbs[0][:, :], dram["wproj"][0], inc="wdma")
    c_wp[1] = ctx.dma("gpsimd", wpbs[1][:, :], dram["wproj"][1], inc="wdma")
    wp_war = [0, 0]
    for cc in range(NFC):
        par = cc % 2
        ctx.wait("tensor", "wdma", c_wp[cc])
        ev = 0
        for nt in range(NT):
            if cc * NT + nt >= 2 and st.gate1_gen is not None:
                try:
                    next(st.gate1_gen)
                except StopIteration:
                    st.gate1_gen = None
            sl = slice(512 * nt, 512 * (nt + 1))
            bank = pb[4 + nt % 2]
            if nt >= 2:
                ctx.wait("tensor", "vec", ev - 1)
            for dc in range(NFC):
                pe = _mm(ctx, bank[:, :], wpbs[par][:, 128 * dc:128 * dc + 128], Z1[dc][:, sl],
                         start=(dc == 0), stop=(dc == NFC - 1),
                         inc="pe" if dc == NFC - 1 else None)
            ctx.wait("vector", "pe", pe)
            ev = ctx.op("vector", lambda e, cc=cc, sl=sl, bank=bank:
                        e.tensor_add(st.x[cc][:, sl], bank[:, :], st.x[cc][:, sl]), inc="vec")
        wp_war[par] = ctx.val("pe")
        if cc + 2 < NFC:
            ctx.wait("gpsimd", "pe", wp_war[par])
            c_wp[cc + 2] = ctx.dma("gpsimd", wpbs[par][:, :], dram["wproj"][cc + 2], inc="wdma")
        ctx.wait("tensor", "vec", ev)
    ctx.wait("tensor", "vec", ctx.val("vec"))


def _gen_ffn1_gating(ctx, st, dram):
    AR, ARF, pb = st.AR, st.ARF, st.pb
    g1bf = AR[:, 41728:41856]
    cwbf1 = AR[:, 41472:41728]
    pexp = ARF[:, 2560:2576]
    nmF = ARF[:, 2576:2577]; m1F = ARF[:, 2577:2578]; m2F = ARF[:, 2578:2579]
    Ssum = ARF[:, 2579:2580]; rec1 = ARF[:, 2580:2581]
    ilt = ARF[:, 2581:2597]; it2 = ARF[:, 2597:2613]; ige = ARF[:, 2613:2629]
    cwF = ARF[:, 2629:2645]
    ctx.wait("gpsimd", "vec", ctx.val("vec"))
    c_g1 = ctx.dma("gpsimd", g1bf[:, :], dram["g1w"][:, :], inc="wdma")
    yield
    ctx.wait("tensor", "wdma", c_g1)
    ctx.wait("tensor", "vec", ctx.val("vec"))
    sig_c = 0
    vread_c = 0
    cw_c = 0
    for j in range(16):
        tsl = slice(128 * j, 128 * (j + 1))
        if j > 0:
            ctx.wait("tensor", "act", sig_c)
            ctx.wait("tensor", "vec", vread_c)
        for fc in range(NFC):
            pe = _mm(ctx, pb[6][:, 0:16], st.xn[fc][:, tsl], g1bf[:, 16 * fc:16 * fc + 16],
                     start=(fc == 0), stop=(fc == NFC - 1),
                     inc="pe" if fc == NFC - 1 else None)
        ctx.wait("vector", "pe", pe)
        if j > 0:
            ctx.wait("vector", "act", sig_c)

        def vf(builder):
            c = ctx.op("vector", builder, inc="vec")
            ctx.wait("vector", "vec", c)
            return c
        vf(lambda e: e.reduce_max(nmF[:, :], pb[6][:, 0:16], axis=AX.X, negate=True))
        vf(lambda e: e.reduce_max(m1F[:, :], pb[6][:, 0:16], axis=AX.X))
        vc = ctx.val("vec")
        ctx.wait("scalar", "vec", vc)
        sig_c = ctx.op("scalar", lambda e:
                       e.activation(pexp[:, :], pb[6][:, 0:16], AF.Exp, bias=nmF[:, 0:1],
                                    accum_out=Ssum[:, 0:1]), inc="act")
        ctx.wait("vector", "act", sig_c)
        vf(lambda e: e.reciprocal(rec1[:, :], Ssum[:, :]))
        vf(lambda e: e.tensor_scalar(ilt[:, :], pb[6][:, 0:16], m1F[:, 0:1], None, ALU.is_lt))
        vf(lambda e: e.tensor_mul(it2[:, :], pb[6][:, 0:16], ilt[:, :]))
        vf(lambda e: e.reduce_max(m2F[:, :], it2[:, :], axis=AX.X))
        vf(lambda e: e.tensor_scalar(ige[:, :], pb[6][:, 0:16], m2F[:, 0:1], None, ALU.is_ge))
        vf(lambda e: e.scalar_tensor_tensor(cwF[:, :], pexp[:, :], rec1[:, 0:1], ige[:, :],
                                            ALU.mult, ALU.mult))
        vread_c = cw_c = ctx.op("vector", lambda e, j=j:
                                e.tensor_copy(cwbf1[:, 16 * j:16 * j + 16], cwF[:, :]), inc="vec")
        st.gate1_cwc = cw_c
        yield


def _emit_ffn1(ctx, st, dram):
    AR, ARF, pb = st.AR, st.ARF, st.pb
    ACT1 = [[AR[:, 8192 * ei + 2048 * ic:8192 * ei + 2048 * (ic + 1)] for ic in range(4)]
            for ei in range(2)]
    wgbuf = [AR[:, 16384:20480], AR[:, 20480:24576]]
    wubuf = [AR[:, 24576:28672], AR[:, 28672:32768]]
    wdbuf = [AR[:, 32768:36864], AR[:, 36864:40960]]
    uprime = AR[:, 40960:41472]
    cwbf1 = AR[:, 41472:41728]
    g1bf = AR[:, 41728:41856]
    sg = [AR[:, 41856:42368], AR[:, 42368:42880]]
    cwrowB1 = AR[:, 42880:43392]
    sact = [AR[:, 2048 * ic:2048 * (ic + 1)] for ic in range(8)]
    wsgb = [AR[:, 16384:17408], AR[:, 17408:18432]]
    wsub = [AR[:, 18432:19456], AR[:, 19456:20480]]
    wsdb = [AR[:, 20480:21504], AR[:, 21504:22528]]
    CWBe = ARF[:, 0:2048]
    cwrowF = ARF[:, 2048:2560]
    pexp = ARF[:, 2560:2576]
    nmF = ARF[:, 2576:2577]; m1F = ARF[:, 2577:2578]; m2F = ARF[:, 2578:2579]
    Ssum = ARF[:, 2579:2580]; rec1 = ARF[:, 2580:2581]
    ilt = ARF[:, 2581:2597]; it2 = ARF[:, 2597:2613]; ige = ARF[:, 2613:2629]
    cwF = ARF[:, 2629:2645]

    g = getattr(st, "gate1_gen", None)
    if g is not None:
        for _ in g:
            pass
        st.gate1_gen = None
    cw_c = st.gate1_cwc
    ctx.wait("gpsimd", "pe", ctx.val("pe"))
    ctx.wait("gpsimd", "vec", ctx.val("vec"))

    wg_war = [0, 0]; wu_war = [0, 0]; wd_war = [0, 0]
    c_wg = [0, 0]; c_wu = [0, 0]; c_wd = [0, 0]
    for ep in range(8):
        acts_done = 0
        for ei in range(2):
            e_ = 2 * ep + ei
            par = ei
            ctx.wait("gpsimd", "pe", wg_war[par])
            c_wg[par] = ctx.dma("gpsimd", wgbuf[par][:, :], dram["wgs"][e_], inc="wdma")
            ctx.wait("gpsimd", "pe", wu_war[par])
            c_wu[par] = ctx.dma("gpsimd", wubuf[par][:, :], dram["wus"][e_], inc="wdma")
            ctx.wait("gpsimd", "pe", wd_war[par])
            c_wd[par] = ctx.dma("gpsimd", wdbuf[par][:, :], dram["wds"][e_], inc="wdma")
            ctx.wait("tensor", "vec", cw_c if (ep == 0 and ei == 0) else ctx.val("vec"))
            mb = [(6, 0), (6, 32), (7, 0), (7, 32)]
            cr = [0, 32, 64, 0]
            pe = 0
            for nt in range(NT):
                bk, rw = mb[nt]
                for jj in range(4):
                    j = nt * 4 + jj
                    pe = _mm(ctx, pb[bk][rw:rw + 1, 128 * jj:128 * jj + 128],
                             cwbf1[:, 16 * j + e_:16 * j + e_ + 1], st.ident[:, 0:128],
                             start=True, stop=True,
                             inc="pe" if (nt == NT - 1 and jj == 3) else None)
            ctx.wait("vector", "pe", pe)
            rows_c = [0] * NT
            for nt in range(3):
                bk, rw = mb[nt]
                rows_c[nt] = ctx.op("vector", lambda e, nt=nt, bk=bk, rw=rw:
                                    e.tensor_copy(cwrowB1[cr[nt]:cr[nt] + 1, 0:512],
                                                  pb[bk][rw:rw + 1, 0:512]), inc="vec")
            cpy = [0] * NT
            for nt in range(NT):
                bank = pb[6 + nt % 2]
                if nt == 0:
                    ctx.wait("tensor", "vec", rows_c[1])
                elif nt == 1:
                    ctx.wait("tensor", "vec", rows_c[3])
                else:
                    ctx.wait("tensor", "vec", max(rows_c[nt], cpy[nt - 2]))
                pe = _mm(ctx, bank[:, :], st.ones_bf[cr[nt]:cr[nt] + 1, 0:128],
                         cwrowB1[cr[nt]:cr[nt] + 1, 0:512], True, True, inc="pe")
                ctx.wait("vector", "pe", pe)
                if nt == 0:
                    rows_c[3] = ctx.op("vector", lambda e:
                                       e.tensor_copy(cwrowB1[0:1, 0:512], pb[7][32:33, 0:512]),
                                       inc="vec")
                cpy[nt] = ctx.op("vector", lambda e, nt=nt, bank=bank:
                                 e.tensor_copy(CWBe[:, 512 * nt:512 * nt + 512], bank[:, :]),
                                 inc="vec")
            ctx.wait("tensor", "wdma", c_wu[par])
            silu_c = {}
            act_c = {}
            for ic in range(4):
                for nt in range(NT):
                    i2 = ic * NT + nt
                    spar = i2 % 2
                    tsl = slice(512 * nt, 512 * (nt + 1))
                    bg = pb[i2 % 2]
                    bu = pb[2 + i2 % 2]
                    if i2 >= 2:
                        ctx.wait("tensor", "act", silu_c[i2 - 2])
                        ctx.wait("tensor", "vec", act_c[i2 - 2])
                    for fc in range(NFC):
                        peg = _mm(ctx, bg[:, :], wgbuf[par][:, 512 * fc + 128 * ic:512 * fc + 128 * ic + 128],
                                  st.xn[fc][:, tsl], start=(fc == 0), stop=(fc == NFC - 1),
                                  inc="pe" if fc == NFC - 1 else None)
                    for fc in range(NFC):
                        peu = _mm(ctx, bu[:, :], wubuf[par][:, 512 * fc + 128 * ic:512 * fc + 128 * ic + 128],
                                  st.xn[fc][:, tsl], start=(fc == 0), stop=(fc == NFC - 1),
                                  inc="pe" if fc == NFC - 1 else None)
                    ctx.wait("scalar", "pe", peg)
                    if i2 >= 2:
                        ctx.wait("scalar", "vec", act_c[i2 - 2])
                    silu_c[i2] = ctx.op("scalar", lambda e, spar=spar, bg=bg:
                                        e.activation(sg[spar][:, :], bg[:, :], AF.Silu), inc="act")
                    ctx.wait("vector", "pe", peu)
                    ctx.wait("vector", "act", silu_c[i2])
                    ctx.op("vector", lambda e, nt=nt, bu=bu:
                           e.tensor_mul(uprime[:, :], bu[:, :], CWBe[:, 512 * nt:512 * nt + 512]))
                    act_c[i2] = ctx.op("vector", lambda e, ei=ei, ic=ic, tsl=tsl, spar=spar:
                                       e.tensor_mul(ACT1[ei][ic][:, tsl], sg[spar][:, :],
                                                    uprime[:, :]), inc="vec")
            wg_war[par] = wu_war[par] = ctx.val("pe")
            acts_done = act_c[15]
        ctx.wait("tensor", "vec", acts_done)
        ctx.wait("tensor", "wdma", max(c_wd))
        ev = {}
        for dc in range(NFC):
            for nt in range(NT):
                i2 = dc * NT + nt
                tsl = slice(512 * nt, 512 * (nt + 1))
                bank = pb[4 + i2 % 2]
                if i2 >= 2:
                    ctx.wait("tensor", "vec", ev[i2 - 2])
                for ei in range(2):
                    for ic in range(4):
                        pe = _mm(ctx, bank[:, :],
                                 wdbuf[ei][:, 1024 * ic + 128 * dc:1024 * ic + 128 * dc + 128],
                                 ACT1[ei][ic][:, tsl],
                                 start=(ei == 0 and ic == 0), stop=(ei == 1 and ic == 3),
                                 inc="pe" if (ei == 1 and ic == 3) else None)
                ctx.wait("vector", "pe", pe)
                ev[i2] = ctx.op("vector", lambda e, dc=dc, tsl=tsl, bank=bank:
                                e.tensor_add(st.x[dc][:, tsl], bank[:, :], st.x[dc][:, tsl]),
                                inc="vec")
        wd_war[0] = wd_war[1] = ctx.val("pe")
        ctx.wait("tensor", "vec", ctx.val("vec"))

    ws_war = [0, 0]
    c_ws = {}
    ctx.wait("gpsimd", "pe", ctx.val("pe"))
    for ic0 in (0, 1):
        c_ws[ic0] = ctx.dma("gpsimd", wsgb[ic0][:, :], dram["wsgs"][ic0], inc="wdma")
        c_ws[ic0] = ctx.dma("gpsimd", wsub[ic0][:, :], dram["wsus"][ic0], inc="wdma")
    silu_c = {}
    act_c = {}
    ctx.wait("tensor", "vec", ctx.val("vec"))
    for ic in range(8):
        par = ic % 2
        ctx.wait("tensor", "wdma", c_ws[ic])
        for nt in range(NT):
            i2 = ic * NT + nt
            spar = i2 % 2
            tsl = slice(512 * nt, 512 * (nt + 1))
            bg = pb[i2 % 2]
            bu = pb[2 + i2 % 2]
            if i2 >= 2:
                ctx.wait("tensor", "act", silu_c[i2 - 2])
                ctx.wait("tensor", "vec", act_c[i2 - 2])
            for fc in range(NFC):
                peg = _mm(ctx, bg[:, :], wsgb[par][:, 128 * fc:128 * fc + 128],
                          st.xn[fc][:, tsl], start=(fc == 0), stop=(fc == NFC - 1),
                          inc="pe" if fc == NFC - 1 else None)
            for fc in range(NFC):
                peu = _mm(ctx, bu[:, :], wsub[par][:, 128 * fc:128 * fc + 128],
                          st.xn[fc][:, tsl], start=(fc == 0), stop=(fc == NFC - 1),
                          inc="pe" if fc == NFC - 1 else None)
            ctx.wait("scalar", "pe", peg)
            if i2 >= 2:
                ctx.wait("scalar", "vec", act_c[i2 - 2])
            silu_c[i2] = ctx.op("scalar", lambda e, spar=spar, bg=bg:
                                e.activation(sg[spar][:, :], bg[:, :], AF.Silu), inc="act")
            ctx.wait("vector", "pe", peu)
            ctx.wait("vector", "act", silu_c[i2])
            act_c[i2] = ctx.op("vector", lambda e, ic=ic, tsl=tsl, spar=spar, bu=bu:
                               e.tensor_mul(sact[ic][:, tsl], bu[:, :], sg[spar][:, :]), inc="vec")
        if ic + 2 < 8:
            ctx.wait("gpsimd", "pe", ctx.val("pe"))
            c_ws[ic + 2] = ctx.dma("gpsimd", wsgb[par][:, :], dram["wsgs"][ic + 2], inc="wdma")
            c_ws[ic + 2] = ctx.dma("gpsimd", wsub[par][:, :], dram["wsus"][ic + 2], inc="wdma")
    ctx.wait("tensor", "vec", act_c[31])
    c_wsd = {}
    ctx.wait("gpsimd", "pe", ctx.val("pe"))
    for dc0 in (0, 1):
        c_wsd[dc0] = ctx.dma("gpsimd", wsdb[dc0][:, :], dram["wsds"][dc0], inc="wdma")
    ev = {}
    for dc in range(NFC):
        par = dc % 2
        ctx.wait("tensor", "wdma", c_wsd[dc])
        for nt in range(NT):
            i2 = dc * NT + nt
            tsl = slice(512 * nt, 512 * (nt + 1))
            bank = pb[4 + i2 % 2]
            if i2 >= 2:
                ctx.wait("tensor", "vec", ev[i2 - 2])
            for ic in range(8):
                pe = _mm(ctx, bank[:, :], wsdb[par][:, 128 * ic:128 * ic + 128],
                         sact[ic][:, tsl], start=(ic == 0), stop=(ic == 7),
                         inc="pe" if ic == 7 else None)
            ctx.wait("vector", "pe", pe)
            ev[i2] = ctx.op("vector", lambda e, dc=dc, tsl=tsl, bank=bank:
                            e.tensor_add(st.x[dc][:, tsl], bank[:, :], st.x[dc][:, tsl]),
                            inc="vec")
        if dc + 2 < 8:
            ctx.wait("gpsimd", "pe", ctx.val("pe"))
            c_wsd[dc + 2] = ctx.dma("gpsimd", wsdb[par][:, :], dram["wsds"][dc + 2], inc="wdma")
    ctx.wait("tensor", "vec", ctx.val("vec"))



def kernel(**inputs):
    shared = _prep_shared(inputs)
    x = np.asarray(inputs["L0_attn_x"], np.float32)
    nc = build_program()
    in_maps = []
    for c in range(8):
        m = dict(shared)
        m["xin"] = np.ascontiguousarray(x[c].T).reshape(NFC, 128, T)
        in_maps.append(m)
    res = run_bass_kernel_spmd(nc, in_maps, core_ids=list(range(8)))
    out = np.empty((8, T, D), np.float32)
    for c in range(8):
        out[c] = res.results[c]["out"].reshape(D, T).T
    return out
